# version 07
# baseline (speedup 1.0000x reference)
import numpy as np
import ml_dtypes
from contextlib import ExitStack

import concourse.bass as bass
import concourse.mybir as mybir
from concourse.bass_utils import run_bass_kernel_spmd

F32 = mybir.dt.float32
BF16 = mybir.dt.bfloat16
I32 = mybir.dt.int32
AF = mybir.ActivationFunctionType
OP = mybir.AluOpType
AXX = mybir.AxisListType.X

S = 2048
D = 2048
INTER = 5504
NC_D = D // 128
NC_S = S // 128
NC_I = INTER // 128
TF = 512
NPASS = S // TF
SCALE = 1.0 / float(np.sqrt(2048.0))
EPS = 1e-5
NEG = -1e30


class Sem:
    def __init__(self, ctx, nc, name):
        self.h = ctx.enter_context(nc.semaphore(name))
        self.n = 0

    def inc(self, instr, k=1):
        instr.then_inc(self.h, k)
        self.n += k
        return self.n


def build_program():
    nc = bass.Bass("TRN2", target_bir_lowering=False, debug=False)

    x_h = nc.dram_tensor("x", [S, D], F32, kind="ExternalInput").ap()
    wq_h = nc.dram_tensor("wq", [D, D], F32, kind="ExternalInput").ap()
    wk_h = nc.dram_tensor("wk", [D, D], F32, kind="ExternalInput").ap()
    wv_h = nc.dram_tensor("wv", [D, D], F32, kind="ExternalInput").ap()
    lut_h = nc.dram_tensor("lut", [1, 16], F32, kind="ExternalInput").ap()
    gidx_h = nc.dram_tensor("gidx", [INTER, D], I32, kind="ExternalInput").ap()
    uidx_h = nc.dram_tensor("uidx", [INTER, D], I32, kind="ExternalInput").ap()
    didx_h = nc.dram_tensor("didx", [D, INTER], I32, kind="ExternalInput").ap()
    out_h = nc.dram_tensor("out", [S, D], F32, kind="ExternalOutput").ap()

    xn_d = nc.dram_tensor("xn_d", [NC_D * 128, S], BF16, kind="Internal").ap()
    v_d = nc.dram_tensor("v_d", [S, D], BF16, kind="Internal").ap()
    att_d = nc.dram_tensor("att_d", [S, D], F32, kind="Internal").ap()
    wg_d = nc.dram_tensor("wg_d", [NC_I * 128, D], BF16, kind="Internal").ap()
    wu_d = nc.dram_tensor("wu_d", [NC_I * 128, D], BF16, kind="Internal").ap()
    wd_d = nc.dram_tensor("wd_d", [NC_D * 128, INTER], BF16, kind="Internal").ap()

    ident_bf = nc.inline_tensor(np.eye(128, dtype=np.float32).astype(
        ml_dtypes.bfloat16), name="ident_bf")
    ident_f32 = nc.inline_tensor(np.eye(128, dtype=np.float32), name="ident_f32")

    with ExitStack() as ctx:
        c_idbf = ctx.enter_context(nc.sbuf_tensor("c_idbf", [128, 128], BF16))
        c_id32 = ctx.enter_context(nc.sbuf_tensor("c_id32", [128, 128], F32))
        c_lut = ctx.enter_context(nc.sbuf_tensor("c_lut", [1, 16], F32))
        c_lutI = ctx.enter_context(nc.sbuf_tensor("c_lutI", [128, 16 * 128], BF16))
        c_ones = ctx.enter_context(nc.sbuf_tensor("c_ones", [1, 128], F32))
        c_lutb = ctx.enter_context(nc.sbuf_tensor("c_lutb", [128, 16], F32))
        c_umask = ctx.enter_context(nc.sbuf_tensor("c_umask", [128, 128], F32))
        c_eps = ctx.enter_context(nc.sbuf_tensor("c_eps", [128, 1], F32))
        c_hat = ctx.enter_context(nc.sbuf_tensor("c_hat", [128, 17], F32))
        st_st = ctx.enter_context(nc.sbuf_tensor("st_st", [128, 8], F32))

        s_pe = Sem(ctx, nc, "s_pe")
        s_dv = Sem(ctx, nc, "s_dv")
        s_se = Sem(ctx, nc, "s_se")
        s_gp = Sem(ctx, nc, "s_gp")
        s_st = Sem(ctx, nc, "s_st")
        sc_ld = Sem(ctx, nc, "sc_ld")
        s_wkl = Sem(ctx, nc, "s_wkl")
        s_wvl = Sem(ctx, nc, "s_wvl")
        s_xr = Sem(ctx, nc, "s_xr")
        pa = [Sem(ctx, nc, "pa0"), Sem(ctx, nc, "pa1")]
        pb = [Sem(ctx, nc, "pb0"), Sem(ctx, nc, "pb1")]
        s_sp = [Sem(ctx, nc, f"sp{i}") for i in range(4)]

        def st_store(eng, out_ap, in_ap):
            s_st.inc(eng.dma_start(out=out_ap, in_=in_ap), 16)
            eng.wait_ge(s_st.h, s_st.n)
            return s_st.n

        with nc.Block() as block:
            def p0_sync(sync):
                sc_ld.inc(sync.dma_start(out=c_idbf[:, :], in_=ident_bf.ap()), 16)
                sc_ld.inc(sync.dma_start(out=c_id32[:, :], in_=ident_f32.ap()), 16)
                sc_ld.inc(sync.dma_start(out=c_lut[:, :], in_=lut_h[:, :]), 16)
            block.sync(p0_sync)
            p0_ld = sc_ld.n

            def p0_gp(g):
                g.memset(c_umask[:, :], 0.0)
                s_gp.inc(g.affine_select(out=c_umask[:, :], in_=c_umask[:, :],
                                         compare_op=OP.is_ge, fill=NEG, base=0,
                                         pattern=[[-1, 128]], channel_multiplier=1))
            block.gpsimd(p0_gp)

            def p0_v(v):
                v.memset(c_ones[:, :], 1.0)
                for k in range(16):
                    v.memset(c_hat[:, k:k + 1], float(-k))
                v.memset(c_hat[:, 16:17], 1.0)
                s_dv.inc(v.memset(c_eps[:, :], EPS))
            block.vector(p0_v)
            p0_dv = s_dv.n

            with nc.psum_tensor("ps_bc", [128, 16], F32) as ps_bc:
                def p0_pe(t):
                    t.wait_ge(sc_ld.h, p0_ld)
                    t.wait_ge(s_dv.h, p0_dv)
                    s_pe.inc(t.matmul(ps_bc[:, :], c_ones[:, :], c_lut[:, :],
                                      start=True, stop=True))
                block.tensor(p0_pe)
                p0_pe_n = s_pe.n

                def p0_v2(v):
                    v.wait_ge(s_pe.h, p0_pe_n)
                    s_dv.inc(v.tensor_copy(out=c_lutb[:, :], in_=ps_bc[:, :]))
                    v.wait_ge(s_dv.h, s_dv.n)
                    for k in range(16):
                        s_dv.inc(v.tensor_scalar(
                            out=c_lutI[:, k * 128:(k + 1) * 128],
                            in0=c_idbf[:, :], scalar1=c_lutb[:, k:k + 1],
                            scalar2=None, op0=OP.mult))
                block.vector(p0_v2)
                p0_lutI_dv = s_dv.n

            es_xn = ExitStack()
            xn_fm = es_xn.enter_context(
                nc.sbuf_tensor("xn_fm", [128, NC_D * S], BF16, side="right"))
            with (
                nc.sbuf_tensor("x_t0", [128, D], F32) as x_t0,
                nc.sbuf_tensor("x_t1", [128, D], F32) as x_t1,
                nc.sbuf_tensor("sq_scr", [128, D], F32) as sq_scr,
                nc.sbuf_tensor("xn_tm", [128, D], BF16) as xn_tm,
                nc.psum_tensor("ps_ta", [128, 512], BF16) as ps_ta,
                nc.psum_tensor("ps_tb", [128, 512], BF16) as ps_tb,
            ):
                ps_tt = [ps_ta, ps_tb]
                x_tiles = [x_t0, x_t1]
                se_base = s_se.n
                pe_base = s_pe.n
                dve_var_at = {}
                dve_xn_at = {}
                evac_at = {}
                sim_dv = s_dv.n
                for tc in range(NC_S):
                    sim_dv += 4
                    dve_var_at[tc] = sim_dv
                    sim_dv += 1
                    dve_xn_at[tc] = sim_dv
                    for dc in range(NC_D):
                        sim_dv += 1
                        evac_at[tc * NC_D + dc] = sim_dv

                xt_ld = {}

                def p1_sync(sync):
                    for tc in range(NC_S):
                        if tc >= 2:
                            sync.wait_ge(s_dv.h, dve_xn_at[tc - 2])
                            sync.wait_ge(s_se.h, se_base + 3 * (tc - 2) + 1)
                        xt_ld[tc] = pa[tc % 2].inc(sync.dma_start(
                            out=x_tiles[tc % 2][:, :],
                            in_=x_h[tc * 128:(tc + 1) * 128, :]), 16)

                block.sync(p1_sync)

                def p1_se(s):
                    for tc in range(NC_S):
                        s.wait_ge(pa[tc % 2].h, xt_ld[tc])
                        s_se.inc(s.activation(out=sq_scr[:, :],
                                              in_=x_tiles[tc % 2][:, :],
                                              func=AF.Square,
                                              accum_out=st_st[:, 1:2]))
                        s.wait_ge(s_dv.h, dve_var_at[tc])
                        s_se.inc(s.activation(out=st_st[:, 4:5], in_=st_st[:, 3:4],
                                              func=AF.Ln, bias=c_eps[:, 0:1],
                                              scale=1.0))
                        s.wait_ge(s_se.h, s_se.n)
                        s_se.inc(s.activation(out=st_st[:, 5:6], in_=st_st[:, 4:5],
                                              func=AF.Exp, scale=-0.5))

                def p1_v(v):
                    for tc in range(NC_S):
                        v.wait_ge(pa[tc % 2].h, xt_ld[tc])
                        v.wait_ge(s_se.h, se_base + 3 * tc + 1)
                        s_dv.inc(v.reduce_sum(out=st_st[:, 0:1],
                                              in_=x_tiles[tc % 2][:, :],
                                              axis=AXX))
                        v.wait_ge(s_dv.h, s_dv.n)
                        s_dv.inc(v.tensor_scalar(out=st_st[:, 2:3],
                                                 in0=st_st[:, 0:1],
                                                 scalar1=1.0 / D, scalar2=None,
                                                 op0=OP.mult))
                        v.wait_ge(s_dv.h, s_dv.n)
                        s_dv.inc(v.tensor_tensor(out=st_st[:, 6:7],
                                                 in0=st_st[:, 2:3],
                                                 in1=st_st[:, 2:3], op=OP.mult))
                        v.wait_ge(s_dv.h, s_dv.n)
                        s_dv.inc(v.scalar_tensor_tensor(
                            out=st_st[:, 3:4], in0=st_st[:, 1:2], scalar=1.0 / D,
                            in1=st_st[:, 6:7], op0=OP.mult, op1=OP.subtract))
                        v.wait_ge(s_se.h, se_base + 3 * tc + 3)
                        if tc >= 1:
                            v.wait_ge(s_pe.h, pe_base + tc * NC_D)
                        s_dv.inc(v.tensor_scalar(out=xn_tm[:, :],
                                                 in0=x_tiles[tc % 2][:, :],
                                                 scalar1=st_st[:, 2:3],
                                                 scalar2=st_st[:, 5:6],
                                                 op0=OP.subtract, op1=OP.mult))
                        for dc in range(NC_D):
                            fl = tc * NC_D + dc
                            v.wait_ge(s_pe.h, pe_base + fl + 1)
                            off = ((fl // 2) % 4) * 128
                            s_dv.inc(v.tensor_copy(
                                out=xn_fm[:, dc * S + tc * 128:
                                          dc * S + (tc + 1) * 128],
                                in_=ps_tt[fl % 2][:, off:off + 128]))

                def p1_pe(t):
                    for tc in range(NC_S):
                        t.wait_ge(s_dv.h, dve_xn_at[tc])
                        for dc in range(NC_D):
                            fl = tc * NC_D + dc
                            if fl >= 2:
                                t.wait_ge(s_dv.h, evac_at[fl - 2])
                            off = ((fl // 2) % 4) * 128
                            s_pe.inc(t.transpose(
                                ps_tt[fl % 2][:, off:off + 128],
                                xn_tm[:, dc * 128:(dc + 1) * 128], c_idbf[:, :]))

                block.scalar(p1_se)
                block.vector(p1_v)
                block.tensor(p1_pe)
                assert s_dv.n == sim_dv, (s_dv.n, sim_dv)
                p1_dv_end = s_dv.n

                def p1_spill(g):
                    g.wait_ge(s_dv.h, p1_dv_end)
                    for dc in range(NC_D):
                        sc_ld.inc(g.dma_start(
                            out=xn_d[dc * 128:(dc + 1) * 128, :],
                            in_=xn_fm[:, dc * S:(dc + 1) * S]), 16)
                block.gpsimd(p1_spill)
                xn_spilled = sc_ld.n

            es_kq = ExitStack()
            k_fm = es_kq.enter_context(nc.sbuf_tensor("k_fm", [128, NC_D * S], BF16))

            with (
                nc.sbuf_tensor("wk_sb", [128, NC_D * D], BF16) as wk_sb,
                nc.psum_tensor("ps_a", [128, 512], F32) as ps_a,
                nc.psum_tensor("ps_b", [128, 512], F32) as ps_b,
            ):
                def p2_wk(g):
                    for k in range(NC_D):
                        s_wkl.inc(g.dma_start(
                            out=wk_sb[:, k * D:(k + 1) * D],
                            in_=wk_h[k * 128:(k + 1) * 128, :]), 16)
                block.gpsimd(p2_wk)
                wk_loaded = s_wkl.n

                pst = [ps_a, ps_b]
                pe0 = s_pe.n
                sim = s_dv.n
                k_evac = {n: sim + n + 1 for n in range(NC_D * 4)}

                def p2_pe(t):
                    t.wait_ge(s_wkl.h, wk_loaded)
                    t.wait_ge(s_dv.h, p1_dv_end)
                    n = 0
                    for ec in range(NC_D):
                        for qq in range(4):
                            if n >= 2:
                                t.wait_ge(s_dv.h, k_evac[n - 2])
                            for k in range(NC_D):
                                mm = t.matmul(
                                    pst[n % 2][:, :],
                                    wk_sb[:, k * D + ec * 128: k * D + (ec + 1) * 128],
                                    xn_fm[:, k * S + qq * 512: k * S + (qq + 1) * 512],
                                    start=(k == 0), stop=(k == NC_D - 1))
                            s_pe.inc(mm)
                            n += 1

                def p2_dv(v):
                    n = 0
                    for ec in range(NC_D):
                        for qq in range(4):
                            v.wait_ge(s_pe.h, pe0 + n + 1)
                            s_dv.inc(v.tensor_copy(
                                out=k_fm[:, ec * S + qq * 512: ec * S + (qq + 1) * 512],
                                in_=pst[n % 2][:, :]))
                            n += 1

                block.tensor(p2_pe)
                p2_pe_end = s_pe.n
                block.vector(p2_dv)
                assert s_dv.n == sim + NC_D * 4
                p2_dv_end = s_dv.n

            with (
                nc.sbuf_tensor("wv_sb", [128, NC_D * D], BF16) as wv_sb,
                nc.sbuf_tensor("v_sb", [128, D], BF16) as v_sb,
                nc.psum_tensor("ps_a3", [128, 512], F32) as ps_a,
                nc.psum_tensor("ps_b3", [128, 512], F32) as ps_b,
            ):
                def p3_wv(g):
                    g.wait_ge(s_pe.h, p2_pe_end)
                    for k in range(NC_D):
                        s_wvl.inc(g.dma_start(
                            out=wv_sb[:, k * D:(k + 1) * D],
                            in_=wv_h[k * 128:(k + 1) * 128, :]), 16)
                block.gpsimd(p3_wv)
                wv_loaded = s_wvl.n

                pst = [ps_a, ps_b]
                pe0 = s_pe.n
                st0 = s_st.n
                sim = s_dv.n
                v_evac = {n: sim + n + 1 for n in range(NC_S * 4)}
                v_store = {tc: st0 + 16 * (tc + 1) for tc in range(NC_S)}

                def p3_pe(t):
                    t.wait_ge(s_wvl.h, wv_loaded)
                    t.wait_ge(s_dv.h, p2_dv_end)
                    n = 0
                    for tc in range(NC_S):
                        for qq in range(4):
                            if n >= 2:
                                t.wait_ge(s_dv.h, v_evac[n - 2])
                            for k in range(NC_D):
                                mm = t.matmul(
                                    pst[n % 2][:, :],
                                    xn_fm[:, k * S + tc * 128: k * S + (tc + 1) * 128],
                                    wv_sb[:, k * D + qq * 512: k * D + (qq + 1) * 512],
                                    start=(k == 0), stop=(k == NC_D - 1))
                            s_pe.inc(mm)
                            n += 1

                def p3_dv(v):
                    n = 0
                    for tc in range(NC_S):
                        if tc >= 1:
                            v.wait_ge(s_st.h, v_store[tc - 1])
                        for qq in range(4):
                            v.wait_ge(s_pe.h, pe0 + n + 1)
                            s_dv.inc(v.tensor_copy(
                                out=v_sb[:, qq * 512:(qq + 1) * 512],
                                in_=pst[n % 2][:, :]))
                            n += 1

                def p3_st(sc):
                    for tc in range(NC_S):
                        sc.wait_ge(s_dv.h, v_evac[tc * 4 + 3])
                        st_store(sc, v_d[tc * 128:(tc + 1) * 128, :], v_sb[:, :])

                block.tensor(p3_pe)
                p3_pe_end = s_pe.n
                block.vector(p3_dv)
                block.scalar(p3_st)
                assert s_st.n == st0 + 16 * NC_S
                p3_dv_end = s_dv.n
                p3_st_end = s_st.n

            es_q = ExitStack()
            q_fm = es_q.enter_context(nc.sbuf_tensor("q_fm", [128, NC_D * S], BF16))
            with (
                nc.sbuf_tensor("wq_sb0", [128, D], BF16) as wq_sb0,
                nc.sbuf_tensor("wq_sb1", [128, D], BF16) as wq_sb1,
                nc.psum_tensor("ps_a4", [128, 512], F32) as ps_a,
                nc.psum_tensor("ps_b4", [128, 512], F32) as ps_b,
            ):
                wq_tiles = [wq_sb0, wq_sb1]
                pst = [ps_a, ps_b]
                pe0 = s_pe.n
                q_pe_at = {ec: pe0 + (ec + 1) * 4 for ec in range(NC_D)}
                sim = s_dv.n
                q_evac = {n: sim + n + 1 for n in range(NC_D * 4)}
                wq_ld = {}

                def p4_wq(g):
                    g.wait_ge(s_pe.h, p3_pe_end)
                    for ec in range(NC_D):
                        if ec >= 2:
                            g.wait_ge(s_pe.h, q_pe_at[ec - 2])
                        src = wq_h[:, ec * 128:(ec + 1) * 128].rearrange(
                            "(k p) n -> p k n", p=128)
                        dst = wq_tiles[ec % 2][:, :].rearrange(
                            "p (k n) -> p k n", n=128)
                        wq_ld[ec] = pb[ec % 2].inc(g.dma_start(out=dst, in_=src), 16)
                block.gpsimd(p4_wq)

                def p4_pe(t):
                    t.wait_ge(s_dv.h, p3_dv_end)
                    n = 0
                    for ec in range(NC_D):
                        t.wait_ge(pb[ec % 2].h, wq_ld[ec])
                        for qq in range(4):
                            if n >= 2:
                                t.wait_ge(s_dv.h, q_evac[n - 2])
                            for k in range(NC_D):
                                mm = t.matmul(
                                    pst[n % 2][:, :],
                                    wq_tiles[ec % 2][:, k * 128:(k + 1) * 128],
                                    xn_fm[:, k * S + qq * 512: k * S + (qq + 1) * 512],
                                    start=(k == 0), stop=(k == NC_D - 1))
                            s_pe.inc(mm)
                            n += 1

                def p4_dv(v):
                    n = 0
                    for ec in range(NC_D):
                        for qq in range(4):
                            v.wait_ge(s_pe.h, pe0 + n + 1)
                            s_dv.inc(v.tensor_copy(
                                out=q_fm[:, ec * S + qq * 512: ec * S + (qq + 1) * 512],
                                in_=pst[n % 2][:, :]))
                            n += 1

                block.tensor(p4_pe)
                p4_pe_end = s_pe.n
                block.vector(p4_dv)
                p4_dv_end = s_dv.n
            es_xn.close()

            with ExitStack() as es5:
                sc_sb = es5.enter_context(nc.sbuf_tensor("sc_sb", [128, S], F32))
                pr_sb = es5.enter_context(nc.sbuf_tensor("pr_sb", [128, S], BF16))
                prT_sb = es5.enter_context(nc.sbuf_tensor("prT_sb", [128, S], BF16))
                vs0 = es5.enter_context(nc.sbuf_tensor("vs0", [128, D], BF16))
                vs1 = es5.enter_context(nc.sbuf_tensor("vs1", [128, D], BF16))
                att_sb = es5.enter_context(nc.sbuf_tensor("att_sb", [128, D], F32))
                smx = es5.enter_context(nc.sbuf_tensor("smx", [128, 4], F32))
                ps_s0 = es5.enter_context(nc.psum_tensor("ps_s0", [128, 512], F32))
                ps_s1 = es5.enter_context(nc.psum_tensor("ps_s1", [128, 512], F32))
                ps_at = es5.enter_context(nc.psum_tensor("ps_at", [128, 2048], F32))
                ps_t5a = es5.enter_context(nc.psum_tensor("ps_t5a", [128, 512], BF16))
                ps_t5b = es5.enter_context(nc.psum_tensor("ps_t5b", [128, 512], BF16))
                ps_tt5 = [ps_t5a, ps_t5b]
                v_tiles = [vs0, vs1]
                ps_s = [ps_s0, ps_s1]
                pe0, dv0, se0, st0 = s_pe.n, s_dv.n, s_se.n, s_st.n
                sc_evac = {}
                g_of = {}
                pr_done = {}
                trT_evac = []
                att_evac = {}
                se_exp = {}
                pe_sc = {}
                pe_tr = {}
                pe_at = {}
                vseq = []
                att_store = {}
                sim_pe, sim_dv, sim_se, sim_st = pe0, dv0, se0, st0
                g = 0
                for qi in range(NC_S):
                    J = (qi + 1) * 128
                    nj = (J + 511) // 512
                    for js in range(nj):
                        g_of[(qi, js)] = g
                        sim_pe += 1
                        pe_sc[g] = sim_pe
                        sim_dv += 1
                        sc_evac[g] = sim_dv
                        g += 1
                    sim_dv += 3
                    sim_se += 1
                    se_exp[qi] = sim_se
                    sim_dv += 1
                    pr_done[qi] = sim_dv
                    for jc in range(qi + 1):
                        sim_pe += 1
                        pe_tr[(qi, jc)] = sim_pe
                        sim_dv += 1
                        trT_evac.append(sim_dv)
                    for jc in range(qi + 1):
                        vseq.append((qi, jc))
                        sim_pe += 1
                        pe_at[(qi, jc)] = sim_pe
                    sim_dv += 4
                    att_evac[qi] = sim_dv
                    sim_st += 16
                    att_store[qi] = sim_st

                def tri(qi, jc):
                    return qi * (qi + 1) // 2 + jc

                v_ld = {}

                def p5_ld(sync):
                    sync.wait_ge(s_st.h, p3_st_end)
                    sync.wait_ge(s_pe.h, p4_pe_end)
                    li = 0
                    for qi in range(NC_S):
                        for jc in range(qi + 1):
                            if li >= 2:
                                sync.wait_ge(s_pe.h, pe_at[vseq[li - 2]])
                            v_ld[li] = pa[li % 2].inc(sync.dma_start(
                                out=v_tiles[li % 2][:, :],
                                in_=v_d[jc * 128:(jc + 1) * 128, :]), 16)
                            li += 1
                block.sync(p5_ld)

                def p5_pe(t):
                    t.wait_ge(s_dv.h, p4_dv_end)
                    gg = 0
                    li = 0
                    for qi in range(NC_S):
                        J = (qi + 1) * 128
                        nj = (J + 511) // 512
                        for js in range(nj):
                            n = min(512, J - js * 512)
                            if gg >= 2:
                                t.wait_ge(s_dv.h, sc_evac[gg - 2])
                            for e in range(NC_D):
                                mm = t.matmul(
                                    ps_s[gg % 2][:, 0:n],
                                    q_fm[:, e * S + qi * 128: e * S + (qi + 1) * 128],
                                    k_fm[:, e * S + js * 512: e * S + js * 512 + n],
                                    start=(e == 0), stop=(e == NC_D - 1))
                            s_pe.inc(mm)
                            gg += 1
                        t.wait_ge(s_dv.h, pr_done[qi])
                        for jc in range(qi + 1):
                            fl = tri(qi, jc)
                            if fl >= 2:
                                t.wait_ge(s_dv.h, trT_evac[fl - 2])
                            off = ((fl // 2) % 4) * 128
                            s_pe.inc(t.transpose(
                                ps_tt5[fl % 2][:, off:off + 128],
                                pr_sb[:, jc * 128:(jc + 1) * 128], c_idbf[:, :]))
                        if qi >= 1:
                            t.wait_ge(s_dv.h, att_evac[qi - 1])
                        for jc in range(qi + 1):
                            t.wait_ge(pa[li % 2].h, v_ld[li])
                            t.wait_ge(s_dv.h, trT_evac[tri(qi, jc)])
                            for es in range(4):
                                mm = t.matmul(
                                    ps_at[:, es * 512:(es + 1) * 512],
                                    prT_sb[:, jc * 128:(jc + 1) * 128],
                                    v_tiles[li % 2][:, es * 512:(es + 1) * 512],
                                    start=(jc == 0), stop=(jc == qi))
                            s_pe.inc(mm)
                            li += 1

                def p5_dv(v):
                    for qi in range(NC_S):
                        J = (qi + 1) * 128
                        nj = (J + 511) // 512
                        for js in range(nj):
                            n = min(512, J - js * 512)
                            gg = g_of[(qi, js)]
                            v.wait_ge(s_pe.h, pe_sc[gg])
                            s_dv.inc(v.tensor_copy(
                                out=sc_sb[:, js * 512: js * 512 + n],
                                in_=ps_s[gg % 2][:, 0:n]))
                        v.wait_ge(s_dv.h, s_dv.n)
                        s_dv.inc(v.tensor_tensor(
                            out=sc_sb[:, qi * 128:(qi + 1) * 128],
                            in0=sc_sb[:, qi * 128:(qi + 1) * 128],
                            in1=c_umask[:, :], op=OP.add))
                        v.wait_ge(s_dv.h, s_dv.n)
                        s_dv.inc(v.reduce_max(out=smx[:, 0:1], in_=sc_sb[:, 0:J],
                                              axis=AXX))
                        v.wait_ge(s_dv.h, s_dv.n)
                        s_dv.inc(v.tensor_scalar(out=smx[:, 1:2], in0=smx[:, 0:1],
                                                 scalar1=-SCALE, scalar2=None,
                                                 op0=OP.mult))
                        v.wait_ge(s_se.h, se_exp[qi])
                        s_dv.inc(v.reciprocal(out=smx[:, 3:4], in_=smx[:, 2:3]))
                        for jc in range(qi + 1):
                            v.wait_ge(s_pe.h, pe_tr[(qi, jc)])
                            fl = tri(qi, jc)
                            off = ((fl // 2) % 4) * 128
                            s_dv.inc(v.tensor_copy(
                                out=prT_sb[:, jc * 128:(jc + 1) * 128],
                                in_=ps_tt5[fl % 2][:, off:off + 128]))
                        v.wait_ge(s_pe.h, pe_at[(qi, qi)])
                        if qi >= 1:
                            v.wait_ge(s_st.h, att_store[qi - 1])
                        v.wait_ge(s_dv.h, s_dv.n)
                        for es in range(4):
                            s_dv.inc(v.tensor_scalar(
                                out=att_sb[:, es * 512:(es + 1) * 512],
                                in0=ps_at[:, es * 512:(es + 1) * 512],
                                scalar1=smx[:, 3:4], scalar2=None, op0=OP.mult))

                def p5_scalar(s):
                    for qi in range(NC_S):
                        J = (qi + 1) * 128
                        s.wait_ge(s_dv.h, pr_done[qi] - 1)
                        if qi >= 1:
                            s.wait_ge(s_pe.h, pe_tr[(qi - 1, qi - 1)])
                        s_se.inc(s.activation(out=pr_sb[:, 0:J], in_=sc_sb[:, 0:J],
                                              func=AF.Exp, bias=smx[:, 1:2],
                                              scale=SCALE, accum_out=smx[:, 2:3]))
                        if qi >= 1:
                            s.wait_ge(s_dv.h, att_evac[qi - 1])
                            st_store(s, att_d[(qi - 1) * 128: qi * 128, :],
                                     att_sb[:, :])
                    s.wait_ge(s_dv.h, att_evac[NC_S - 1])
                    st_store(s, att_d[(NC_S - 1) * 128: NC_S * 128, :], att_sb[:, :])

                block.tensor(p5_pe)
                block.vector(p5_dv)
                block.scalar(p5_scalar)
                assert s_pe.n == sim_pe, (s_pe.n, sim_pe)
                assert s_dv.n == sim_dv, (s_dv.n, sim_dv)
                assert s_se.n == sim_se and s_st.n == sim_st
                p5_st_end = s_st.n
            es_q.close()
            es_kq.close()

            with ExitStack() as es55:
                mi_t = [es55.enter_context(nc.sbuf_tensor(f"mi{i}", [128, INTER], I32))
                        for i in range(2)]
                ib_t = [es55.enter_context(nc.sbuf_tensor(f"ib{i}", [128, INTER], BF16))
                        for i in range(2)]
                pl_t = [es55.enter_context(nc.sbuf_tensor(f"pl{i}", [128, 16 * 512], BF16))
                        for i in range(2)]
                ma_t = [es55.enter_context(nc.sbuf_tensor(f"ma{i}", [128, INTER], BF16))
                        for i in range(2)]
                stg_t = [es55.enter_context(nc.sbuf_tensor(f"stg{i}", [128, INTER], BF16))
                         for i in range(2)]
                ps_acc = [es55.enter_context(nc.psum_tensor(f"ps_ac{i}", [128, 512], F32))
                          for i in range(2)]
                ps_trm = [es55.enter_context(nc.psum_tensor(f"ps_tr{i}", [128, 1024], BF16))
                          for i in range(4)]
                chunks = []
                for c in range(NC_I):
                    chunks.append((gidx_h, c * 128, D, NC_D, wg_d, c * 128))
                    chunks.append((uidx_h, c * 128, D, NC_D, wu_d, c * 128))
                for kk in range(NC_D):
                    chunks.append((didx_h, kk * 128, INTER, NC_I, wd_d, kk * 128))
                NCH = len(chunks)

                def mregions(w):
                    out = []
                    off = 0
                    while off < w:
                        out.append((off, min(512, w - off)))
                        off += 512
                    return out

                cast_at = {}
                masks_at = {}
                masks_last = {}
                accum_at = {}
                evac_at = {}
                evac_last = {}
                tr_at = {}
                trevac_at = {}
                trevac_done = {}
                store_at = {}
                greg = []
                gbt = []
                gbt_idx = {}
                sim_se, sim_dv, sim_pe = s_se.n, s_dv.n, s_pe.n
                sim_sp = [s_sp[i].n for i in range(4)]
                SC = set(ci for ci in range(2 * NC_I) if ci % 5 == 2)
                for ci in range(NCH + 1):
                    cj = ci - 1
                    if 0 <= cj < NCH:
                        ntr = chunks[cj][3]
                        for tr in range(ntr):
                            sim_pe += 1
                            tr_at[(cj, tr)] = sim_pe
                        nbt = (ntr + 3) // 4
                        for bt in range(nbt):
                            sim_dv += 1
                            trevac_at[(cj, bt)] = sim_dv
                            gbt_idx[(cj, bt)] = len(gbt)
                            gbt.append((cj, bt))
                        trevac_done[cj] = trevac_at[(cj, nbt - 1)]
                        par = cj % 2
                        sim_sp[2 * par + 0] += 16
                        sim_sp[2 * par + 1] += 16
                        store_at[cj] = (sim_sp[2 * par + 0], sim_sp[2 * par + 1])
                    if ci < NCH:
                        w = chunks[ci][2]
                        sim_se += 1
                        cast_at[ci] = sim_se
                        for ri in range(len(mregions(w))):
                            if ci in SC:
                                sim_se += 1
                                masks_at[(ci, ri)] = ("se", sim_se)
                            else:
                                sim_dv += 16
                                masks_at[(ci, ri)] = ("dv", sim_dv)
                            sim_pe += 1
                            accum_at[(ci, ri)] = sim_pe
                            greg.append((ci, ri))
                        for ri in range(len(mregions(w))):
                            sim_se += 1
                            evac_at[(ci, ri)] = sim_se
                        masks_last[ci] = masks_at[(ci, len(mregions(w)) - 1)]
                        evac_last[ci] = evac_at[(ci, len(mregions(w)) - 1)]

                greg_idx = {cr: i for i, cr in enumerate(greg)}

                def _wait_masks(eng, mv):
                    kind, val = mv
                    eng.wait_ge(s_se.h if kind == "se" else s_dv.h, val)
                mld_a = {}
                mld_b = {}

                def _m_store(eng, cj, half):
                    src2, r02, w2, ntr2, dst2, dr02 = chunks[cj]
                    hw2 = w2 // 2
                    par = cj % 2
                    si = 2 * par + half
                    eng.wait_ge(s_dv.h, trevac_done[cj])
                    if cj >= 2:
                        eng.wait_ge(s_sp[si].h, store_at[cj - 2][half])
                    lo = 0 if half == 0 else hw2
                    hi = hw2 if half == 0 else w2
                    v = s_sp[si].inc(eng.dma_start(
                        out=dst2[dr02:dr02 + 128, lo:hi],
                        in_=stg_t[cj % 2][:, lo:hi]), 16)
                    assert v == store_at[cj][half], (cj, half, v, store_at[cj])

                def m_ld_sync(sync):
                    sync.wait_ge(s_st.h, p5_st_end)
                    for ci in range(NCH + 1):
                        if ci < NCH:
                            src, r0, w, ntr, _, _ = chunks[ci]
                            hw_ = w // 2
                            if ci >= 2:
                                sync.wait_ge(s_se.h, cast_at[ci - 2])
                            mld_a[ci] = pa[ci % 2].inc(sync.dma_start(
                                out=mi_t[ci % 2][:, 0:hw_],
                                in_=src[r0:r0 + 128, 0:hw_]), 16)
                        cj = ci - 1
                        if 0 <= cj < NCH:
                            _m_store(sync, cj, 0)
                block.sync(m_ld_sync)

                def m_ld_gps(g):
                    g.wait_ge(s_st.h, p5_st_end)
                    for ci in range(NCH + 1):
                        if ci < NCH:
                            src, r0, w, ntr, _, _ = chunks[ci]
                            hw_ = w // 2
                            if ci >= 2:
                                g.wait_ge(s_se.h, cast_at[ci - 2])
                            mld_b[ci] = pb[ci % 2].inc(g.dma_start(
                                out=mi_t[ci % 2][:, hw_:w],
                                in_=src[r0:r0 + 128, hw_:w]), 16)
                        cj = ci - 1
                        if 0 <= cj < NCH:
                            _m_store(g, cj, 1)
                block.gpsimd(m_ld_gps)

                def m_sc(sc):
                    for ci in range(NCH):
                        w = chunks[ci][2]
                        sc.wait_ge(pa[ci % 2].h, mld_a[ci])
                        sc.wait_ge(pb[ci % 2].h, mld_b[ci])
                        if ci >= 2:
                            _wait_masks(sc, masks_last[ci - 2])
                        n = s_se.inc(sc.copy(out=ib_t[ci % 2][:, 0:w],
                                             in_=mi_t[ci % 2][:, 0:w]))
                        assert n == cast_at[ci]
                        if ci in SC:
                            for ri, (off, wr) in enumerate(mregions(w)):
                                g_i = greg_idx[(ci, ri)]
                                if g_i >= 2:
                                    sc.wait_ge(s_pe.h, accum_at[greg[g_i - 2]])
                                for k in range(16):
                                    sc.activation(
                                        out=pl_t[g_i % 2][:, k * 512:k * 512 + wr],
                                        in_=ib_t[ci % 2][:, off:off + wr],
                                        func=AF.Abs, bias=c_hat[:, k:k + 1],
                                        scale=1.0)
                                mm = None
                                for k in range(16):
                                    mm = sc.activation(
                                        out=pl_t[g_i % 2][:, k * 512:k * 512 + wr],
                                        in_=pl_t[g_i % 2][:, k * 512:k * 512 + wr],
                                        func=AF.Relu, bias=c_hat[:, 16:17],
                                        scale=-1.0)
                                n = s_se.inc(mm)
                                assert masks_at[(ci, ri)] == ("se", n)
                        for ri, (off, wr) in enumerate(mregions(w)):
                            sc.wait_ge(s_pe.h, accum_at[(ci, ri)])
                            if ri == 0 and ci >= 2:
                                sc.wait_ge(s_pe.h,
                                           tr_at[(ci - 2, chunks[ci - 2][3] - 1)])
                            g_i = greg_idx[(ci, ri)]
                            n = s_se.inc(sc.copy(
                                out=ma_t[ci % 2][:, off:off + wr],
                                in_=ps_acc[g_i % 2][:, 0:wr]))
                            assert n == evac_at[(ci, ri)]
                block.scalar(m_sc)

                def m_dv(v):
                    g_i = 0
                    for ci in range(NCH + 1):
                        cj = ci - 1
                        if 0 <= cj < NCH:
                            ntr = chunks[cj][3]
                            nbt = (ntr + 3) // 4
                            for bt in range(nbt):
                                t0_, t1_ = bt * 4, min(ntr, bt * 4 + 4)
                                v.wait_ge(s_pe.h, tr_at[(cj, t1_ - 1)])
                                if bt == 0 and cj >= 2:
                                    v.wait_ge(s_sp[2 * (cj % 2)].h,
                                              store_at[cj - 2][0])
                                    v.wait_ge(s_sp[2 * (cj % 2) + 1].h,
                                              store_at[cj - 2][1])
                                gb = gbt_idx[(cj, bt)]
                                n = s_dv.inc(v.tensor_copy(
                                    out=stg_t[cj % 2][:, t0_ * 128:t1_ * 128],
                                    in_=ps_trm[gb % 4][:, 0:(t1_ - t0_) * 128]))
                                assert n == trevac_at[(cj, bt)]
                        if ci < NCH:
                            w = chunks[ci][2]
                            if ci in SC:
                                g_i += len(mregions(w))
                            else:
                                v.wait_ge(s_se.h, cast_at[ci])
                                for ri, (off, wr) in enumerate(mregions(w)):
                                    if g_i >= 2:
                                        v.wait_ge(s_pe.h, accum_at[greg[g_i - 2]])
                                    for k in range(16):
                                        s_dv.inc(v.tensor_scalar(
                                            out=pl_t[g_i % 2][:, k * 512:k * 512 + wr],
                                            in0=ib_t[ci % 2][:, off:off + wr],
                                            scalar1=float(k), scalar2=None,
                                            op0=OP.is_equal))
                                    assert masks_at[(ci, ri)] == ("dv", s_dv.n)
                                    g_i += 1
                block.vector(m_dv)

                def m_pe(t):
                    t.wait_ge(s_dv.h, p0_lutI_dv)
                    g_i = 0
                    for ci in range(NCH + 1):
                        cj = ci - 1
                        if 0 <= cj < NCH:
                            ntr = chunks[cj][3]
                            t.wait_ge(s_se.h, evac_last[cj])
                            nbt = (ntr + 3) // 4
                            for bt in range(nbt):
                                gb = gbt_idx[(cj, bt)]
                                if gb >= 4:
                                    t.wait_ge(s_dv.h, trevac_at[gbt[gb - 4]])
                                t0_, t1_ = bt * 4, min(ntr, bt * 4 + 4)
                                for tr in range(t0_, t1_):
                                    n = s_pe.inc(t.transpose(
                                        ps_trm[gb % 4][:, (tr - t0_) * 128:
                                                       (tr - t0_ + 1) * 128],
                                        ma_t[cj % 2][:, tr * 128:(tr + 1) * 128],
                                        c_idbf[:, :]))
                                    assert n == tr_at[(cj, tr)]
                        if ci < NCH:
                            w = chunks[ci][2]
                            for ri, (off, wr) in enumerate(mregions(w)):
                                _wait_masks(t, masks_at[(ci, ri)])
                                if g_i >= 2:
                                    t.wait_ge(s_se.h, evac_at[greg[g_i - 2]])
                                mm = None
                                for k in range(16):
                                    mm = t.matmul(
                                        ps_acc[g_i % 2][:, 0:wr],
                                        c_lutI[:, k * 128:(k + 1) * 128],
                                        pl_t[g_i % 2][:, k * 512:k * 512 + wr],
                                        start=(k == 0), stop=(k == 15))
                                n = s_pe.inc(mm)
                                assert n == accum_at[(ci, ri)]
                                g_i += 1
                block.tensor(m_pe)

                assert s_se.n == sim_se and s_dv.n == sim_dv and s_pe.n == sim_pe
                for i in range(4):
                    assert s_sp[i].n == sim_sp[i], (i, s_sp[i].n, sim_sp[i])

            with ExitStack() as es6:
                xn_fm2 = es6.enter_context(nc.sbuf_tensor("xn_fm2", [128, NC_D * S], BF16, side="right"))
                h_sb = es6.enter_context(nc.sbuf_tensor("h_sb", [128, NC_I * TF], BF16))
                out_sb = es6.enter_context(nc.sbuf_tensor("out_sb", [128, 4 * D], F32))
                wgt0 = es6.enter_context(nc.sbuf_tensor("wgt0", [128, D], BF16))
                wgt1 = es6.enter_context(nc.sbuf_tensor("wgt1", [128, D], BF16))
                wut0 = es6.enter_context(nc.sbuf_tensor("wut0", [128, D], BF16))
                wut1 = es6.enter_context(nc.sbuf_tensor("wut1", [128, D], BF16))
                wdt0 = es6.enter_context(nc.sbuf_tensor("wdt0", [128, INTER], BF16))
                wdt1 = es6.enter_context(nc.sbuf_tensor("wdt1", [128, INTER], BF16))
                xr_sb = es6.enter_context(nc.sbuf_tensor("xr_sb", [128, D], F32))
                ar_sb = es6.enter_context(nc.sbuf_tensor("ar_sb", [128, D], F32))
                gact = es6.enter_context(nc.sbuf_tensor("gact", [128, TF], F32))
                dn0 = es6.enter_context(nc.sbuf_tensor("dn0", [128, TF], F32))
                dn1 = es6.enter_context(nc.sbuf_tensor("dn1", [128, TF], F32))
                ps_g0 = es6.enter_context(nc.psum_tensor("ps_g0", [128, 512], F32))
                ps_g1 = es6.enter_context(nc.psum_tensor("ps_g1", [128, 512], F32))
                ps_u0 = es6.enter_context(nc.psum_tensor("ps_u0", [128, 512], F32))
                ps_u1 = es6.enter_context(nc.psum_tensor("ps_u1", [128, 512], F32))
                ps_d0 = es6.enter_context(nc.psum_tensor("ps_d0", [128, 512], F32))
                ps_d1 = es6.enter_context(nc.psum_tensor("ps_d1", [128, 512], F32))
                ps_o0 = es6.enter_context(nc.psum_tensor("ps_o0", [128, 512], F32))
                ps_o1 = es6.enter_context(nc.psum_tensor("ps_o1", [128, 512], F32))
                ps_oo = [ps_o0, ps_o1]
                wg_t = [wgt0, wgt1]
                wu_t = [wut0, wut1]
                wd_t = [wdt0, wdt1]
                dn_t = [dn0, dn1]
                psg = [ps_g0, ps_g1]
                psu = [ps_u0, ps_u1]
                psd = [ps_d0, ps_d1]

                def p6_xn(sync):
                    sync.wait_ge(sc_ld.h, xn_spilled)
                    sync.wait_ge(s_st.h, p5_st_end)
                    for dc in range(NC_D):
                        sc_ld.inc(sync.dma_start(
                            out=xn_fm2[:, dc * S:(dc + 1) * S],
                            in_=xn_d[dc * 128:(dc + 1) * 128, :]), 16)
                block.sync(p6_xn)
                xn_reloaded = sc_ld.n

                pe0, dv0, se0, st0 = s_pe.n, s_dv.n, s_se.n, s_st.n
                sim_pe, sim_dv, sim_se, sim_st = pe0, dv0, se0, st0
                gu_pe = {}
                h_dv = {}
                silu_se = {}
                res_dv = {}
                dn_pe = {}
                dn_dv = {}
                tr_pe = {}
                acc_dv = {}
                out_st = {}
                gu_list = []
                dn_list = []
                for pp in range(NPASS):
                    for tq in range(4):
                        sim_dv += 1
                        res_dv[(pp, tq)] = sim_dv
                    for c in range(NC_I):
                        gu_list.append((pp, c))
                        sim_pe += 2
                        gu_pe[(pp, c)] = sim_pe
                        sim_se += 1
                        silu_se[(pp, c)] = sim_se
                        sim_dv += 1
                        h_dv[(pp, c)] = sim_dv
                    for kk in range(NC_D):
                        dn_list.append((pp, kk))
                        sim_pe += 1
                        dn_pe[(pp, kk)] = sim_pe
                        sim_dv += 1
                        dn_dv[(pp, kk)] = sim_dv
                        for tq in range(4):
                            sim_pe += 1
                            tr_pe[(pp, kk, tq)] = sim_pe
                            sim_dv += 1
                            acc_dv[(pp, kk, tq)] = sim_dv
                    for tq in range(4):
                        sim_st += 16
                        out_st[(pp, tq)] = sim_st

                wldg = {}
                wldd = {}

                def _wait_mat(eng, ci):
                    par = ci % 2
                    eng.wait_ge(s_sp[2 * par].h, store_at[ci][0])
                    eng.wait_ge(s_sp[2 * par + 1].h, store_at[ci][1])

                def p6_wld(sync):
                    gi = 0
                    di = 0
                    for pp in range(NPASS):
                        for c in range(NC_I):
                            if gi >= 2:
                                sync.wait_ge(s_pe.h, gu_pe[gu_list[gi - 2]])
                            if pp == 0:
                                _wait_mat(sync, 2 * c)
                                _wait_mat(sync, 2 * c + 1)
                            sg = pa[gi % 2]
                            sg.inc(sync.dma_start(
                                out=wg_t[gi % 2][:, :],
                                in_=wg_d[c * 128:(c + 1) * 128, :]), 16)
                            wldg[(pp, c)] = sg.inc(sync.dma_start(
                                out=wu_t[gi % 2][:, :],
                                in_=wu_d[c * 128:(c + 1) * 128, :]), 16)
                            gi += 1
                        for kk in range(NC_D):
                            if di >= 2:
                                sync.wait_ge(s_pe.h, dn_pe[dn_list[di - 2]])
                            if pp == 0:
                                _wait_mat(sync, 2 * NC_I + kk)
                            wldd[(pp, kk)] = pb[di % 2].inc(sync.dma_start(
                                out=wd_t[di % 2][:, :],
                                in_=wd_d[kk * 128:(kk + 1) * 128, :]), 16)
                            di += 1
                block.sync(p6_wld)

                xld = {}

                def p6_xld(g):
                    g.wait_ge(s_st.h, p5_st_end)
                    for pp in range(NPASS):
                        for tq in range(4):
                            r0 = pp * TF + tq * 128
                            s_xr.inc(g.dma_start(out=xr_sb[:, :],
                                                 in_=x_h[r0:r0 + 128, :]), 16)
                            xld[(pp, tq)] = s_xr.inc(g.dma_start(
                                out=ar_sb[:, :],
                                in_=att_d[r0:r0 + 128, :]), 16)
                            g.wait_ge(s_dv.h, res_dv[(pp, tq)])
                block.gpsimd(p6_xld)

                def p6_pe(t):
                    t.wait_ge(sc_ld.h, xn_reloaded)
                    gi = 0
                    di = 0
                    for pp in range(NPASS):
                        for c in range(NC_I):
                            t.wait_ge(pa[gi % 2].h, wldg[(pp, c)])
                            if gi >= 2:
                                prev = gu_list[gi - 2]
                                t.wait_ge(s_dv.h, h_dv[prev])
                                t.wait_ge(s_se.h, silu_se[prev])
                            for k in range(NC_D):
                                mm = t.matmul(
                                    psg[gi % 2][:, :],
                                    wg_t[gi % 2][:, k * 128:(k + 1) * 128],
                                    xn_fm2[:, k * S + pp * TF: k * S + (pp + 1) * TF],
                                    start=(k == 0), stop=(k == NC_D - 1))
                            s_pe.inc(mm)
                            for k in range(NC_D):
                                mm = t.matmul(
                                    psu[gi % 2][:, :],
                                    wu_t[gi % 2][:, k * 128:(k + 1) * 128],
                                    xn_fm2[:, k * S + pp * TF: k * S + (pp + 1) * TF],
                                    start=(k == 0), stop=(k == NC_D - 1))
                            s_pe.inc(mm)
                            gi += 1
                        for kk in range(NC_D):
                            t.wait_ge(pb[di % 2].h, wldd[(pp, kk)])
                            t.wait_ge(s_dv.h, h_dv[(pp, NC_I - 1)])
                            if di >= 2:
                                t.wait_ge(s_dv.h, dn_dv[dn_list[di - 2]])
                            for i in range(NC_I):
                                mm = t.matmul(
                                    psd[di % 2][:, :],
                                    wd_t[di % 2][:, i * 128:(i + 1) * 128],
                                    h_sb[:, i * TF:(i + 1) * TF],
                                    start=(i == 0), stop=(i == NC_I - 1))
                            s_pe.inc(mm)
                            t.wait_ge(s_dv.h, dn_dv[(pp, kk)])
                            for tq in range(4):
                                gl_tr = di * 4 + tq
                                if gl_tr >= 2:
                                    pfl = gl_tr - 2
                                    pk = dn_list[pfl // 4]
                                    t.wait_ge(s_dv.h,
                                              acc_dv[(pk[0], pk[1], pfl % 4)])
                                s_pe.inc(t.transpose(
                                    ps_oo[tq % 2][:, (tq // 2) * 128:
                                                  (tq // 2 + 1) * 128],
                                    dn_t[di % 2][:, tq * 128:(tq + 1) * 128],
                                    c_id32[:, :]))
                            di += 1

                def p6_scalar(s):
                    gi = 0
                    for pp in range(NPASS):
                        for c in range(NC_I):
                            s.wait_ge(s_pe.h, gu_pe[(pp, c)] - 1)
                            if gi >= 1:
                                s.wait_ge(s_dv.h, h_dv[gu_list[gi - 1]])
                            s_se.inc(s.activation(out=gact[:, :],
                                                  in_=psg[gi % 2][:, :],
                                                  func=AF.Silu))
                            gi += 1
                        for tq in range(4):
                            s.wait_ge(s_dv.h, acc_dv[(pp, NC_D - 1, tq)])
                            r0 = pp * TF + tq * 128
                            st_store(s, out_h[r0:r0 + 128, :],
                                     out_sb[:, tq * D:(tq + 1) * D])
                    s.wait_ge(s_st.h, s_st.n)

                def p6_dv(v):
                    gi = 0
                    di = 0
                    for pp in range(NPASS):
                        for tq in range(4):
                            v.wait_ge(s_xr.h, xld[(pp, tq)])
                            if pp > 0:
                                v.wait_ge(s_st.h, out_st[(pp - 1, 3)])
                            s_dv.inc(v.tensor_tensor(
                                out=out_sb[:, tq * D:(tq + 1) * D],
                                in0=xr_sb[:, :], in1=ar_sb[:, :], op=OP.add))
                        for c in range(NC_I):
                            v.wait_ge(s_se.h, silu_se[(pp, c)])
                            v.wait_ge(s_pe.h, gu_pe[(pp, c)])
                            s_dv.inc(v.tensor_tensor(
                                out=h_sb[:, c * TF:(c + 1) * TF],
                                in0=gact[:, :], in1=psu[gi % 2][:, :], op=OP.mult))
                            gi += 1
                        for kk in range(NC_D):
                            v.wait_ge(s_pe.h, dn_pe[(pp, kk)])
                            if di >= 2:
                                pk = dn_list[di - 2]
                                v.wait_ge(s_pe.h, tr_pe[(pk[0], pk[1], 3)])
                            s_dv.inc(v.tensor_copy(out=dn_t[di % 2][:, :],
                                                   in_=psd[di % 2][:, :]))
                            for tq in range(4):
                                v.wait_ge(s_pe.h, tr_pe[(pp, kk, tq)])
                                s_dv.inc(v.tensor_tensor(
                                    out=out_sb[:, tq * D + kk * 128:
                                               tq * D + (kk + 1) * 128],
                                    in0=out_sb[:, tq * D + kk * 128:
                                               tq * D + (kk + 1) * 128],
                                    in1=ps_oo[tq % 2][:, (tq // 2) * 128:
                                                      (tq // 2 + 1) * 128],
                                    op=OP.add))
                            di += 1

                block.tensor(p6_pe)
                block.scalar(p6_scalar)
                block.vector(p6_dv)
                assert s_pe.n == sim_pe, (s_pe.n, sim_pe)
                assert s_dv.n == sim_dv, (s_dv.n, sim_dv)
                assert s_se.n == sim_se and s_st.n == sim_st

    return nc


_NC_CACHE = {}


def kernel(L0_attn_x, L0_attn_Wq, L0_attn_Wk, L0_attn_Wv,
           L0_ffn_lut, L0_ffn_gate_idx, L0_ffn_up_idx, L0_ffn_down_idx):
    x = np.ascontiguousarray(np.asarray(L0_attn_x, dtype=np.float32))
    wq = np.ascontiguousarray(np.asarray(L0_attn_Wq, dtype=np.float32))
    wk = np.ascontiguousarray(np.asarray(L0_attn_Wk, dtype=np.float32))
    wv = np.ascontiguousarray(np.asarray(L0_attn_Wv, dtype=np.float32))
    lut = np.ascontiguousarray(np.asarray(L0_ffn_lut, dtype=np.float32)).reshape(1, 16)
    gidx = np.ascontiguousarray(np.asarray(L0_ffn_gate_idx, dtype=np.int32))
    uidx = np.ascontiguousarray(np.asarray(L0_ffn_up_idx, dtype=np.int32))
    didx = np.ascontiguousarray(np.asarray(L0_ffn_down_idx, dtype=np.int32))

    if "nc" not in _NC_CACHE:
        _NC_CACHE["nc"] = build_program()
    nc = _NC_CACHE["nc"]

    in_maps = []
    for b in range(8):
        in_maps.append({
            "x": np.ascontiguousarray(x[b]),
            "wq": wq, "wk": wk, "wv": wv, "lut": lut,
            "gidx": gidx, "uidx": uidx, "didx": didx,
        })
    res = run_bass_kernel_spmd(nc, in_maps, core_ids=list(range(8)))
    out = np.stack([np.asarray(res.results[b]["out"]) for b in range(8)], axis=0)
    return out.astype(np.float32)


# version 08
# speedup vs baseline: 1.1743x; 1.1743x over previous
import numpy as np
import ml_dtypes
from contextlib import ExitStack

import concourse.bass as bass
import concourse.mybir as mybir
from concourse.bass_utils import run_bass_kernel_spmd

F32 = mybir.dt.float32
BF16 = mybir.dt.bfloat16
I32 = mybir.dt.int32
AF = mybir.ActivationFunctionType
OP = mybir.AluOpType
AXX = mybir.AxisListType.X

S = 2048
D = 2048
INTER = 5504
NC_D = D // 128
NC_S = S // 128
NC_I = INTER // 128
TF = 512
NPASS = S // TF
SCALE = 1.0 / float(np.sqrt(2048.0))
EPS = 1e-5
NEG = -1e30


class Sem:
    def __init__(self, ctx, nc, name):
        self.h = ctx.enter_context(nc.semaphore(name))
        self.n = 0

    def inc(self, instr, k=1):
        instr.then_inc(self.h, k)
        self.n += k
        return self.n


def build_program():
    nc = bass.Bass("TRN2", target_bir_lowering=False, debug=False)

    x_h = nc.dram_tensor("x", [S, D], F32, kind="ExternalInput").ap()
    wq_h = nc.dram_tensor("wq", [D, D], F32, kind="ExternalInput").ap()
    wk_h = nc.dram_tensor("wk", [D, D], F32, kind="ExternalInput").ap()
    wv_h = nc.dram_tensor("wv", [D, D], F32, kind="ExternalInput").ap()
    lut_h = nc.dram_tensor("lut", [1, 16], F32, kind="ExternalInput").ap()
    gidx_h = nc.dram_tensor("gidx", [INTER, D], I32, kind="ExternalInput").ap()
    uidx_h = nc.dram_tensor("uidx", [INTER, D], I32, kind="ExternalInput").ap()
    didx_h = nc.dram_tensor("didx", [D, INTER], I32, kind="ExternalInput").ap()
    out_h = nc.dram_tensor("out", [S, D], F32, kind="ExternalOutput").ap()

    xn_d = nc.dram_tensor("xn_d", [NC_D * 128, S], BF16, kind="Internal").ap()
    v_d = nc.dram_tensor("v_d", [S, D], BF16, kind="Internal").ap()
    att_d = nc.dram_tensor("att_d", [S, D], F32, kind="Internal").ap()
    wg_d = nc.dram_tensor("wg_d", [NC_I * 128, D], BF16, kind="Internal").ap()
    wu_d = nc.dram_tensor("wu_d", [NC_I * 128, D], BF16, kind="Internal").ap()
    wd_d = nc.dram_tensor("wd_d", [NC_D * 128, INTER], BF16, kind="Internal").ap()

    ident_bf = nc.inline_tensor(np.eye(128, dtype=np.float32).astype(
        ml_dtypes.bfloat16), name="ident_bf")
    ident_f32 = nc.inline_tensor(np.eye(128, dtype=np.float32), name="ident_f32")

    with ExitStack() as ctx:
        c_idbf = ctx.enter_context(nc.sbuf_tensor("c_idbf", [128, 128], BF16))
        c_id32 = ctx.enter_context(nc.sbuf_tensor("c_id32", [128, 128], F32))
        c_lut = ctx.enter_context(nc.sbuf_tensor("c_lut", [1, 16], F32))
        c_lutI = ctx.enter_context(nc.sbuf_tensor("c_lutI", [128, 16 * 128], BF16))
        c_ones = ctx.enter_context(nc.sbuf_tensor("c_ones", [1, 128], F32))
        c_lutb = ctx.enter_context(nc.sbuf_tensor("c_lutb", [128, 16], F32))
        c_umask = ctx.enter_context(nc.sbuf_tensor("c_umask", [128, 128], F32))
        c_eps = ctx.enter_context(nc.sbuf_tensor("c_eps", [128, 1], F32))
        st_st = ctx.enter_context(nc.sbuf_tensor("st_st", [128, 8], F32))

        s_pe = Sem(ctx, nc, "s_pe")
        s_dv = Sem(ctx, nc, "s_dv")
        s_se = Sem(ctx, nc, "s_se")
        s_gp = Sem(ctx, nc, "s_gp")
        s_st = Sem(ctx, nc, "s_st")
        sc_ld = Sem(ctx, nc, "sc_ld")
        s_wkl = Sem(ctx, nc, "s_wkl")
        s_wvl = Sem(ctx, nc, "s_wvl")
        s_xr = Sem(ctx, nc, "s_xr")
        pa = [Sem(ctx, nc, "pa0"), Sem(ctx, nc, "pa1")]
        pb = [Sem(ctx, nc, "pb0"), Sem(ctx, nc, "pb1")]
        s_sp = [Sem(ctx, nc, f"sp{i}") for i in range(4)]

        def st_store(eng, out_ap, in_ap):
            s_st.inc(eng.dma_start(out=out_ap, in_=in_ap), 16)
            eng.wait_ge(s_st.h, s_st.n)
            return s_st.n

        with nc.Block() as block:
            def p0_sync(sync):
                sc_ld.inc(sync.dma_start(out=c_idbf[:, :], in_=ident_bf.ap()), 16)
                sc_ld.inc(sync.dma_start(out=c_id32[:, :], in_=ident_f32.ap()), 16)
                sc_ld.inc(sync.dma_start(out=c_lut[:, :], in_=lut_h[:, :]), 16)
            block.sync(p0_sync)
            p0_ld = sc_ld.n

            def p0_gp(g):
                g.memset(c_umask[:, :], 0.0)
                s_gp.inc(g.affine_select(out=c_umask[:, :], in_=c_umask[:, :],
                                         compare_op=OP.is_ge, fill=NEG, base=0,
                                         pattern=[[-1, 128]], channel_multiplier=1))
            block.gpsimd(p0_gp)

            def p0_v(v):
                v.memset(c_ones[:, :], 1.0)
                s_dv.inc(v.memset(c_eps[:, :], EPS))
            block.vector(p0_v)
            p0_dv = s_dv.n

            with nc.psum_tensor("ps_bc", [128, 16], F32) as ps_bc:
                def p0_pe(t):
                    t.wait_ge(sc_ld.h, p0_ld)
                    t.wait_ge(s_dv.h, p0_dv)
                    s_pe.inc(t.matmul(ps_bc[:, :], c_ones[:, :], c_lut[:, :],
                                      start=True, stop=True))
                block.tensor(p0_pe)
                p0_pe_n = s_pe.n

                def p0_v2(v):
                    v.wait_ge(s_pe.h, p0_pe_n)
                    s_dv.inc(v.tensor_copy(out=c_lutb[:, :], in_=ps_bc[:, :]))
                    v.wait_ge(s_dv.h, s_dv.n)
                    for k in range(16):
                        s_dv.inc(v.tensor_scalar(
                            out=c_lutI[:, k * 128:(k + 1) * 128],
                            in0=c_idbf[:, :], scalar1=c_lutb[:, k:k + 1],
                            scalar2=None, op0=OP.mult))
                block.vector(p0_v2)
                p0_lutI_dv = s_dv.n

            es_xn = ExitStack()
            xn_fm = es_xn.enter_context(
                nc.sbuf_tensor("xn_fm", [128, NC_D * S], BF16, side="right"))
            with (
                nc.sbuf_tensor("x_t0", [128, D], F32) as x_t0,
                nc.sbuf_tensor("x_t1", [128, D], F32) as x_t1,
                nc.sbuf_tensor("sq_scr", [128, D], F32) as sq_scr,
                nc.sbuf_tensor("xn_tm", [128, D], BF16) as xn_tm,
                nc.psum_tensor("ps_ta", [128, 512], BF16) as ps_ta,
                nc.psum_tensor("ps_tb", [128, 512], BF16) as ps_tb,
            ):
                ps_tt = [ps_ta, ps_tb]
                x_tiles = [x_t0, x_t1]
                se_base = s_se.n
                pe_base = s_pe.n
                dve_var_at = {}
                dve_xn_at = {}
                evac_at = {}
                sim_dv = s_dv.n
                for tc in range(NC_S):
                    sim_dv += 4
                    dve_var_at[tc] = sim_dv
                    sim_dv += 1
                    dve_xn_at[tc] = sim_dv
                    for dc in range(NC_D):
                        sim_dv += 1
                        evac_at[tc * NC_D + dc] = sim_dv

                xt_ld = {}

                def p1_sync(sync):
                    for tc in range(NC_S):
                        if tc >= 2:
                            sync.wait_ge(s_dv.h, dve_xn_at[tc - 2])
                            sync.wait_ge(s_se.h, se_base + 3 * (tc - 2) + 1)
                        xt_ld[tc] = pa[tc % 2].inc(sync.dma_start(
                            out=x_tiles[tc % 2][:, :],
                            in_=x_h[tc * 128:(tc + 1) * 128, :]), 16)

                block.sync(p1_sync)

                def p1_se(s):
                    for tc in range(NC_S):
                        s.wait_ge(pa[tc % 2].h, xt_ld[tc])
                        s_se.inc(s.activation(out=sq_scr[:, :],
                                              in_=x_tiles[tc % 2][:, :],
                                              func=AF.Square,
                                              accum_out=st_st[:, 1:2]))
                        s.wait_ge(s_dv.h, dve_var_at[tc])
                        s_se.inc(s.activation(out=st_st[:, 4:5], in_=st_st[:, 3:4],
                                              func=AF.Ln, bias=c_eps[:, 0:1],
                                              scale=1.0))
                        s.wait_ge(s_se.h, s_se.n)
                        s_se.inc(s.activation(out=st_st[:, 5:6], in_=st_st[:, 4:5],
                                              func=AF.Exp, scale=-0.5))

                def p1_v(v):
                    for tc in range(NC_S):
                        v.wait_ge(pa[tc % 2].h, xt_ld[tc])
                        v.wait_ge(s_se.h, se_base + 3 * tc + 1)
                        s_dv.inc(v.reduce_sum(out=st_st[:, 0:1],
                                              in_=x_tiles[tc % 2][:, :],
                                              axis=AXX))
                        v.wait_ge(s_dv.h, s_dv.n)
                        s_dv.inc(v.tensor_scalar(out=st_st[:, 2:3],
                                                 in0=st_st[:, 0:1],
                                                 scalar1=1.0 / D, scalar2=None,
                                                 op0=OP.mult))
                        v.wait_ge(s_dv.h, s_dv.n)
                        s_dv.inc(v.tensor_tensor(out=st_st[:, 6:7],
                                                 in0=st_st[:, 2:3],
                                                 in1=st_st[:, 2:3], op=OP.mult))
                        v.wait_ge(s_dv.h, s_dv.n)
                        s_dv.inc(v.scalar_tensor_tensor(
                            out=st_st[:, 3:4], in0=st_st[:, 1:2], scalar=1.0 / D,
                            in1=st_st[:, 6:7], op0=OP.mult, op1=OP.subtract))
                        v.wait_ge(s_se.h, se_base + 3 * tc + 3)
                        if tc >= 1:
                            v.wait_ge(s_pe.h, pe_base + tc * NC_D)
                        s_dv.inc(v.tensor_scalar(out=xn_tm[:, :],
                                                 in0=x_tiles[tc % 2][:, :],
                                                 scalar1=st_st[:, 2:3],
                                                 scalar2=st_st[:, 5:6],
                                                 op0=OP.subtract, op1=OP.mult))
                        for dc in range(NC_D):
                            fl = tc * NC_D + dc
                            v.wait_ge(s_pe.h, pe_base + fl + 1)
                            off = ((fl // 2) % 4) * 128
                            s_dv.inc(v.tensor_copy(
                                out=xn_fm[:, dc * S + tc * 128:
                                          dc * S + (tc + 1) * 128],
                                in_=ps_tt[fl % 2][:, off:off + 128]))

                def p1_pe(t):
                    for tc in range(NC_S):
                        t.wait_ge(s_dv.h, dve_xn_at[tc])
                        for dc in range(NC_D):
                            fl = tc * NC_D + dc
                            if fl >= 2:
                                t.wait_ge(s_dv.h, evac_at[fl - 2])
                            off = ((fl // 2) % 4) * 128
                            s_pe.inc(t.transpose(
                                ps_tt[fl % 2][:, off:off + 128],
                                xn_tm[:, dc * 128:(dc + 1) * 128], c_idbf[:, :]))

                block.scalar(p1_se)
                block.vector(p1_v)
                block.tensor(p1_pe)
                assert s_dv.n == sim_dv, (s_dv.n, sim_dv)
                p1_dv_end = s_dv.n

                def p1_spill(g):
                    g.wait_ge(s_dv.h, p1_dv_end)
                    for dc in range(NC_D):
                        sc_ld.inc(g.dma_start(
                            out=xn_d[dc * 128:(dc + 1) * 128, :],
                            in_=xn_fm[:, dc * S:(dc + 1) * S]), 16)
                block.gpsimd(p1_spill)
                xn_spilled = sc_ld.n

            es_kq = ExitStack()
            k_fm = es_kq.enter_context(nc.sbuf_tensor("k_fm", [128, NC_D * S], BF16))

            with (
                nc.sbuf_tensor("wk_sb", [128, NC_D * D], BF16) as wk_sb,
                nc.psum_tensor("ps_a", [128, 512], F32) as ps_a,
                nc.psum_tensor("ps_b", [128, 512], F32) as ps_b,
            ):
                def p2_wk(g):
                    for k in range(NC_D):
                        s_wkl.inc(g.dma_start(
                            out=wk_sb[:, k * D:(k + 1) * D],
                            in_=wk_h[k * 128:(k + 1) * 128, :]), 16)
                block.gpsimd(p2_wk)
                wk_loaded = s_wkl.n

                pst = [ps_a, ps_b]
                pe0 = s_pe.n
                sim = s_dv.n
                k_evac = {n: sim + n + 1 for n in range(NC_D * 4)}

                def p2_pe(t):
                    t.wait_ge(s_wkl.h, wk_loaded)
                    t.wait_ge(s_dv.h, p1_dv_end)
                    n = 0
                    for ec in range(NC_D):
                        for qq in range(4):
                            if n >= 2:
                                t.wait_ge(s_dv.h, k_evac[n - 2])
                            for k in range(NC_D):
                                mm = t.matmul(
                                    pst[n % 2][:, :],
                                    wk_sb[:, k * D + ec * 128: k * D + (ec + 1) * 128],
                                    xn_fm[:, k * S + qq * 512: k * S + (qq + 1) * 512],
                                    start=(k == 0), stop=(k == NC_D - 1))
                            s_pe.inc(mm)
                            n += 1

                def p2_dv(v):
                    n = 0
                    for ec in range(NC_D):
                        for qq in range(4):
                            v.wait_ge(s_pe.h, pe0 + n + 1)
                            s_dv.inc(v.tensor_copy(
                                out=k_fm[:, ec * S + qq * 512: ec * S + (qq + 1) * 512],
                                in_=pst[n % 2][:, :]))
                            n += 1

                block.tensor(p2_pe)
                p2_pe_end = s_pe.n
                block.vector(p2_dv)
                assert s_dv.n == sim + NC_D * 4
                p2_dv_end = s_dv.n

            with (
                nc.sbuf_tensor("wv_sb", [128, NC_D * D], BF16) as wv_sb,
                nc.sbuf_tensor("v_sb", [128, D], BF16) as v_sb,
                nc.psum_tensor("ps_a3", [128, 512], F32) as ps_a,
                nc.psum_tensor("ps_b3", [128, 512], F32) as ps_b,
            ):
                def p3_wv(g):
                    g.wait_ge(s_pe.h, p2_pe_end)
                    for k in range(NC_D):
                        s_wvl.inc(g.dma_start(
                            out=wv_sb[:, k * D:(k + 1) * D],
                            in_=wv_h[k * 128:(k + 1) * 128, :]), 16)
                block.gpsimd(p3_wv)
                wv_loaded = s_wvl.n

                pst = [ps_a, ps_b]
                pe0 = s_pe.n
                st0 = s_st.n
                sim = s_dv.n
                v_evac = {n: sim + n + 1 for n in range(NC_S * 4)}
                v_store = {tc: st0 + 16 * (tc + 1) for tc in range(NC_S)}

                def p3_pe(t):
                    t.wait_ge(s_wvl.h, wv_loaded)
                    t.wait_ge(s_dv.h, p2_dv_end)
                    n = 0
                    for tc in range(NC_S):
                        for qq in range(4):
                            if n >= 2:
                                t.wait_ge(s_dv.h, v_evac[n - 2])
                            for k in range(NC_D):
                                mm = t.matmul(
                                    pst[n % 2][:, :],
                                    xn_fm[:, k * S + tc * 128: k * S + (tc + 1) * 128],
                                    wv_sb[:, k * D + qq * 512: k * D + (qq + 1) * 512],
                                    start=(k == 0), stop=(k == NC_D - 1))
                            s_pe.inc(mm)
                            n += 1

                def p3_dv(v):
                    n = 0
                    for tc in range(NC_S):
                        if tc >= 1:
                            v.wait_ge(s_st.h, v_store[tc - 1])
                        for qq in range(4):
                            v.wait_ge(s_pe.h, pe0 + n + 1)
                            s_dv.inc(v.tensor_copy(
                                out=v_sb[:, qq * 512:(qq + 1) * 512],
                                in_=pst[n % 2][:, :]))
                            n += 1

                def p3_st(sc):
                    for tc in range(NC_S):
                        sc.wait_ge(s_dv.h, v_evac[tc * 4 + 3])
                        st_store(sc, v_d[tc * 128:(tc + 1) * 128, :], v_sb[:, :])

                block.tensor(p3_pe)
                p3_pe_end = s_pe.n
                block.vector(p3_dv)
                block.scalar(p3_st)
                assert s_st.n == st0 + 16 * NC_S
                p3_dv_end = s_dv.n
                p3_st_end = s_st.n

            es_q = ExitStack()
            q_fm = es_q.enter_context(nc.sbuf_tensor("q_fm", [128, NC_D * S], BF16))
            with (
                nc.sbuf_tensor("wq_sb0", [128, D], BF16) as wq_sb0,
                nc.sbuf_tensor("wq_sb1", [128, D], BF16) as wq_sb1,
                nc.psum_tensor("ps_a4", [128, 512], F32) as ps_a,
                nc.psum_tensor("ps_b4", [128, 512], F32) as ps_b,
            ):
                wq_tiles = [wq_sb0, wq_sb1]
                pst = [ps_a, ps_b]
                pe0 = s_pe.n
                q_pe_at = {ec: pe0 + (ec + 1) * 4 for ec in range(NC_D)}
                sim = s_dv.n
                q_evac = {n: sim + n + 1 for n in range(NC_D * 4)}
                wq_ld = {}

                def p4_wq(g):
                    g.wait_ge(s_pe.h, p3_pe_end)
                    for ec in range(NC_D):
                        if ec >= 2:
                            g.wait_ge(s_pe.h, q_pe_at[ec - 2])
                        src = wq_h[:, ec * 128:(ec + 1) * 128].rearrange(
                            "(k p) n -> p k n", p=128)
                        dst = wq_tiles[ec % 2][:, :].rearrange(
                            "p (k n) -> p k n", n=128)
                        wq_ld[ec] = pb[ec % 2].inc(g.dma_start(out=dst, in_=src), 16)
                block.gpsimd(p4_wq)

                def p4_pe(t):
                    t.wait_ge(s_dv.h, p3_dv_end)
                    n = 0
                    for ec in range(NC_D):
                        t.wait_ge(pb[ec % 2].h, wq_ld[ec])
                        for qq in range(4):
                            if n >= 2:
                                t.wait_ge(s_dv.h, q_evac[n - 2])
                            for k in range(NC_D):
                                mm = t.matmul(
                                    pst[n % 2][:, :],
                                    wq_tiles[ec % 2][:, k * 128:(k + 1) * 128],
                                    xn_fm[:, k * S + qq * 512: k * S + (qq + 1) * 512],
                                    start=(k == 0), stop=(k == NC_D - 1))
                            s_pe.inc(mm)
                            n += 1

                def p4_dv(v):
                    n = 0
                    for ec in range(NC_D):
                        for qq in range(4):
                            v.wait_ge(s_pe.h, pe0 + n + 1)
                            s_dv.inc(v.tensor_copy(
                                out=q_fm[:, ec * S + qq * 512: ec * S + (qq + 1) * 512],
                                in_=pst[n % 2][:, :]))
                            n += 1

                block.tensor(p4_pe)
                p4_pe_end = s_pe.n
                block.vector(p4_dv)
                p4_dv_end = s_dv.n
            es_xn.close()

            with ExitStack() as es5:
                sc_sb = es5.enter_context(nc.sbuf_tensor("sc_sb", [128, S], F32))
                pr_sb = es5.enter_context(nc.sbuf_tensor("pr_sb", [128, S], BF16))
                prT_sb = es5.enter_context(nc.sbuf_tensor("prT_sb", [128, S], BF16))
                vs0 = es5.enter_context(nc.sbuf_tensor("vs0", [128, D], BF16))
                vs1 = es5.enter_context(nc.sbuf_tensor("vs1", [128, D], BF16))
                att_sb = es5.enter_context(nc.sbuf_tensor("att_sb", [128, D], F32))
                smx = es5.enter_context(nc.sbuf_tensor("smx", [128, 4], F32))
                ps_s0 = es5.enter_context(nc.psum_tensor("ps_s0", [128, 512], F32))
                ps_s1 = es5.enter_context(nc.psum_tensor("ps_s1", [128, 512], F32))
                ps_at = es5.enter_context(nc.psum_tensor("ps_at", [128, 2048], F32))
                ps_t5a = es5.enter_context(nc.psum_tensor("ps_t5a", [128, 512], BF16))
                ps_t5b = es5.enter_context(nc.psum_tensor("ps_t5b", [128, 512], BF16))
                ps_tt5 = [ps_t5a, ps_t5b]
                v_tiles = [vs0, vs1]
                ps_s = [ps_s0, ps_s1]
                pe0, dv0, se0, st0 = s_pe.n, s_dv.n, s_se.n, s_st.n
                sc_evac = {}
                g_of = {}
                pr_done = {}
                trT_evac = []
                att_evac = {}
                se_exp = {}
                pe_sc = {}
                pe_tr = {}
                pe_at = {}
                vseq = []
                att_store = {}
                sim_pe, sim_dv, sim_se, sim_st = pe0, dv0, se0, st0
                g = 0
                for qi in range(NC_S):
                    J = (qi + 1) * 128
                    nj = (J + 511) // 512
                    for js in range(nj):
                        g_of[(qi, js)] = g
                        sim_pe += 1
                        pe_sc[g] = sim_pe
                        sim_dv += 1
                        sc_evac[g] = sim_dv
                        g += 1
                    sim_dv += 3
                    sim_se += 1
                    se_exp[qi] = sim_se
                    sim_dv += 1
                    pr_done[qi] = sim_dv
                    for jc in range(qi + 1):
                        sim_pe += 1
                        pe_tr[(qi, jc)] = sim_pe
                        sim_dv += 1
                        trT_evac.append(sim_dv)
                    for jc in range(qi + 1):
                        vseq.append((qi, jc))
                        sim_pe += 1
                        pe_at[(qi, jc)] = sim_pe
                    sim_dv += 4
                    att_evac[qi] = sim_dv
                    sim_st += 16
                    att_store[qi] = sim_st

                def tri(qi, jc):
                    return qi * (qi + 1) // 2 + jc

                v_ld = {}

                def p5_ld(sync):
                    sync.wait_ge(s_st.h, p3_st_end)
                    sync.wait_ge(s_pe.h, p4_pe_end)
                    li = 0
                    for qi in range(NC_S):
                        for jc in range(qi + 1):
                            if li >= 2:
                                sync.wait_ge(s_pe.h, pe_at[vseq[li - 2]])
                            v_ld[li] = pa[li % 2].inc(sync.dma_start(
                                out=v_tiles[li % 2][:, :],
                                in_=v_d[jc * 128:(jc + 1) * 128, :]), 16)
                            li += 1
                block.sync(p5_ld)

                def p5_pe(t):
                    t.wait_ge(s_dv.h, p4_dv_end)
                    gg = 0
                    li = 0
                    for qi in range(NC_S):
                        J = (qi + 1) * 128
                        nj = (J + 511) // 512
                        for js in range(nj):
                            n = min(512, J - js * 512)
                            if gg >= 2:
                                t.wait_ge(s_dv.h, sc_evac[gg - 2])
                            for e in range(NC_D):
                                mm = t.matmul(
                                    ps_s[gg % 2][:, 0:n],
                                    q_fm[:, e * S + qi * 128: e * S + (qi + 1) * 128],
                                    k_fm[:, e * S + js * 512: e * S + js * 512 + n],
                                    start=(e == 0), stop=(e == NC_D - 1))
                            s_pe.inc(mm)
                            gg += 1
                        t.wait_ge(s_dv.h, pr_done[qi])
                        for jc in range(qi + 1):
                            fl = tri(qi, jc)
                            if fl >= 2:
                                t.wait_ge(s_dv.h, trT_evac[fl - 2])
                            off = ((fl // 2) % 4) * 128
                            s_pe.inc(t.transpose(
                                ps_tt5[fl % 2][:, off:off + 128],
                                pr_sb[:, jc * 128:(jc + 1) * 128], c_idbf[:, :]))
                        if qi >= 1:
                            t.wait_ge(s_dv.h, att_evac[qi - 1])
                        for jc in range(qi + 1):
                            t.wait_ge(pa[li % 2].h, v_ld[li])
                            t.wait_ge(s_dv.h, trT_evac[tri(qi, jc)])
                            for es in range(4):
                                mm = t.matmul(
                                    ps_at[:, es * 512:(es + 1) * 512],
                                    prT_sb[:, jc * 128:(jc + 1) * 128],
                                    v_tiles[li % 2][:, es * 512:(es + 1) * 512],
                                    start=(jc == 0), stop=(jc == qi))
                            s_pe.inc(mm)
                            li += 1

                def p5_dv(v):
                    for qi in range(NC_S):
                        J = (qi + 1) * 128
                        nj = (J + 511) // 512
                        for js in range(nj):
                            n = min(512, J - js * 512)
                            gg = g_of[(qi, js)]
                            v.wait_ge(s_pe.h, pe_sc[gg])
                            s_dv.inc(v.tensor_copy(
                                out=sc_sb[:, js * 512: js * 512 + n],
                                in_=ps_s[gg % 2][:, 0:n]))
                        v.wait_ge(s_dv.h, s_dv.n)
                        s_dv.inc(v.tensor_tensor(
                            out=sc_sb[:, qi * 128:(qi + 1) * 128],
                            in0=sc_sb[:, qi * 128:(qi + 1) * 128],
                            in1=c_umask[:, :], op=OP.add))
                        v.wait_ge(s_dv.h, s_dv.n)
                        s_dv.inc(v.reduce_max(out=smx[:, 0:1], in_=sc_sb[:, 0:J],
                                              axis=AXX))
                        v.wait_ge(s_dv.h, s_dv.n)
                        s_dv.inc(v.tensor_scalar(out=smx[:, 1:2], in0=smx[:, 0:1],
                                                 scalar1=-SCALE, scalar2=None,
                                                 op0=OP.mult))
                        v.wait_ge(s_se.h, se_exp[qi])
                        s_dv.inc(v.reciprocal(out=smx[:, 3:4], in_=smx[:, 2:3]))
                        for jc in range(qi + 1):
                            v.wait_ge(s_pe.h, pe_tr[(qi, jc)])
                            fl = tri(qi, jc)
                            off = ((fl // 2) % 4) * 128
                            s_dv.inc(v.tensor_copy(
                                out=prT_sb[:, jc * 128:(jc + 1) * 128],
                                in_=ps_tt5[fl % 2][:, off:off + 128]))
                        v.wait_ge(s_pe.h, pe_at[(qi, qi)])
                        if qi >= 1:
                            v.wait_ge(s_st.h, att_store[qi - 1])
                        v.wait_ge(s_dv.h, s_dv.n)
                        for es in range(4):
                            s_dv.inc(v.tensor_scalar(
                                out=att_sb[:, es * 512:(es + 1) * 512],
                                in0=ps_at[:, es * 512:(es + 1) * 512],
                                scalar1=smx[:, 3:4], scalar2=None, op0=OP.mult))

                def p5_scalar(s):
                    for qi in range(NC_S):
                        J = (qi + 1) * 128
                        s.wait_ge(s_dv.h, pr_done[qi] - 1)
                        if qi >= 1:
                            s.wait_ge(s_pe.h, pe_tr[(qi - 1, qi - 1)])
                        s_se.inc(s.activation(out=pr_sb[:, 0:J], in_=sc_sb[:, 0:J],
                                              func=AF.Exp, bias=smx[:, 1:2],
                                              scale=SCALE, accum_out=smx[:, 2:3]))
                        if qi >= 1:
                            s.wait_ge(s_dv.h, att_evac[qi - 1])
                            st_store(s, att_d[(qi - 1) * 128: qi * 128, :],
                                     att_sb[:, :])
                    s.wait_ge(s_dv.h, att_evac[NC_S - 1])
                    st_store(s, att_d[(NC_S - 1) * 128: NC_S * 128, :], att_sb[:, :])

                block.tensor(p5_pe)
                block.vector(p5_dv)
                block.scalar(p5_scalar)
                assert s_pe.n == sim_pe, (s_pe.n, sim_pe)
                assert s_dv.n == sim_dv, (s_dv.n, sim_dv)
                assert s_se.n == sim_se and s_st.n == sim_st
                p5_st_end = s_st.n
            es_q.close()
            es_kq.close()

            with ExitStack() as es55:
                mi_t = [es55.enter_context(nc.sbuf_tensor(f"mi{i}", [128, INTER], I32))
                        for i in range(2)]
                ib_t = [es55.enter_context(nc.sbuf_tensor(f"ib{i}", [128, INTER], BF16))
                        for i in range(2)]
                pl_t = [es55.enter_context(nc.sbuf_tensor(f"pl{i}", [128, 16 * 512], BF16))
                        for i in range(2)]
                ma_t = [es55.enter_context(nc.sbuf_tensor(f"ma{i}", [128, INTER], BF16))
                        for i in range(2)]
                stg_t = [es55.enter_context(nc.sbuf_tensor(f"stg{i}", [128, INTER], BF16))
                         for i in range(2)]
                ps_acc = [es55.enter_context(nc.psum_tensor(f"ps_ac{i}", [128, 512], F32))
                          for i in range(2)]
                ps_trm = [es55.enter_context(nc.psum_tensor(f"ps_tr{i}", [128, 1024], BF16))
                          for i in range(4)]
                chunks = []
                for c in range(NC_I):
                    chunks.append((gidx_h, c * 128, D, NC_D, wg_d, c * 128))
                    chunks.append((uidx_h, c * 128, D, NC_D, wu_d, c * 128))
                for kk in range(NC_D):
                    chunks.append((didx_h, kk * 128, INTER, NC_I, wd_d, kk * 128))
                NCH = len(chunks)

                def mregions(w):
                    out = []
                    off = 0
                    while off < w:
                        out.append((off, min(512, w - off)))
                        off += 512
                    return out

                cast_at = {}
                masks_at = {}
                masks_last = {}
                accum_at = {}
                evac_at = {}
                evac_last = {}
                tr_at = {}
                trevac_at = {}
                trevac_done = {}
                store_at = {}
                greg = []
                gbt = []
                gbt_idx = {}
                sim_se, sim_dv, sim_pe = s_se.n, s_dv.n, s_pe.n
                sim_sp = [s_sp[i].n for i in range(4)]
                for ci in range(NCH + 1):
                    cj = ci - 1
                    if 0 <= cj < NCH:
                        ntr = chunks[cj][3]
                        for tr in range(ntr):
                            sim_pe += 1
                            tr_at[(cj, tr)] = sim_pe
                        nbt = (ntr + 3) // 4
                        for bt in range(nbt):
                            sim_dv += 1
                            trevac_at[(cj, bt)] = sim_dv
                            gbt_idx[(cj, bt)] = len(gbt)
                            gbt.append((cj, bt))
                        trevac_done[cj] = trevac_at[(cj, nbt - 1)]
                        par = cj % 2
                        sim_sp[2 * par + 0] += 16
                        sim_sp[2 * par + 1] += 16
                        store_at[cj] = (sim_sp[2 * par + 0], sim_sp[2 * par + 1])
                    if ci < NCH:
                        w = chunks[ci][2]
                        sim_se += 1
                        cast_at[ci] = sim_se
                        for ri in range(len(mregions(w))):
                            sim_dv += 16
                            masks_at[(ci, ri)] = sim_dv
                            sim_pe += 1
                            accum_at[(ci, ri)] = sim_pe
                            sim_se += 1
                            evac_at[(ci, ri)] = sim_se
                            greg.append((ci, ri))
                        masks_last[ci] = masks_at[(ci, len(mregions(w)) - 1)]
                        evac_last[ci] = evac_at[(ci, len(mregions(w)) - 1)]

                greg_idx = {cr: i for i, cr in enumerate(greg)}
                mld_a = {}
                mld_b = {}

                def _m_store(eng, cj, half):
                    src2, r02, w2, ntr2, dst2, dr02 = chunks[cj]
                    hw2 = w2 // 2
                    par = cj % 2
                    si = 2 * par + half
                    eng.wait_ge(s_dv.h, trevac_done[cj])
                    if cj >= 2:
                        eng.wait_ge(s_sp[si].h, store_at[cj - 2][half])
                    lo = 0 if half == 0 else hw2
                    hi = hw2 if half == 0 else w2
                    v = s_sp[si].inc(eng.dma_start(
                        out=dst2[dr02:dr02 + 128, lo:hi],
                        in_=stg_t[cj % 2][:, lo:hi]), 16)
                    assert v == store_at[cj][half], (cj, half, v, store_at[cj])

                def m_ld_sync(sync):
                    sync.wait_ge(s_st.h, p5_st_end)
                    for ci in range(NCH + 1):
                        if ci < NCH:
                            src, r0, w, ntr, _, _ = chunks[ci]
                            hw_ = w // 2
                            if ci >= 2:
                                sync.wait_ge(s_se.h, cast_at[ci - 2])
                            mld_a[ci] = pa[ci % 2].inc(sync.dma_start(
                                out=mi_t[ci % 2][:, 0:hw_],
                                in_=src[r0:r0 + 128, 0:hw_]), 16)
                        cj = ci - 1
                        if 0 <= cj < NCH:
                            _m_store(sync, cj, 0)
                block.sync(m_ld_sync)

                def m_ld_gps(g):
                    g.wait_ge(s_st.h, p5_st_end)
                    for ci in range(NCH + 1):
                        if ci < NCH:
                            src, r0, w, ntr, _, _ = chunks[ci]
                            hw_ = w // 2
                            if ci >= 2:
                                g.wait_ge(s_se.h, cast_at[ci - 2])
                            mld_b[ci] = pb[ci % 2].inc(g.dma_start(
                                out=mi_t[ci % 2][:, hw_:w],
                                in_=src[r0:r0 + 128, hw_:w]), 16)
                        cj = ci - 1
                        if 0 <= cj < NCH:
                            _m_store(g, cj, 1)
                block.gpsimd(m_ld_gps)

                def m_sc(sc):
                    for ci in range(NCH):
                        w = chunks[ci][2]
                        sc.wait_ge(pa[ci % 2].h, mld_a[ci])
                        sc.wait_ge(pb[ci % 2].h, mld_b[ci])
                        if ci >= 2:
                            sc.wait_ge(s_dv.h, masks_last[ci - 2])
                        n = s_se.inc(sc.copy(out=ib_t[ci % 2][:, 0:w],
                                             in_=mi_t[ci % 2][:, 0:w]))
                        assert n == cast_at[ci]
                        for ri, (off, wr) in enumerate(mregions(w)):
                            sc.wait_ge(s_pe.h, accum_at[(ci, ri)])
                            if ri == 0 and ci >= 2:
                                sc.wait_ge(s_pe.h,
                                           tr_at[(ci - 2, chunks[ci - 2][3] - 1)])
                            g_i = greg_idx[(ci, ri)]
                            n = s_se.inc(sc.copy(
                                out=ma_t[ci % 2][:, off:off + wr],
                                in_=ps_acc[g_i % 2][:, 0:wr]))
                            assert n == evac_at[(ci, ri)]
                block.scalar(m_sc)

                def m_dv(v):
                    g_i = 0
                    for ci in range(NCH + 1):
                        cj = ci - 1
                        if 0 <= cj < NCH:
                            ntr = chunks[cj][3]
                            nbt = (ntr + 3) // 4
                            for bt in range(nbt):
                                t0_, t1_ = bt * 4, min(ntr, bt * 4 + 4)
                                v.wait_ge(s_pe.h, tr_at[(cj, t1_ - 1)])
                                if bt == 0 and cj >= 2:
                                    v.wait_ge(s_sp[2 * (cj % 2)].h,
                                              store_at[cj - 2][0])
                                    v.wait_ge(s_sp[2 * (cj % 2) + 1].h,
                                              store_at[cj - 2][1])
                                gb = gbt_idx[(cj, bt)]
                                n = s_dv.inc(v.tensor_copy(
                                    out=stg_t[cj % 2][:, t0_ * 128:t1_ * 128],
                                    in_=ps_trm[gb % 4][:, 0:(t1_ - t0_) * 128]))
                                assert n == trevac_at[(cj, bt)]
                        if ci < NCH:
                            w = chunks[ci][2]
                            v.wait_ge(s_se.h, cast_at[ci])
                            for ri, (off, wr) in enumerate(mregions(w)):
                                if g_i >= 2:
                                    v.wait_ge(s_pe.h, accum_at[greg[g_i - 2]])
                                for k in range(16):
                                    s_dv.inc(v.tensor_scalar(
                                        out=pl_t[g_i % 2][:, k * 512:k * 512 + wr],
                                        in0=ib_t[ci % 2][:, off:off + wr],
                                        scalar1=float(k), scalar2=None,
                                        op0=OP.is_equal))
                                assert s_dv.n == masks_at[(ci, ri)]
                                g_i += 1
                block.vector(m_dv)

                def m_pe(t):
                    t.wait_ge(s_dv.h, p0_lutI_dv)
                    g_i = 0
                    for ci in range(NCH + 1):
                        cj = ci - 1
                        if 0 <= cj < NCH:
                            ntr = chunks[cj][3]
                            t.wait_ge(s_se.h, evac_last[cj])
                            nbt = (ntr + 3) // 4
                            for bt in range(nbt):
                                gb = gbt_idx[(cj, bt)]
                                if gb >= 4:
                                    t.wait_ge(s_dv.h, trevac_at[gbt[gb - 4]])
                                t0_, t1_ = bt * 4, min(ntr, bt * 4 + 4)
                                for tr in range(t0_, t1_):
                                    n = s_pe.inc(t.transpose(
                                        ps_trm[gb % 4][:, (tr - t0_) * 128:
                                                       (tr - t0_ + 1) * 128],
                                        ma_t[cj % 2][:, tr * 128:(tr + 1) * 128],
                                        c_idbf[:, :]))
                                    assert n == tr_at[(cj, tr)]
                        if ci < NCH:
                            w = chunks[ci][2]
                            for ri, (off, wr) in enumerate(mregions(w)):
                                t.wait_ge(s_dv.h, masks_at[(ci, ri)])
                                if g_i >= 2:
                                    t.wait_ge(s_se.h, evac_at[greg[g_i - 2]])
                                mm = None
                                for k in range(16):
                                    mm = t.matmul(
                                        ps_acc[g_i % 2][:, 0:wr],
                                        c_lutI[:, k * 128:(k + 1) * 128],
                                        pl_t[g_i % 2][:, k * 512:k * 512 + wr],
                                        start=(k == 0), stop=(k == 15))
                                n = s_pe.inc(mm)
                                assert n == accum_at[(ci, ri)]
                                g_i += 1
                block.tensor(m_pe)

                assert s_se.n == sim_se and s_dv.n == sim_dv and s_pe.n == sim_pe
                for i in range(4):
                    assert s_sp[i].n == sim_sp[i], (i, s_sp[i].n, sim_sp[i])

            with ExitStack() as es6:
                xn_fm2 = es6.enter_context(nc.sbuf_tensor("xn_fm2", [128, NC_D * S], BF16, side="right"))
                h_sb = es6.enter_context(nc.sbuf_tensor("h_sb", [128, NC_I * TF], BF16))
                out_sb = es6.enter_context(nc.sbuf_tensor("out_sb", [128, 4 * D], F32))
                wgt0 = es6.enter_context(nc.sbuf_tensor("wgt0", [128, D], BF16))
                wgt1 = es6.enter_context(nc.sbuf_tensor("wgt1", [128, D], BF16))
                wut0 = es6.enter_context(nc.sbuf_tensor("wut0", [128, D], BF16))
                wut1 = es6.enter_context(nc.sbuf_tensor("wut1", [128, D], BF16))
                wdt0 = es6.enter_context(nc.sbuf_tensor("wdt0", [128, INTER], BF16))
                wdt1 = es6.enter_context(nc.sbuf_tensor("wdt1", [128, INTER], BF16))
                xr_sb = es6.enter_context(nc.sbuf_tensor("xr_sb", [128, D], F32))
                ar_sb = es6.enter_context(nc.sbuf_tensor("ar_sb", [128, D], F32))
                gact = es6.enter_context(nc.sbuf_tensor("gact", [128, TF], F32))
                dn0 = es6.enter_context(nc.sbuf_tensor("dn0", [128, TF], F32))
                dn1 = es6.enter_context(nc.sbuf_tensor("dn1", [128, TF], F32))
                ps_g0 = es6.enter_context(nc.psum_tensor("ps_g0", [128, 512], F32))
                ps_g1 = es6.enter_context(nc.psum_tensor("ps_g1", [128, 512], F32))
                ps_u0 = es6.enter_context(nc.psum_tensor("ps_u0", [128, 512], F32))
                ps_u1 = es6.enter_context(nc.psum_tensor("ps_u1", [128, 512], F32))
                ps_d0 = es6.enter_context(nc.psum_tensor("ps_d0", [128, 512], F32))
                ps_d1 = es6.enter_context(nc.psum_tensor("ps_d1", [128, 512], F32))
                ps_o0 = es6.enter_context(nc.psum_tensor("ps_o0", [128, 512], F32))
                ps_o1 = es6.enter_context(nc.psum_tensor("ps_o1", [128, 512], F32))
                ps_oo = [ps_o0, ps_o1]
                wg_t = [wgt0, wgt1]
                wu_t = [wut0, wut1]
                wd_t = [wdt0, wdt1]
                dn_t = [dn0, dn1]
                psg = [ps_g0, ps_g1]
                psu = [ps_u0, ps_u1]
                psd = [ps_d0, ps_d1]

                def p6_xn(sync):
                    sync.wait_ge(sc_ld.h, xn_spilled)
                    sync.wait_ge(s_st.h, p5_st_end)
                    for dc in range(NC_D):
                        sc_ld.inc(sync.dma_start(
                            out=xn_fm2[:, dc * S:(dc + 1) * S],
                            in_=xn_d[dc * 128:(dc + 1) * 128, :]), 16)
                block.sync(p6_xn)
                xn_reloaded = sc_ld.n

                pe0, dv0, se0, st0 = s_pe.n, s_dv.n, s_se.n, s_st.n
                sim_pe, sim_dv, sim_se, sim_st = pe0, dv0, se0, st0
                gu_pe = {}
                h_dv = {}
                silu_se = {}
                res_dv = {}
                dn_pe = {}
                dn_dv = {}
                tr_pe = {}
                acc_dv = {}
                out_st = {}
                gu_list = []
                dn_list = []
                for pp in range(NPASS):
                    for tq in range(4):
                        sim_dv += 1
                        res_dv[(pp, tq)] = sim_dv
                    for c in range(NC_I):
                        gu_list.append((pp, c))
                        sim_pe += 2
                        gu_pe[(pp, c)] = sim_pe
                        sim_se += 1
                        silu_se[(pp, c)] = sim_se
                        sim_dv += 1
                        h_dv[(pp, c)] = sim_dv
                    for kk in range(NC_D):
                        dn_list.append((pp, kk))
                        sim_pe += 1
                        dn_pe[(pp, kk)] = sim_pe
                        sim_dv += 1
                        dn_dv[(pp, kk)] = sim_dv
                        for tq in range(4):
                            sim_pe += 1
                            tr_pe[(pp, kk, tq)] = sim_pe
                            sim_dv += 1
                            acc_dv[(pp, kk, tq)] = sim_dv
                    for tq in range(4):
                        sim_st += 16
                        out_st[(pp, tq)] = sim_st

                wldg = {}
                wldd = {}

                def _wait_mat(eng, ci):
                    par = ci % 2
                    eng.wait_ge(s_sp[2 * par].h, store_at[ci][0])
                    eng.wait_ge(s_sp[2 * par + 1].h, store_at[ci][1])

                def p6_wld(sync):
                    gi = 0
                    di = 0
                    for pp in range(NPASS):
                        for c in range(NC_I):
                            if gi >= 2:
                                sync.wait_ge(s_pe.h, gu_pe[gu_list[gi - 2]])
                            if pp == 0:
                                _wait_mat(sync, 2 * c)
                                _wait_mat(sync, 2 * c + 1)
                            sg = pa[gi % 2]
                            sg.inc(sync.dma_start(
                                out=wg_t[gi % 2][:, :],
                                in_=wg_d[c * 128:(c + 1) * 128, :]), 16)
                            wldg[(pp, c)] = sg.inc(sync.dma_start(
                                out=wu_t[gi % 2][:, :],
                                in_=wu_d[c * 128:(c + 1) * 128, :]), 16)
                            gi += 1
                block.sync(p6_wld)

                xld = {}

                def p6_xld(g):
                    g.wait_ge(s_st.h, p5_st_end)
                    di = 0
                    for pp in range(NPASS):
                        for tq in range(4):
                            r0 = pp * TF + tq * 128
                            s_xr.inc(g.dma_start(out=xr_sb[:, :],
                                                 in_=x_h[r0:r0 + 128, :]), 16)
                            xld[(pp, tq)] = s_xr.inc(g.dma_start(
                                out=ar_sb[:, :],
                                in_=att_d[r0:r0 + 128, :]), 16)
                            g.wait_ge(s_dv.h, res_dv[(pp, tq)])
                        for kk in range(NC_D):
                            if di >= 2:
                                g.wait_ge(s_pe.h, dn_pe[dn_list[di - 2]])
                            if pp == 0:
                                _wait_mat(g, 2 * NC_I + kk)
                            wldd[(pp, kk)] = pb[di % 2].inc(g.dma_start(
                                out=wd_t[di % 2][:, :],
                                in_=wd_d[kk * 128:(kk + 1) * 128, :]), 16)
                            di += 1
                block.gpsimd(p6_xld)

                def p6_pe(t):
                    t.wait_ge(sc_ld.h, xn_reloaded)
                    gi = 0
                    di = 0
                    for pp in range(NPASS):
                        for c in range(NC_I):
                            t.wait_ge(pa[gi % 2].h, wldg[(pp, c)])
                            if gi >= 2:
                                prev = gu_list[gi - 2]
                                t.wait_ge(s_dv.h, h_dv[prev])
                                t.wait_ge(s_se.h, silu_se[prev])
                            for k in range(NC_D):
                                mm = t.matmul(
                                    psg[gi % 2][:, :],
                                    wg_t[gi % 2][:, k * 128:(k + 1) * 128],
                                    xn_fm2[:, k * S + pp * TF: k * S + (pp + 1) * TF],
                                    start=(k == 0), stop=(k == NC_D - 1))
                            s_pe.inc(mm)
                            for k in range(NC_D):
                                mm = t.matmul(
                                    psu[gi % 2][:, :],
                                    wu_t[gi % 2][:, k * 128:(k + 1) * 128],
                                    xn_fm2[:, k * S + pp * TF: k * S + (pp + 1) * TF],
                                    start=(k == 0), stop=(k == NC_D - 1))
                            s_pe.inc(mm)
                            gi += 1
                        for kk in range(NC_D):
                            t.wait_ge(pb[di % 2].h, wldd[(pp, kk)])
                            t.wait_ge(s_dv.h, h_dv[(pp, NC_I - 1)])
                            if di >= 2:
                                t.wait_ge(s_dv.h, dn_dv[dn_list[di - 2]])
                            for i in range(NC_I):
                                mm = t.matmul(
                                    psd[di % 2][:, :],
                                    wd_t[di % 2][:, i * 128:(i + 1) * 128],
                                    h_sb[:, i * TF:(i + 1) * TF],
                                    start=(i == 0), stop=(i == NC_I - 1))
                            s_pe.inc(mm)
                            t.wait_ge(s_dv.h, dn_dv[(pp, kk)])
                            for tq in range(4):
                                gl_tr = di * 4 + tq
                                if gl_tr >= 2:
                                    pfl = gl_tr - 2
                                    pk = dn_list[pfl // 4]
                                    t.wait_ge(s_dv.h,
                                              acc_dv[(pk[0], pk[1], pfl % 4)])
                                s_pe.inc(t.transpose(
                                    ps_oo[tq % 2][:, (tq // 2) * 128:
                                                  (tq // 2 + 1) * 128],
                                    dn_t[di % 2][:, tq * 128:(tq + 1) * 128],
                                    c_id32[:, :]))
                            di += 1

                def p6_scalar(s):
                    gi = 0
                    for pp in range(NPASS):
                        for c in range(NC_I):
                            s.wait_ge(s_pe.h, gu_pe[(pp, c)] - 1)
                            if gi >= 1:
                                s.wait_ge(s_dv.h, h_dv[gu_list[gi - 1]])
                            s_se.inc(s.activation(out=gact[:, :],
                                                  in_=psg[gi % 2][:, :],
                                                  func=AF.Silu))
                            gi += 1
                        for tq in range(4):
                            s.wait_ge(s_dv.h, acc_dv[(pp, NC_D - 1, tq)])
                            r0 = pp * TF + tq * 128
                            st_store(s, out_h[r0:r0 + 128, :],
                                     out_sb[:, tq * D:(tq + 1) * D])
                    s.wait_ge(s_st.h, s_st.n)

                def p6_dv(v):
                    gi = 0
                    di = 0
                    for pp in range(NPASS):
                        for tq in range(4):
                            v.wait_ge(s_xr.h, xld[(pp, tq)])
                            if pp > 0:
                                v.wait_ge(s_st.h, out_st[(pp - 1, 3)])
                            s_dv.inc(v.tensor_tensor(
                                out=out_sb[:, tq * D:(tq + 1) * D],
                                in0=xr_sb[:, :], in1=ar_sb[:, :], op=OP.add))
                        for c in range(NC_I):
                            v.wait_ge(s_se.h, silu_se[(pp, c)])
                            v.wait_ge(s_pe.h, gu_pe[(pp, c)])
                            s_dv.inc(v.tensor_tensor(
                                out=h_sb[:, c * TF:(c + 1) * TF],
                                in0=gact[:, :], in1=psu[gi % 2][:, :], op=OP.mult))
                            gi += 1
                        for kk in range(NC_D):
                            v.wait_ge(s_pe.h, dn_pe[(pp, kk)])
                            if di >= 2:
                                pk = dn_list[di - 2]
                                v.wait_ge(s_pe.h, tr_pe[(pk[0], pk[1], 3)])
                            s_dv.inc(v.tensor_copy(out=dn_t[di % 2][:, :],
                                                   in_=psd[di % 2][:, :]))
                            for tq in range(4):
                                v.wait_ge(s_pe.h, tr_pe[(pp, kk, tq)])
                                s_dv.inc(v.tensor_tensor(
                                    out=out_sb[:, tq * D + kk * 128:
                                               tq * D + (kk + 1) * 128],
                                    in0=out_sb[:, tq * D + kk * 128:
                                               tq * D + (kk + 1) * 128],
                                    in1=ps_oo[tq % 2][:, (tq // 2) * 128:
                                                      (tq // 2 + 1) * 128],
                                    op=OP.add))
                            di += 1

                block.tensor(p6_pe)
                block.scalar(p6_scalar)
                block.vector(p6_dv)
                assert s_pe.n == sim_pe, (s_pe.n, sim_pe)
                assert s_dv.n == sim_dv, (s_dv.n, sim_dv)
                assert s_se.n == sim_se and s_st.n == sim_st

    return nc


_NC_CACHE = {}


def kernel(L0_attn_x, L0_attn_Wq, L0_attn_Wk, L0_attn_Wv,
           L0_ffn_lut, L0_ffn_gate_idx, L0_ffn_up_idx, L0_ffn_down_idx):
    x = np.ascontiguousarray(np.asarray(L0_attn_x, dtype=np.float32))
    wq = np.ascontiguousarray(np.asarray(L0_attn_Wq, dtype=np.float32))
    wk = np.ascontiguousarray(np.asarray(L0_attn_Wk, dtype=np.float32))
    wv = np.ascontiguousarray(np.asarray(L0_attn_Wv, dtype=np.float32))
    lut = np.ascontiguousarray(np.asarray(L0_ffn_lut, dtype=np.float32)).reshape(1, 16)
    gidx = np.ascontiguousarray(np.asarray(L0_ffn_gate_idx, dtype=np.int32))
    uidx = np.ascontiguousarray(np.asarray(L0_ffn_up_idx, dtype=np.int32))
    didx = np.ascontiguousarray(np.asarray(L0_ffn_down_idx, dtype=np.int32))

    if "nc" not in _NC_CACHE:
        _NC_CACHE["nc"] = build_program()
    nc = _NC_CACHE["nc"]

    in_maps = []
    for b in range(8):
        in_maps.append({
            "x": np.ascontiguousarray(x[b]),
            "wq": wq, "wk": wk, "wv": wv, "lut": lut,
            "gidx": gidx, "uidx": uidx, "didx": didx,
        })
    res = run_bass_kernel_spmd(nc, in_maps, core_ids=list(range(8)))
    out = np.stack([np.asarray(res.results[b]["out"]) for b in range(8)], axis=0)
    return out.astype(np.float32)


# version 09 (final)
# speedup vs baseline: 1.1833x; 1.0076x over previous
import numpy as np
import ml_dtypes
from contextlib import ExitStack

import concourse.bass as bass
import concourse.mybir as mybir
from concourse.bass_utils import run_bass_kernel_spmd

F32 = mybir.dt.float32
BF16 = mybir.dt.bfloat16
I32 = mybir.dt.int32
AF = mybir.ActivationFunctionType
OP = mybir.AluOpType
AXX = mybir.AxisListType.X

S = 2048
D = 2048
INTER = 5504
NC_D = D // 128
NC_S = S // 128
NC_I = INTER // 128
TF = 512
NPASS = S // TF
SCALE = 1.0 / float(np.sqrt(2048.0))
EPS = 1e-5
NEG = -1e30


class Sem:
    def __init__(self, ctx, nc, name):
        self.h = ctx.enter_context(nc.semaphore(name))
        self.n = 0

    def inc(self, instr, k=1):
        instr.then_inc(self.h, k)
        self.n += k
        return self.n


def build_program():
    nc = bass.Bass("TRN2", target_bir_lowering=False, debug=False)

    x_h = nc.dram_tensor("x", [S, D], F32, kind="ExternalInput").ap()
    wq_h = nc.dram_tensor("wq", [D, D], F32, kind="ExternalInput").ap()
    wk_h = nc.dram_tensor("wk", [D, D], F32, kind="ExternalInput").ap()
    wv_h = nc.dram_tensor("wv", [D, D], F32, kind="ExternalInput").ap()
    lut_h = nc.dram_tensor("lut", [1, 16], F32, kind="ExternalInput").ap()
    gidx_h = nc.dram_tensor("gidx", [INTER, D], I32, kind="ExternalInput").ap()
    uidx_h = nc.dram_tensor("uidx", [INTER, D], I32, kind="ExternalInput").ap()
    didx_h = nc.dram_tensor("didx", [D, INTER], I32, kind="ExternalInput").ap()
    out_h = nc.dram_tensor("out", [S, D], F32, kind="ExternalOutput").ap()

    xn_d = nc.dram_tensor("xn_d", [NC_D * 128, S], BF16, kind="Internal").ap()
    v_d = nc.dram_tensor("v_d", [S, D], BF16, kind="Internal").ap()
    att_d = nc.dram_tensor("att_d", [S, D], F32, kind="Internal").ap()
    wg_d = nc.dram_tensor("wg_d", [NC_I * 128, D], BF16, kind="Internal").ap()
    wu_d = nc.dram_tensor("wu_d", [NC_I * 128, D], BF16, kind="Internal").ap()
    wd_d = nc.dram_tensor("wd_d", [NC_D * 128, INTER], BF16, kind="Internal").ap()

    ident_bf = nc.inline_tensor(np.eye(128, dtype=np.float32).astype(
        ml_dtypes.bfloat16), name="ident_bf")
    ident_f32 = nc.inline_tensor(np.eye(128, dtype=np.float32), name="ident_f32")

    with ExitStack() as ctx:
        c_idbf = ctx.enter_context(nc.sbuf_tensor("c_idbf", [128, 128], BF16))
        c_id32 = ctx.enter_context(nc.sbuf_tensor("c_id32", [128, 128], F32))
        c_lut = ctx.enter_context(nc.sbuf_tensor("c_lut", [1, 16], F32))
        c_lutI = ctx.enter_context(nc.sbuf_tensor("c_lutI", [128, 16 * 128], BF16))
        c_ones = ctx.enter_context(nc.sbuf_tensor("c_ones", [1, 128], F32))
        c_lutb = ctx.enter_context(nc.sbuf_tensor("c_lutb", [128, 16], F32))
        c_umask = ctx.enter_context(nc.sbuf_tensor("c_umask", [128, 128], F32))
        c_eps = ctx.enter_context(nc.sbuf_tensor("c_eps", [128, 1], F32))
        st_st = ctx.enter_context(nc.sbuf_tensor("st_st", [128, 8], F32))

        s_pe = Sem(ctx, nc, "s_pe")
        s_dv = Sem(ctx, nc, "s_dv")
        s_se = Sem(ctx, nc, "s_se")
        s_gp = Sem(ctx, nc, "s_gp")
        s_st = Sem(ctx, nc, "s_st")
        sc_ld = Sem(ctx, nc, "sc_ld")
        s_wkl = Sem(ctx, nc, "s_wkl")
        s_wvl = Sem(ctx, nc, "s_wvl")
        s_xr = Sem(ctx, nc, "s_xr")
        pa = [Sem(ctx, nc, "pa0"), Sem(ctx, nc, "pa1")]
        pb = [Sem(ctx, nc, "pb0"), Sem(ctx, nc, "pb1")]
        s_sp = [Sem(ctx, nc, f"sp{i}") for i in range(4)]

        def st_store(eng, out_ap, in_ap):
            s_st.inc(eng.dma_start(out=out_ap, in_=in_ap), 16)
            eng.wait_ge(s_st.h, s_st.n)
            return s_st.n

        with nc.Block() as block:
            def p0_sync(sync):
                sc_ld.inc(sync.dma_start(out=c_idbf[:, :], in_=ident_bf.ap()), 16)
                sc_ld.inc(sync.dma_start(out=c_id32[:, :], in_=ident_f32.ap()), 16)
                sc_ld.inc(sync.dma_start(out=c_lut[:, :], in_=lut_h[:, :]), 16)
            block.sync(p0_sync)
            p0_ld = sc_ld.n

            def p0_gp(g):
                g.memset(c_umask[:, :], 0.0)
                s_gp.inc(g.affine_select(out=c_umask[:, :], in_=c_umask[:, :],
                                         compare_op=OP.is_ge, fill=NEG, base=0,
                                         pattern=[[-1, 128]], channel_multiplier=1))
            block.gpsimd(p0_gp)

            def p0_v(v):
                v.memset(c_ones[:, :], 1.0)
                s_dv.inc(v.memset(c_eps[:, :], EPS))
            block.vector(p0_v)
            p0_dv = s_dv.n

            with nc.psum_tensor("ps_bc", [128, 16], F32) as ps_bc:
                def p0_pe(t):
                    t.wait_ge(sc_ld.h, p0_ld)
                    t.wait_ge(s_dv.h, p0_dv)
                    s_pe.inc(t.matmul(ps_bc[:, :], c_ones[:, :], c_lut[:, :],
                                      start=True, stop=True))
                block.tensor(p0_pe)
                p0_pe_n = s_pe.n

                def p0_v2(v):
                    v.wait_ge(s_pe.h, p0_pe_n)
                    s_dv.inc(v.tensor_copy(out=c_lutb[:, :], in_=ps_bc[:, :]))
                    v.wait_ge(s_dv.h, s_dv.n)
                    for k in range(16):
                        s_dv.inc(v.tensor_scalar(
                            out=c_lutI[:, k * 128:(k + 1) * 128],
                            in0=c_idbf[:, :], scalar1=c_lutb[:, k:k + 1],
                            scalar2=None, op0=OP.mult))
                block.vector(p0_v2)
                p0_lutI_dv = s_dv.n

            es_xn = ExitStack()
            xn_fm = es_xn.enter_context(
                nc.sbuf_tensor("xn_fm", [128, NC_D * S], BF16, side="right"))
            with (
                nc.sbuf_tensor("x_t0", [128, D], F32) as x_t0,
                nc.sbuf_tensor("x_t1", [128, D], F32) as x_t1,
                nc.sbuf_tensor("sq_scr", [128, D], F32) as sq_scr,
                nc.sbuf_tensor("xn_tm", [128, D], BF16) as xn_tm,
                nc.psum_tensor("ps_ta", [128, 512], BF16) as ps_ta,
                nc.psum_tensor("ps_tb", [128, 512], BF16) as ps_tb,
            ):
                ps_tt = [ps_ta, ps_tb]
                x_tiles = [x_t0, x_t1]
                se_base = s_se.n
                pe_base = s_pe.n
                dve_var_at = {}
                dve_xn_at = {}
                evac_at = {}
                sim_dv = s_dv.n
                for tc in range(NC_S):
                    sim_dv += 4
                    dve_var_at[tc] = sim_dv
                    sim_dv += 1
                    dve_xn_at[tc] = sim_dv
                    for dc in range(NC_D):
                        sim_dv += 1
                        evac_at[tc * NC_D + dc] = sim_dv

                xt_ld = {}

                def p1_sync(sync):
                    for tc in range(NC_S):
                        if tc >= 2:
                            sync.wait_ge(s_dv.h, dve_xn_at[tc - 2])
                            sync.wait_ge(s_se.h, se_base + 3 * (tc - 2) + 1)
                        xt_ld[tc] = pa[tc % 2].inc(sync.dma_start(
                            out=x_tiles[tc % 2][:, :],
                            in_=x_h[tc * 128:(tc + 1) * 128, :]), 16)

                block.sync(p1_sync)

                def p1_se(s):
                    for tc in range(NC_S):
                        s.wait_ge(pa[tc % 2].h, xt_ld[tc])
                        s_se.inc(s.activation(out=sq_scr[:, :],
                                              in_=x_tiles[tc % 2][:, :],
                                              func=AF.Square,
                                              accum_out=st_st[:, 1:2]))
                        s.wait_ge(s_dv.h, dve_var_at[tc])
                        s_se.inc(s.activation(out=st_st[:, 4:5], in_=st_st[:, 3:4],
                                              func=AF.Ln, bias=c_eps[:, 0:1],
                                              scale=1.0))
                        s.wait_ge(s_se.h, s_se.n)
                        s_se.inc(s.activation(out=st_st[:, 5:6], in_=st_st[:, 4:5],
                                              func=AF.Exp, scale=-0.5))

                def p1_v(v):
                    for tc in range(NC_S):
                        v.wait_ge(pa[tc % 2].h, xt_ld[tc])
                        v.wait_ge(s_se.h, se_base + 3 * tc + 1)
                        s_dv.inc(v.reduce_sum(out=st_st[:, 0:1],
                                              in_=x_tiles[tc % 2][:, :],
                                              axis=AXX))
                        v.wait_ge(s_dv.h, s_dv.n)
                        s_dv.inc(v.tensor_scalar(out=st_st[:, 2:3],
                                                 in0=st_st[:, 0:1],
                                                 scalar1=1.0 / D, scalar2=None,
                                                 op0=OP.mult))
                        v.wait_ge(s_dv.h, s_dv.n)
                        s_dv.inc(v.tensor_tensor(out=st_st[:, 6:7],
                                                 in0=st_st[:, 2:3],
                                                 in1=st_st[:, 2:3], op=OP.mult))
                        v.wait_ge(s_dv.h, s_dv.n)
                        s_dv.inc(v.scalar_tensor_tensor(
                            out=st_st[:, 3:4], in0=st_st[:, 1:2], scalar=1.0 / D,
                            in1=st_st[:, 6:7], op0=OP.mult, op1=OP.subtract))
                        v.wait_ge(s_se.h, se_base + 3 * tc + 3)
                        if tc >= 1:
                            v.wait_ge(s_pe.h, pe_base + tc * NC_D)
                        s_dv.inc(v.tensor_scalar(out=xn_tm[:, :],
                                                 in0=x_tiles[tc % 2][:, :],
                                                 scalar1=st_st[:, 2:3],
                                                 scalar2=st_st[:, 5:6],
                                                 op0=OP.subtract, op1=OP.mult))
                        for dc in range(NC_D):
                            fl = tc * NC_D + dc
                            v.wait_ge(s_pe.h, pe_base + fl + 1)
                            off = ((fl // 2) % 4) * 128
                            s_dv.inc(v.tensor_copy(
                                out=xn_fm[:, dc * S + tc * 128:
                                          dc * S + (tc + 1) * 128],
                                in_=ps_tt[fl % 2][:, off:off + 128]))

                def p1_pe(t):
                    for tc in range(NC_S):
                        t.wait_ge(s_dv.h, dve_xn_at[tc])
                        for dc in range(NC_D):
                            fl = tc * NC_D + dc
                            if fl >= 2:
                                t.wait_ge(s_dv.h, evac_at[fl - 2])
                            off = ((fl // 2) % 4) * 128
                            s_pe.inc(t.transpose(
                                ps_tt[fl % 2][:, off:off + 128],
                                xn_tm[:, dc * 128:(dc + 1) * 128], c_idbf[:, :]))

                block.scalar(p1_se)
                block.vector(p1_v)
                block.tensor(p1_pe)
                assert s_dv.n == sim_dv, (s_dv.n, sim_dv)
                p1_dv_end = s_dv.n

                def p1_spill(g):
                    g.wait_ge(s_dv.h, p1_dv_end)
                    for dc in range(NC_D):
                        sc_ld.inc(g.dma_start(
                            out=xn_d[dc * 128:(dc + 1) * 128, :],
                            in_=xn_fm[:, dc * S:(dc + 1) * S]), 16)
                block.gpsimd(p1_spill)
                xn_spilled = sc_ld.n

            es_kq = ExitStack()
            k_fm = es_kq.enter_context(nc.sbuf_tensor("k_fm", [128, NC_D * S], BF16))

            with (
                nc.sbuf_tensor("wk_sb", [128, NC_D * D], BF16) as wk_sb,
                nc.psum_tensor("ps_a", [128, 512], F32) as ps_a,
                nc.psum_tensor("ps_b", [128, 512], F32) as ps_b,
            ):
                def p2_wk(g):
                    for k in range(NC_D):
                        s_wkl.inc(g.dma_start(
                            out=wk_sb[:, k * D:(k + 1) * D],
                            in_=wk_h[k * 128:(k + 1) * 128, :]), 16)
                block.gpsimd(p2_wk)
                wk_loaded = s_wkl.n

                pst = [ps_a, ps_b]
                pe0 = s_pe.n
                sim = s_dv.n
                k_evac = {n: sim + n + 1 for n in range(NC_D * 4)}

                def p2_pe(t):
                    t.wait_ge(s_wkl.h, wk_loaded)
                    t.wait_ge(s_dv.h, p1_dv_end)
                    n = 0
                    for ec in range(NC_D):
                        for qq in range(4):
                            if n >= 2:
                                t.wait_ge(s_dv.h, k_evac[n - 2])
                            for k in range(NC_D):
                                mm = t.matmul(
                                    pst[n % 2][:, :],
                                    wk_sb[:, k * D + ec * 128: k * D + (ec + 1) * 128],
                                    xn_fm[:, k * S + qq * 512: k * S + (qq + 1) * 512],
                                    start=(k == 0), stop=(k == NC_D - 1))
                            s_pe.inc(mm)
                            n += 1

                def p2_dv(v):
                    n = 0
                    for ec in range(NC_D):
                        for qq in range(4):
                            v.wait_ge(s_pe.h, pe0 + n + 1)
                            s_dv.inc(v.tensor_copy(
                                out=k_fm[:, ec * S + qq * 512: ec * S + (qq + 1) * 512],
                                in_=pst[n % 2][:, :]))
                            n += 1

                block.tensor(p2_pe)
                p2_pe_end = s_pe.n
                block.vector(p2_dv)
                assert s_dv.n == sim + NC_D * 4
                p2_dv_end = s_dv.n

            with (
                nc.sbuf_tensor("wv_sb", [128, NC_D * D], BF16) as wv_sb,
                nc.sbuf_tensor("v_sb", [128, D], BF16) as v_sb,
                nc.psum_tensor("ps_a3", [128, 512], F32) as ps_a,
                nc.psum_tensor("ps_b3", [128, 512], F32) as ps_b,
            ):
                def p3_wv(g):
                    g.wait_ge(s_pe.h, p2_pe_end)
                    for k in range(NC_D):
                        s_wvl.inc(g.dma_start(
                            out=wv_sb[:, k * D:(k + 1) * D],
                            in_=wv_h[k * 128:(k + 1) * 128, :]), 16)
                block.gpsimd(p3_wv)
                wv_loaded = s_wvl.n

                pst = [ps_a, ps_b]
                pe0 = s_pe.n
                st0 = s_st.n
                sim = s_dv.n
                v_evac = {n: sim + n + 1 for n in range(NC_S * 4)}
                v_store = {tc: st0 + 16 * (tc + 1) for tc in range(NC_S)}

                def p3_pe(t):
                    t.wait_ge(s_wvl.h, wv_loaded)
                    t.wait_ge(s_dv.h, p2_dv_end)
                    n = 0
                    for tc in range(NC_S):
                        for qq in range(4):
                            if n >= 2:
                                t.wait_ge(s_dv.h, v_evac[n - 2])
                            for k in range(NC_D):
                                mm = t.matmul(
                                    pst[n % 2][:, :],
                                    xn_fm[:, k * S + tc * 128: k * S + (tc + 1) * 128],
                                    wv_sb[:, k * D + qq * 512: k * D + (qq + 1) * 512],
                                    start=(k == 0), stop=(k == NC_D - 1))
                            s_pe.inc(mm)
                            n += 1

                def p3_dv(v):
                    n = 0
                    for tc in range(NC_S):
                        if tc >= 1:
                            v.wait_ge(s_st.h, v_store[tc - 1])
                        for qq in range(4):
                            v.wait_ge(s_pe.h, pe0 + n + 1)
                            s_dv.inc(v.tensor_copy(
                                out=v_sb[:, qq * 512:(qq + 1) * 512],
                                in_=pst[n % 2][:, :]))
                            n += 1

                def p3_st(sc):
                    for tc in range(NC_S):
                        sc.wait_ge(s_dv.h, v_evac[tc * 4 + 3])
                        st_store(sc, v_d[tc * 128:(tc + 1) * 128, :], v_sb[:, :])

                block.tensor(p3_pe)
                p3_pe_end = s_pe.n
                block.vector(p3_dv)
                block.scalar(p3_st)
                assert s_st.n == st0 + 16 * NC_S
                p3_dv_end = s_dv.n
                p3_st_end = s_st.n

            es_q = ExitStack()
            q_fm = es_q.enter_context(nc.sbuf_tensor("q_fm", [128, NC_D * S], BF16))
            with (
                nc.sbuf_tensor("wq_sb0", [128, D], BF16) as wq_sb0,
                nc.sbuf_tensor("wq_sb1", [128, D], BF16) as wq_sb1,
                nc.psum_tensor("ps_a4", [128, 512], F32) as ps_a,
                nc.psum_tensor("ps_b4", [128, 512], F32) as ps_b,
            ):
                wq_tiles = [wq_sb0, wq_sb1]
                pst = [ps_a, ps_b]
                pe0 = s_pe.n
                q_pe_at = {ec: pe0 + (ec + 1) * 4 for ec in range(NC_D)}
                sim = s_dv.n
                q_evac = {n: sim + n + 1 for n in range(NC_D * 4)}
                wq_ld = {}

                def p4_wq(g):
                    g.wait_ge(s_pe.h, p3_pe_end)
                    for ec in range(NC_D):
                        if ec >= 2:
                            g.wait_ge(s_pe.h, q_pe_at[ec - 2])
                        src = wq_h[:, ec * 128:(ec + 1) * 128].rearrange(
                            "(k p) n -> p k n", p=128)
                        dst = wq_tiles[ec % 2][:, :].rearrange(
                            "p (k n) -> p k n", n=128)
                        wq_ld[ec] = pb[ec % 2].inc(g.dma_start(out=dst, in_=src), 16)
                block.gpsimd(p4_wq)

                def p4_pe(t):
                    t.wait_ge(s_dv.h, p3_dv_end)
                    n = 0
                    for ec in range(NC_D):
                        t.wait_ge(pb[ec % 2].h, wq_ld[ec])
                        for qq in range(4):
                            if n >= 2:
                                t.wait_ge(s_dv.h, q_evac[n - 2])
                            for k in range(NC_D):
                                mm = t.matmul(
                                    pst[n % 2][:, :],
                                    wq_tiles[ec % 2][:, k * 128:(k + 1) * 128],
                                    xn_fm[:, k * S + qq * 512: k * S + (qq + 1) * 512],
                                    start=(k == 0), stop=(k == NC_D - 1))
                            s_pe.inc(mm)
                            n += 1

                def p4_dv(v):
                    n = 0
                    for ec in range(NC_D):
                        for qq in range(4):
                            v.wait_ge(s_pe.h, pe0 + n + 1)
                            s_dv.inc(v.tensor_copy(
                                out=q_fm[:, ec * S + qq * 512: ec * S + (qq + 1) * 512],
                                in_=pst[n % 2][:, :]))
                            n += 1

                block.tensor(p4_pe)
                p4_pe_end = s_pe.n
                block.vector(p4_dv)
                p4_dv_end = s_dv.n
            es_xn.close()

            with ExitStack() as es5:
                sc_sb = es5.enter_context(nc.sbuf_tensor("sc_sb", [128, S], F32))
                pr_sb = es5.enter_context(nc.sbuf_tensor("pr_sb", [128, S], BF16))
                prT_sb = es5.enter_context(nc.sbuf_tensor("prT_sb", [128, S], BF16))
                vs0 = es5.enter_context(nc.sbuf_tensor("vs0", [128, D], BF16))
                vs1 = es5.enter_context(nc.sbuf_tensor("vs1", [128, D], BF16))
                att_sb = es5.enter_context(nc.sbuf_tensor("att_sb", [128, D], F32))
                smx = es5.enter_context(nc.sbuf_tensor("smx", [128, 4], F32))
                ps_s0 = es5.enter_context(nc.psum_tensor("ps_s0", [128, 512], F32))
                ps_s1 = es5.enter_context(nc.psum_tensor("ps_s1", [128, 512], F32))
                ps_at = es5.enter_context(nc.psum_tensor("ps_at", [128, 2048], F32))
                ps_t5a = es5.enter_context(nc.psum_tensor("ps_t5a", [128, 512], BF16))
                ps_t5b = es5.enter_context(nc.psum_tensor("ps_t5b", [128, 512], BF16))
                ps_tt5 = [ps_t5a, ps_t5b]
                v_tiles = [vs0, vs1]
                ps_s = [ps_s0, ps_s1]
                pe0, dv0, se0, st0 = s_pe.n, s_dv.n, s_se.n, s_st.n
                sc_evac = {}
                g_of = {}
                pr_done = {}
                trT_evac = []
                att_evac = {}
                se_exp = {}
                pe_sc = {}
                pe_tr = {}
                pe_at = {}
                vseq = []
                att_store = {}
                sim_pe, sim_dv, sim_se, sim_st = pe0, dv0, se0, st0
                g = 0
                for qi in range(NC_S):
                    J = (qi + 1) * 128
                    nj = (J + 511) // 512
                    for js in range(nj):
                        g_of[(qi, js)] = g
                        sim_pe += 1
                        pe_sc[g] = sim_pe
                        sim_dv += 1
                        sc_evac[g] = sim_dv
                        g += 1
                    sim_dv += 3
                    sim_se += 1
                    se_exp[qi] = sim_se
                    sim_dv += 1
                    pr_done[qi] = sim_dv
                    for jc in range(qi + 1):
                        sim_pe += 1
                        pe_tr[(qi, jc)] = sim_pe
                        sim_dv += 1
                        trT_evac.append(sim_dv)
                    for jc in range(qi + 1):
                        vseq.append((qi, jc))
                        sim_pe += 1
                        pe_at[(qi, jc)] = sim_pe
                    sim_dv += 4
                    att_evac[qi] = sim_dv
                    sim_st += 16
                    att_store[qi] = sim_st

                def tri(qi, jc):
                    return qi * (qi + 1) // 2 + jc

                v_ld = {}

                def p5_ld(sync):
                    sync.wait_ge(s_st.h, p3_st_end)
                    sync.wait_ge(s_pe.h, p4_pe_end)
                    li = 0
                    for qi in range(NC_S):
                        for jc in range(qi + 1):
                            if li >= 2:
                                sync.wait_ge(s_pe.h, pe_at[vseq[li - 2]])
                            v_ld[li] = pa[li % 2].inc(sync.dma_start(
                                out=v_tiles[li % 2][:, :],
                                in_=v_d[jc * 128:(jc + 1) * 128, :]), 16)
                            li += 1
                block.sync(p5_ld)

                def p5_pe(t):
                    t.wait_ge(s_dv.h, p4_dv_end)
                    gg = 0
                    li = 0
                    for qi in range(NC_S):
                        J = (qi + 1) * 128
                        nj = (J + 511) // 512
                        for js in range(nj):
                            n = min(512, J - js * 512)
                            if gg >= 2:
                                t.wait_ge(s_dv.h, sc_evac[gg - 2])
                            for e in range(NC_D):
                                mm = t.matmul(
                                    ps_s[gg % 2][:, 0:n],
                                    q_fm[:, e * S + qi * 128: e * S + (qi + 1) * 128],
                                    k_fm[:, e * S + js * 512: e * S + js * 512 + n],
                                    start=(e == 0), stop=(e == NC_D - 1))
                            s_pe.inc(mm)
                            gg += 1
                        t.wait_ge(s_dv.h, pr_done[qi])
                        for jc in range(qi + 1):
                            fl = tri(qi, jc)
                            if fl >= 2:
                                t.wait_ge(s_dv.h, trT_evac[fl - 2])
                            off = ((fl // 2) % 4) * 128
                            s_pe.inc(t.transpose(
                                ps_tt5[fl % 2][:, off:off + 128],
                                pr_sb[:, jc * 128:(jc + 1) * 128], c_idbf[:, :]))
                        if qi >= 1:
                            t.wait_ge(s_dv.h, att_evac[qi - 1])
                        for jc in range(qi + 1):
                            t.wait_ge(pa[li % 2].h, v_ld[li])
                            t.wait_ge(s_dv.h, trT_evac[tri(qi, jc)])
                            for es in range(4):
                                mm = t.matmul(
                                    ps_at[:, es * 512:(es + 1) * 512],
                                    prT_sb[:, jc * 128:(jc + 1) * 128],
                                    v_tiles[li % 2][:, es * 512:(es + 1) * 512],
                                    start=(jc == 0), stop=(jc == qi))
                            s_pe.inc(mm)
                            li += 1

                def p5_dv(v):
                    for qi in range(NC_S):
                        J = (qi + 1) * 128
                        nj = (J + 511) // 512
                        for js in range(nj):
                            n = min(512, J - js * 512)
                            gg = g_of[(qi, js)]
                            v.wait_ge(s_pe.h, pe_sc[gg])
                            s_dv.inc(v.tensor_copy(
                                out=sc_sb[:, js * 512: js * 512 + n],
                                in_=ps_s[gg % 2][:, 0:n]))
                        v.wait_ge(s_dv.h, s_dv.n)
                        s_dv.inc(v.tensor_tensor(
                            out=sc_sb[:, qi * 128:(qi + 1) * 128],
                            in0=sc_sb[:, qi * 128:(qi + 1) * 128],
                            in1=c_umask[:, :], op=OP.add))
                        v.wait_ge(s_dv.h, s_dv.n)
                        s_dv.inc(v.reduce_max(out=smx[:, 0:1], in_=sc_sb[:, 0:J],
                                              axis=AXX))
                        v.wait_ge(s_dv.h, s_dv.n)
                        s_dv.inc(v.tensor_scalar(out=smx[:, 1:2], in0=smx[:, 0:1],
                                                 scalar1=-SCALE, scalar2=None,
                                                 op0=OP.mult))
                        v.wait_ge(s_se.h, se_exp[qi])
                        s_dv.inc(v.reciprocal(out=smx[:, 3:4], in_=smx[:, 2:3]))
                        for jc in range(qi + 1):
                            v.wait_ge(s_pe.h, pe_tr[(qi, jc)])
                            fl = tri(qi, jc)
                            off = ((fl // 2) % 4) * 128
                            s_dv.inc(v.tensor_copy(
                                out=prT_sb[:, jc * 128:(jc + 1) * 128],
                                in_=ps_tt5[fl % 2][:, off:off + 128]))
                        v.wait_ge(s_pe.h, pe_at[(qi, qi)])
                        if qi >= 1:
                            v.wait_ge(s_st.h, att_store[qi - 1])
                        v.wait_ge(s_dv.h, s_dv.n)
                        for es in range(4):
                            s_dv.inc(v.tensor_scalar(
                                out=att_sb[:, es * 512:(es + 1) * 512],
                                in0=ps_at[:, es * 512:(es + 1) * 512],
                                scalar1=smx[:, 3:4], scalar2=None, op0=OP.mult))

                def p5_scalar(s):
                    for qi in range(NC_S):
                        J = (qi + 1) * 128
                        s.wait_ge(s_dv.h, pr_done[qi] - 1)
                        if qi >= 1:
                            s.wait_ge(s_pe.h, pe_tr[(qi - 1, qi - 1)])
                        s_se.inc(s.activation(out=pr_sb[:, 0:J], in_=sc_sb[:, 0:J],
                                              func=AF.Exp, bias=smx[:, 1:2],
                                              scale=SCALE, accum_out=smx[:, 2:3]))
                        if qi >= 1:
                            s.wait_ge(s_dv.h, att_evac[qi - 1])
                            st_store(s, att_d[(qi - 1) * 128: qi * 128, :],
                                     att_sb[:, :])
                    s.wait_ge(s_dv.h, att_evac[NC_S - 1])
                    st_store(s, att_d[(NC_S - 1) * 128: NC_S * 128, :], att_sb[:, :])

                block.tensor(p5_pe)
                block.vector(p5_dv)
                block.scalar(p5_scalar)
                assert s_pe.n == sim_pe, (s_pe.n, sim_pe)
                assert s_dv.n == sim_dv, (s_dv.n, sim_dv)
                assert s_se.n == sim_se and s_st.n == sim_st
                p5_st_end = s_st.n
            es_q.close()
            es_kq.close()

            with ExitStack() as es55:
                mi_t = [es55.enter_context(nc.sbuf_tensor(f"mi{i}", [128, INTER], I32))
                        for i in range(2)]
                ib_t = [es55.enter_context(nc.sbuf_tensor(f"ib{i}", [128, INTER], BF16))
                        for i in range(2)]
                pl_t = [es55.enter_context(nc.sbuf_tensor(f"pl{i}", [128, 16 * 512], BF16))
                        for i in range(3)]
                ma_t = [es55.enter_context(nc.sbuf_tensor(f"ma{i}", [128, INTER], BF16))
                        for i in range(2)]
                stg_t = [es55.enter_context(nc.sbuf_tensor(f"stg{i}", [128, INTER], BF16))
                         for i in range(2)]
                ps_acc = [es55.enter_context(nc.psum_tensor(f"ps_ac{i}", [128, 512], F32))
                          for i in range(2)]
                ps_trm = [es55.enter_context(nc.psum_tensor(f"ps_tr{i}", [128, 1024], BF16))
                          for i in range(4)]
                chunks = []
                for c in range(NC_I):
                    chunks.append((gidx_h, c * 128, D, NC_D, wg_d, c * 128))
                    chunks.append((uidx_h, c * 128, D, NC_D, wu_d, c * 128))
                for kk in range(NC_D):
                    chunks.append((didx_h, kk * 128, INTER, NC_I, wd_d, kk * 128))
                NCH = len(chunks)

                def mregions(w):
                    out = []
                    off = 0
                    while off < w:
                        out.append((off, min(512, w - off)))
                        off += 512
                    return out

                cast_at = {}
                masks_at = {}
                masks_last = {}
                accum_at = {}
                evac_at = {}
                evac_last = {}
                tr_at = {}
                trevac_at = {}
                trevac_done = {}
                store_at = {}
                greg = []
                gbt = []
                gbt_idx = {}
                sim_se, sim_dv, sim_pe = s_se.n, s_dv.n, s_pe.n
                sim_sp = [s_sp[i].n for i in range(4)]
                for ci in range(NCH + 1):
                    cj = ci - 1
                    if 0 <= cj < NCH:
                        ntr = chunks[cj][3]
                        for tr in range(ntr):
                            sim_pe += 1
                            tr_at[(cj, tr)] = sim_pe
                        nbt = (ntr + 3) // 4
                        for bt in range(nbt):
                            sim_dv += 1
                            trevac_at[(cj, bt)] = sim_dv
                            gbt_idx[(cj, bt)] = len(gbt)
                            gbt.append((cj, bt))
                        trevac_done[cj] = trevac_at[(cj, nbt - 1)]
                        par = cj % 2
                        sim_sp[2 * par + 0] += 16
                        sim_sp[2 * par + 1] += 16
                        store_at[cj] = (sim_sp[2 * par + 0], sim_sp[2 * par + 1])
                    if ci < NCH:
                        w = chunks[ci][2]
                        sim_se += 1
                        cast_at[ci] = sim_se
                        for ri in range(len(mregions(w))):
                            sim_dv += 16
                            masks_at[(ci, ri)] = sim_dv
                            sim_pe += 1
                            accum_at[(ci, ri)] = sim_pe
                            sim_se += 1
                            evac_at[(ci, ri)] = sim_se
                            greg.append((ci, ri))
                        masks_last[ci] = masks_at[(ci, len(mregions(w)) - 1)]
                        evac_last[ci] = evac_at[(ci, len(mregions(w)) - 1)]

                greg_idx = {cr: i for i, cr in enumerate(greg)}
                mld_a = {}
                mld_b = {}

                def _m_store(eng, cj, half):
                    src2, r02, w2, ntr2, dst2, dr02 = chunks[cj]
                    hw2 = w2 // 2
                    par = cj % 2
                    si = 2 * par + half
                    eng.wait_ge(s_dv.h, trevac_done[cj])
                    if cj >= 2:
                        eng.wait_ge(s_sp[si].h, store_at[cj - 2][half])
                    lo = 0 if half == 0 else hw2
                    hi = hw2 if half == 0 else w2
                    v = s_sp[si].inc(eng.dma_start(
                        out=dst2[dr02:dr02 + 128, lo:hi],
                        in_=stg_t[cj % 2][:, lo:hi]), 16)
                    assert v == store_at[cj][half], (cj, half, v, store_at[cj])

                def m_ld_sync(sync):
                    sync.wait_ge(s_st.h, p5_st_end)
                    for ci in range(NCH + 1):
                        if ci < NCH:
                            src, r0, w, ntr, _, _ = chunks[ci]
                            hw_ = w // 2
                            if ci >= 2:
                                sync.wait_ge(s_se.h, cast_at[ci - 2])
                            mld_a[ci] = pa[ci % 2].inc(sync.dma_start(
                                out=mi_t[ci % 2][:, 0:hw_],
                                in_=src[r0:r0 + 128, 0:hw_]), 16)
                        cj = ci - 1
                        if 0 <= cj < NCH:
                            _m_store(sync, cj, 0)
                block.sync(m_ld_sync)

                def m_ld_gps(g):
                    g.wait_ge(s_st.h, p5_st_end)
                    for ci in range(NCH + 1):
                        if ci < NCH:
                            src, r0, w, ntr, _, _ = chunks[ci]
                            hw_ = w // 2
                            if ci >= 2:
                                g.wait_ge(s_se.h, cast_at[ci - 2])
                            mld_b[ci] = pb[ci % 2].inc(g.dma_start(
                                out=mi_t[ci % 2][:, hw_:w],
                                in_=src[r0:r0 + 128, hw_:w]), 16)
                        cj = ci - 1
                        if 0 <= cj < NCH:
                            _m_store(g, cj, 1)
                block.gpsimd(m_ld_gps)

                def m_sc(sc):
                    for ci in range(NCH):
                        w = chunks[ci][2]
                        sc.wait_ge(pa[ci % 2].h, mld_a[ci])
                        sc.wait_ge(pb[ci % 2].h, mld_b[ci])
                        if ci >= 2:
                            sc.wait_ge(s_dv.h, masks_last[ci - 2])
                        n = s_se.inc(sc.copy(out=ib_t[ci % 2][:, 0:w],
                                             in_=mi_t[ci % 2][:, 0:w]))
                        assert n == cast_at[ci]
                        for ri, (off, wr) in enumerate(mregions(w)):
                            sc.wait_ge(s_pe.h, accum_at[(ci, ri)])
                            if ri == 0 and ci >= 2:
                                sc.wait_ge(s_pe.h,
                                           tr_at[(ci - 2, chunks[ci - 2][3] - 1)])
                            g_i = greg_idx[(ci, ri)]
                            n = s_se.inc(sc.copy(
                                out=ma_t[ci % 2][:, off:off + wr],
                                in_=ps_acc[g_i % 2][:, 0:wr]))
                            assert n == evac_at[(ci, ri)]
                block.scalar(m_sc)

                def m_dv(v):
                    g_i = 0
                    for ci in range(NCH + 1):
                        cj = ci - 1
                        if 0 <= cj < NCH:
                            ntr = chunks[cj][3]
                            nbt = (ntr + 3) // 4
                            for bt in range(nbt):
                                t0_, t1_ = bt * 4, min(ntr, bt * 4 + 4)
                                v.wait_ge(s_pe.h, tr_at[(cj, t1_ - 1)])
                                if bt == 0 and cj >= 2:
                                    v.wait_ge(s_sp[2 * (cj % 2)].h,
                                              store_at[cj - 2][0])
                                    v.wait_ge(s_sp[2 * (cj % 2) + 1].h,
                                              store_at[cj - 2][1])
                                gb = gbt_idx[(cj, bt)]
                                n = s_dv.inc(v.tensor_copy(
                                    out=stg_t[cj % 2][:, t0_ * 128:t1_ * 128],
                                    in_=ps_trm[gb % 4][:, 0:(t1_ - t0_) * 128]))
                                assert n == trevac_at[(cj, bt)]
                        if ci < NCH:
                            w = chunks[ci][2]
                            v.wait_ge(s_se.h, cast_at[ci])
                            for ri, (off, wr) in enumerate(mregions(w)):
                                if g_i >= 3:
                                    v.wait_ge(s_pe.h, accum_at[greg[g_i - 3]])
                                for k in range(16):
                                    s_dv.inc(v.tensor_scalar(
                                        out=pl_t[g_i % 3][:, k * 512:k * 512 + wr],
                                        in0=ib_t[ci % 2][:, off:off + wr],
                                        scalar1=float(k), scalar2=None,
                                        op0=OP.is_equal))
                                assert s_dv.n == masks_at[(ci, ri)]
                                g_i += 1
                block.vector(m_dv)

                def m_pe(t):
                    t.wait_ge(s_dv.h, p0_lutI_dv)
                    g_i = 0
                    for ci in range(NCH + 1):
                        cj = ci - 1
                        if 0 <= cj < NCH:
                            ntr = chunks[cj][3]
                            t.wait_ge(s_se.h, evac_last[cj])
                            nbt = (ntr + 3) // 4
                            for bt in range(nbt):
                                gb = gbt_idx[(cj, bt)]
                                if gb >= 4:
                                    t.wait_ge(s_dv.h, trevac_at[gbt[gb - 4]])
                                t0_, t1_ = bt * 4, min(ntr, bt * 4 + 4)
                                for tr in range(t0_, t1_):
                                    n = s_pe.inc(t.transpose(
                                        ps_trm[gb % 4][:, (tr - t0_) * 128:
                                                       (tr - t0_ + 1) * 128],
                                        ma_t[cj % 2][:, tr * 128:(tr + 1) * 128],
                                        c_idbf[:, :]))
                                    assert n == tr_at[(cj, tr)]
                        if ci < NCH:
                            w = chunks[ci][2]
                            for ri, (off, wr) in enumerate(mregions(w)):
                                t.wait_ge(s_dv.h, masks_at[(ci, ri)])
                                if g_i >= 2:
                                    t.wait_ge(s_se.h, evac_at[greg[g_i - 2]])
                                mm = None
                                for k in range(16):
                                    mm = t.matmul(
                                        ps_acc[g_i % 2][:, 0:wr],
                                        c_lutI[:, k * 128:(k + 1) * 128],
                                        pl_t[g_i % 3][:, k * 512:k * 512 + wr],
                                        start=(k == 0), stop=(k == 15))
                                n = s_pe.inc(mm)
                                assert n == accum_at[(ci, ri)]
                                g_i += 1
                block.tensor(m_pe)

                assert s_se.n == sim_se and s_dv.n == sim_dv and s_pe.n == sim_pe
                for i in range(4):
                    assert s_sp[i].n == sim_sp[i], (i, s_sp[i].n, sim_sp[i])

            with ExitStack() as es6:
                xn_fm2 = es6.enter_context(nc.sbuf_tensor("xn_fm2", [128, NC_D * S], BF16, side="right"))
                h_sb = es6.enter_context(nc.sbuf_tensor("h_sb", [128, NC_I * TF], BF16))
                out_sb = es6.enter_context(nc.sbuf_tensor("out_sb", [128, 4 * D], F32))
                wgt0 = es6.enter_context(nc.sbuf_tensor("wgt0", [128, D], BF16))
                wgt1 = es6.enter_context(nc.sbuf_tensor("wgt1", [128, D], BF16))
                wut0 = es6.enter_context(nc.sbuf_tensor("wut0", [128, D], BF16))
                wut1 = es6.enter_context(nc.sbuf_tensor("wut1", [128, D], BF16))
                wdt0 = es6.enter_context(nc.sbuf_tensor("wdt0", [128, INTER], BF16))
                wdt1 = es6.enter_context(nc.sbuf_tensor("wdt1", [128, INTER], BF16))
                xr_sb = es6.enter_context(nc.sbuf_tensor("xr_sb", [128, D], F32))
                ar_sb = es6.enter_context(nc.sbuf_tensor("ar_sb", [128, D], F32))
                gact = es6.enter_context(nc.sbuf_tensor("gact", [128, TF], F32))
                dn0 = es6.enter_context(nc.sbuf_tensor("dn0", [128, TF], F32))
                dn1 = es6.enter_context(nc.sbuf_tensor("dn1", [128, TF], F32))
                ps_g0 = es6.enter_context(nc.psum_tensor("ps_g0", [128, 512], F32))
                ps_g1 = es6.enter_context(nc.psum_tensor("ps_g1", [128, 512], F32))
                ps_u0 = es6.enter_context(nc.psum_tensor("ps_u0", [128, 512], F32))
                ps_u1 = es6.enter_context(nc.psum_tensor("ps_u1", [128, 512], F32))
                ps_d0 = es6.enter_context(nc.psum_tensor("ps_d0", [128, 512], F32))
                ps_d1 = es6.enter_context(nc.psum_tensor("ps_d1", [128, 512], F32))
                ps_o0 = es6.enter_context(nc.psum_tensor("ps_o0", [128, 512], F32))
                ps_o1 = es6.enter_context(nc.psum_tensor("ps_o1", [128, 512], F32))
                ps_oo = [ps_o0, ps_o1]
                wg_t = [wgt0, wgt1]
                wu_t = [wut0, wut1]
                wd_t = [wdt0, wdt1]
                dn_t = [dn0, dn1]
                psg = [ps_g0, ps_g1]
                psu = [ps_u0, ps_u1]
                psd = [ps_d0, ps_d1]

                def p6_xn(sync):
                    sync.wait_ge(sc_ld.h, xn_spilled)
                    sync.wait_ge(s_st.h, p5_st_end)
                    for dc in range(NC_D):
                        sc_ld.inc(sync.dma_start(
                            out=xn_fm2[:, dc * S:(dc + 1) * S],
                            in_=xn_d[dc * 128:(dc + 1) * 128, :]), 16)
                block.sync(p6_xn)
                xn_reloaded = sc_ld.n

                pe0, dv0, se0, st0 = s_pe.n, s_dv.n, s_se.n, s_st.n
                sim_pe, sim_dv, sim_se, sim_st = pe0, dv0, se0, st0
                gu_pe = {}
                h_dv = {}
                silu_se = {}
                res_dv = {}
                dn_pe = {}
                dn_dv = {}
                tr_pe = {}
                acc_dv = {}
                out_st = {}
                gu_list = []
                dn_list = []
                for pp in range(NPASS):
                    for tq in range(4):
                        sim_dv += 1
                        res_dv[(pp, tq)] = sim_dv
                    for c in range(NC_I):
                        gu_list.append((pp, c))
                        sim_pe += 2
                        gu_pe[(pp, c)] = sim_pe
                        sim_se += 1
                        silu_se[(pp, c)] = sim_se
                        sim_dv += 1
                        h_dv[(pp, c)] = sim_dv
                    for kk in range(NC_D):
                        dn_list.append((pp, kk))
                        sim_pe += 1
                        dn_pe[(pp, kk)] = sim_pe
                        sim_dv += 1
                        dn_dv[(pp, kk)] = sim_dv
                        for tq in range(4):
                            sim_pe += 1
                            tr_pe[(pp, kk, tq)] = sim_pe
                            sim_dv += 1
                            acc_dv[(pp, kk, tq)] = sim_dv
                    for tq in range(4):
                        sim_st += 16
                        out_st[(pp, tq)] = sim_st

                wldg = {}
                wldd = {}

                def _wait_mat(eng, ci):
                    par = ci % 2
                    eng.wait_ge(s_sp[2 * par].h, store_at[ci][0])
                    eng.wait_ge(s_sp[2 * par + 1].h, store_at[ci][1])

                def p6_wld(sync):
                    gi = 0
                    di = 0
                    for pp in range(NPASS):
                        for c in range(NC_I):
                            if gi >= 2:
                                sync.wait_ge(s_pe.h, gu_pe[gu_list[gi - 2]])
                            if pp == 0:
                                _wait_mat(sync, 2 * c)
                                _wait_mat(sync, 2 * c + 1)
                            sg = pa[gi % 2]
                            sg.inc(sync.dma_start(
                                out=wg_t[gi % 2][:, :],
                                in_=wg_d[c * 128:(c + 1) * 128, :]), 16)
                            wldg[(pp, c)] = sg.inc(sync.dma_start(
                                out=wu_t[gi % 2][:, :],
                                in_=wu_d[c * 128:(c + 1) * 128, :]), 16)
                            gi += 1
                        for kk in range(NC_D):
                            if di >= 2:
                                sync.wait_ge(s_pe.h, dn_pe[dn_list[di - 2]])
                            if pp == 0:
                                _wait_mat(sync, 2 * NC_I + kk)
                            wldd[(pp, kk)] = pb[di % 2].inc(sync.dma_start(
                                out=wd_t[di % 2][:, :],
                                in_=wd_d[kk * 128:(kk + 1) * 128, :]), 16)
                            di += 1
                block.sync(p6_wld)

                xld = {}

                def p6_xld(g):
                    g.wait_ge(s_st.h, p5_st_end)
                    for pp in range(NPASS):
                        for tq in range(4):
                            r0 = pp * TF + tq * 128
                            s_xr.inc(g.dma_start(out=xr_sb[:, :],
                                                 in_=x_h[r0:r0 + 128, :]), 16)
                            xld[(pp, tq)] = s_xr.inc(g.dma_start(
                                out=ar_sb[:, :],
                                in_=att_d[r0:r0 + 128, :]), 16)
                            g.wait_ge(s_dv.h, res_dv[(pp, tq)])
                block.gpsimd(p6_xld)

                def p6_pe(t):
                    t.wait_ge(sc_ld.h, xn_reloaded)
                    gi = 0
                    di = 0
                    for pp in range(NPASS):
                        for c in range(NC_I):
                            t.wait_ge(pa[gi % 2].h, wldg[(pp, c)])
                            if gi >= 2:
                                prev = gu_list[gi - 2]
                                t.wait_ge(s_dv.h, h_dv[prev])
                                t.wait_ge(s_se.h, silu_se[prev])
                            for k in range(NC_D):
                                mm = t.matmul(
                                    psg[gi % 2][:, :],
                                    wg_t[gi % 2][:, k * 128:(k + 1) * 128],
                                    xn_fm2[:, k * S + pp * TF: k * S + (pp + 1) * TF],
                                    start=(k == 0), stop=(k == NC_D - 1))
                            s_pe.inc(mm)
                            for k in range(NC_D):
                                mm = t.matmul(
                                    psu[gi % 2][:, :],
                                    wu_t[gi % 2][:, k * 128:(k + 1) * 128],
                                    xn_fm2[:, k * S + pp * TF: k * S + (pp + 1) * TF],
                                    start=(k == 0), stop=(k == NC_D - 1))
                            s_pe.inc(mm)
                            gi += 1
                        for kk in range(NC_D):
                            t.wait_ge(pb[di % 2].h, wldd[(pp, kk)])
                            t.wait_ge(s_dv.h, h_dv[(pp, NC_I - 1)])
                            if di >= 2:
                                t.wait_ge(s_dv.h, dn_dv[dn_list[di - 2]])
                            for i in range(NC_I):
                                mm = t.matmul(
                                    psd[di % 2][:, :],
                                    wd_t[di % 2][:, i * 128:(i + 1) * 128],
                                    h_sb[:, i * TF:(i + 1) * TF],
                                    start=(i == 0), stop=(i == NC_I - 1))
                            s_pe.inc(mm)
                            t.wait_ge(s_dv.h, dn_dv[(pp, kk)])
                            for tq in range(4):
                                gl_tr = di * 4 + tq
                                if gl_tr >= 2:
                                    pfl = gl_tr - 2
                                    pk = dn_list[pfl // 4]
                                    t.wait_ge(s_dv.h,
                                              acc_dv[(pk[0], pk[1], pfl % 4)])
                                s_pe.inc(t.transpose(
                                    ps_oo[tq % 2][:, (tq // 2) * 128:
                                                  (tq // 2 + 1) * 128],
                                    dn_t[di % 2][:, tq * 128:(tq + 1) * 128],
                                    c_id32[:, :]))
                            di += 1

                def p6_scalar(s):
                    gi = 0
                    for pp in range(NPASS):
                        for c in range(NC_I):
                            s.wait_ge(s_pe.h, gu_pe[(pp, c)] - 1)
                            if gi >= 1:
                                s.wait_ge(s_dv.h, h_dv[gu_list[gi - 1]])
                            s_se.inc(s.activation(out=gact[:, :],
                                                  in_=psg[gi % 2][:, :],
                                                  func=AF.Silu))
                            gi += 1
                        for tq in range(4):
                            s.wait_ge(s_dv.h, acc_dv[(pp, NC_D - 1, tq)])
                            r0 = pp * TF + tq * 128
                            st_store(s, out_h[r0:r0 + 128, :],
                                     out_sb[:, tq * D:(tq + 1) * D])
                    s.wait_ge(s_st.h, s_st.n)

                def p6_dv(v):
                    gi = 0
                    di = 0
                    for pp in range(NPASS):
                        for tq in range(4):
                            v.wait_ge(s_xr.h, xld[(pp, tq)])
                            if pp > 0:
                                v.wait_ge(s_st.h, out_st[(pp - 1, 3)])
                            s_dv.inc(v.tensor_tensor(
                                out=out_sb[:, tq * D:(tq + 1) * D],
                                in0=xr_sb[:, :], in1=ar_sb[:, :], op=OP.add))
                        for c in range(NC_I):
                            v.wait_ge(s_se.h, silu_se[(pp, c)])
                            v.wait_ge(s_pe.h, gu_pe[(pp, c)])
                            s_dv.inc(v.tensor_tensor(
                                out=h_sb[:, c * TF:(c + 1) * TF],
                                in0=gact[:, :], in1=psu[gi % 2][:, :], op=OP.mult))
                            gi += 1
                        for kk in range(NC_D):
                            v.wait_ge(s_pe.h, dn_pe[(pp, kk)])
                            if di >= 2:
                                pk = dn_list[di - 2]
                                v.wait_ge(s_pe.h, tr_pe[(pk[0], pk[1], 3)])
                            s_dv.inc(v.tensor_copy(out=dn_t[di % 2][:, :],
                                                   in_=psd[di % 2][:, :]))
                            for tq in range(4):
                                v.wait_ge(s_pe.h, tr_pe[(pp, kk, tq)])
                                s_dv.inc(v.tensor_tensor(
                                    out=out_sb[:, tq * D + kk * 128:
                                               tq * D + (kk + 1) * 128],
                                    in0=out_sb[:, tq * D + kk * 128:
                                               tq * D + (kk + 1) * 128],
                                    in1=ps_oo[tq % 2][:, (tq // 2) * 128:
                                                      (tq // 2 + 1) * 128],
                                    op=OP.add))
                            di += 1

                block.tensor(p6_pe)
                block.scalar(p6_scalar)
                block.vector(p6_dv)
                assert s_pe.n == sim_pe, (s_pe.n, sim_pe)
                assert s_dv.n == sim_dv, (s_dv.n, sim_dv)
                assert s_se.n == sim_se and s_st.n == sim_st

    return nc


_NC_CACHE = {}


def kernel(L0_attn_x, L0_attn_Wq, L0_attn_Wk, L0_attn_Wv,
           L0_ffn_lut, L0_ffn_gate_idx, L0_ffn_up_idx, L0_ffn_down_idx):
    x = np.ascontiguousarray(np.asarray(L0_attn_x, dtype=np.float32))
    wq = np.ascontiguousarray(np.asarray(L0_attn_Wq, dtype=np.float32))
    wk = np.ascontiguousarray(np.asarray(L0_attn_Wk, dtype=np.float32))
    wv = np.ascontiguousarray(np.asarray(L0_attn_Wv, dtype=np.float32))
    lut = np.ascontiguousarray(np.asarray(L0_ffn_lut, dtype=np.float32)).reshape(1, 16)
    gidx = np.ascontiguousarray(np.asarray(L0_ffn_gate_idx, dtype=np.int32))
    uidx = np.ascontiguousarray(np.asarray(L0_ffn_up_idx, dtype=np.int32))
    didx = np.ascontiguousarray(np.asarray(L0_ffn_down_idx, dtype=np.int32))

    if "nc" not in _NC_CACHE:
        _NC_CACHE["nc"] = build_program()
    nc = _NC_CACHE["nc"]

    in_maps = []
    for b in range(8):
        in_maps.append({
            "x": np.ascontiguousarray(x[b]),
            "wq": wq, "wk": wk, "wv": wv, "lut": lut,
            "gidx": gidx, "uidx": uidx, "didx": didx,
        })
    res = run_bass_kernel_spmd(nc, in_maps, core_ids=list(range(8)))
    out = np.stack([np.asarray(res.results[b]["out"]) for b in range(8)], axis=0)
    return out.astype(np.float32)


# version 10
# speedup vs baseline: 1.1880x; 1.0040x over previous
import numpy as np
import ml_dtypes
from contextlib import ExitStack

import concourse.bass as bass
import concourse.mybir as mybir
from concourse.bass_utils import run_bass_kernel_spmd

F32 = mybir.dt.float32
BF16 = mybir.dt.bfloat16
I32 = mybir.dt.int32
AF = mybir.ActivationFunctionType
OP = mybir.AluOpType
AXX = mybir.AxisListType.X

S = 2048
D = 2048
INTER = 5504
NC_D = D // 128
NC_S = S // 128
NC_I = INTER // 128
TF = 512
NPASS = S // TF
SCALE = 1.0 / float(np.sqrt(2048.0))
EPS = 1e-5
NEG = -1e30


class Sem:
    def __init__(self, ctx, nc, name):
        self.h = ctx.enter_context(nc.semaphore(name))
        self.n = 0

    def inc(self, instr, k=1):
        instr.then_inc(self.h, k)
        self.n += k
        return self.n


def build_program():
    nc = bass.Bass("TRN2", target_bir_lowering=False, debug=False)

    x_h = nc.dram_tensor("x", [S, D], F32, kind="ExternalInput").ap()
    wq_h = nc.dram_tensor("wq", [D, D], F32, kind="ExternalInput").ap()
    wk_h = nc.dram_tensor("wk", [D, D], F32, kind="ExternalInput").ap()
    wv_h = nc.dram_tensor("wv", [D, D], F32, kind="ExternalInput").ap()
    lut_h = nc.dram_tensor("lut", [1, 16], F32, kind="ExternalInput").ap()
    gidx_h = nc.dram_tensor("gidx", [INTER, D], I32, kind="ExternalInput").ap()
    uidx_h = nc.dram_tensor("uidx", [INTER, D], I32, kind="ExternalInput").ap()
    didx_h = nc.dram_tensor("didx", [D, INTER], I32, kind="ExternalInput").ap()
    out_h = nc.dram_tensor("out", [S, D], F32, kind="ExternalOutput").ap()

    xn_d = nc.dram_tensor("xn_d", [NC_D * 128, S], BF16, kind="Internal").ap()
    v_d = nc.dram_tensor("v_d", [S, D], BF16, kind="Internal").ap()
    att_d = nc.dram_tensor("att_d", [S, D], F32, kind="Internal").ap()
    wg_d = nc.dram_tensor("wg_d", [NC_I * 128, D], BF16, kind="Internal").ap()
    wu_d = nc.dram_tensor("wu_d", [NC_I * 128, D], BF16, kind="Internal").ap()
    wd_d = nc.dram_tensor("wd_d", [NC_D * 128, INTER], BF16, kind="Internal").ap()

    ident_bf = nc.inline_tensor(np.eye(128, dtype=np.float32).astype(
        ml_dtypes.bfloat16), name="ident_bf")
    ident_f32 = nc.inline_tensor(np.eye(128, dtype=np.float32), name="ident_f32")

    with ExitStack() as ctx:
        c_idbf = ctx.enter_context(nc.sbuf_tensor("c_idbf", [128, 128], BF16))
        c_id32 = ctx.enter_context(nc.sbuf_tensor("c_id32", [128, 128], F32))
        c_lut = ctx.enter_context(nc.sbuf_tensor("c_lut", [1, 16], F32))
        c_lutI = ctx.enter_context(nc.sbuf_tensor("c_lutI", [128, 16 * 128], BF16))
        c_ones = ctx.enter_context(nc.sbuf_tensor("c_ones", [1, 128], F32))
        c_lutb = ctx.enter_context(nc.sbuf_tensor("c_lutb", [128, 16], F32))
        c_umask = ctx.enter_context(nc.sbuf_tensor("c_umask", [128, 128], F32))
        c_eps = ctx.enter_context(nc.sbuf_tensor("c_eps", [128, 1], F32))
        st_st = ctx.enter_context(nc.sbuf_tensor("st_st", [128, 8], F32))

        s_pe = Sem(ctx, nc, "s_pe")
        s_dv = Sem(ctx, nc, "s_dv")
        s_se = Sem(ctx, nc, "s_se")
        s_gp = Sem(ctx, nc, "s_gp")
        s_st = Sem(ctx, nc, "s_st")
        sc_ld = Sem(ctx, nc, "sc_ld")
        s_wkl = Sem(ctx, nc, "s_wkl")
        s_wvl = Sem(ctx, nc, "s_wvl")
        s_xr = Sem(ctx, nc, "s_xr")
        pa = [Sem(ctx, nc, "pa0"), Sem(ctx, nc, "pa1")]
        pb = [Sem(ctx, nc, "pb0"), Sem(ctx, nc, "pb1")]
        s_sp = [Sem(ctx, nc, f"sp{i}") for i in range(4)]

        def st_store(eng, out_ap, in_ap):
            s_st.inc(eng.dma_start(out=out_ap, in_=in_ap), 16)
            eng.wait_ge(s_st.h, s_st.n)
            return s_st.n

        with nc.Block() as block:
            def p0_sync(sync):
                sc_ld.inc(sync.dma_start(out=c_idbf[:, :], in_=ident_bf.ap()), 16)
                sc_ld.inc(sync.dma_start(out=c_id32[:, :], in_=ident_f32.ap()), 16)
                sc_ld.inc(sync.dma_start(out=c_lut[:, :], in_=lut_h[:, :]), 16)
            block.sync(p0_sync)
            p0_ld = sc_ld.n

            def p0_gp(g):
                g.memset(c_umask[:, :], 0.0)
                s_gp.inc(g.affine_select(out=c_umask[:, :], in_=c_umask[:, :],
                                         compare_op=OP.is_ge, fill=NEG, base=0,
                                         pattern=[[-1, 128]], channel_multiplier=1))
            block.gpsimd(p0_gp)

            def p0_v(v):
                v.memset(c_ones[:, :], 1.0)
                s_dv.inc(v.memset(c_eps[:, :], EPS))
            block.vector(p0_v)
            p0_dv = s_dv.n

            with nc.psum_tensor("ps_bc", [128, 16], F32) as ps_bc:
                def p0_pe(t):
                    t.wait_ge(sc_ld.h, p0_ld)
                    t.wait_ge(s_dv.h, p0_dv)
                    s_pe.inc(t.matmul(ps_bc[:, :], c_ones[:, :], c_lut[:, :],
                                      start=True, stop=True))
                block.tensor(p0_pe)
                p0_pe_n = s_pe.n

                def p0_v2(v):
                    v.wait_ge(s_pe.h, p0_pe_n)
                    s_dv.inc(v.tensor_copy(out=c_lutb[:, :], in_=ps_bc[:, :]))
                    v.wait_ge(s_dv.h, s_dv.n)
                    for k in range(16):
                        s_dv.inc(v.tensor_scalar(
                            out=c_lutI[:, k * 128:(k + 1) * 128],
                            in0=c_idbf[:, :], scalar1=c_lutb[:, k:k + 1],
                            scalar2=None, op0=OP.mult))
                block.vector(p0_v2)
                p0_lutI_dv = s_dv.n

            es_xn = ExitStack()
            xn_fm = es_xn.enter_context(
                nc.sbuf_tensor("xn_fm", [128, NC_D * S], BF16, side="right"))
            with (
                nc.sbuf_tensor("x_t0", [128, D], F32) as x_t0,
                nc.sbuf_tensor("x_t1", [128, D], F32) as x_t1,
                nc.sbuf_tensor("sq_scr", [128, D], F32) as sq_scr,
                nc.sbuf_tensor("xn_tm", [128, D], BF16) as xn_tm,
                nc.psum_tensor("ps_ta", [128, 512], BF16) as ps_ta,
                nc.psum_tensor("ps_tb", [128, 512], BF16) as ps_tb,
            ):
                ps_tt = [ps_ta, ps_tb]
                x_tiles = [x_t0, x_t1]
                se_base = s_se.n
                pe_base = s_pe.n
                dve_var_at = {}
                dve_xn_at = {}
                evac_at = {}
                sim_dv = s_dv.n
                for tc in range(NC_S):
                    sim_dv += 4
                    dve_var_at[tc] = sim_dv
                    sim_dv += 1
                    dve_xn_at[tc] = sim_dv
                    for dc in range(NC_D):
                        sim_dv += 1
                        evac_at[tc * NC_D + dc] = sim_dv

                xt_ld = {}

                def p1_sync(sync):
                    for tc in range(NC_S):
                        if tc >= 2:
                            sync.wait_ge(s_dv.h, dve_xn_at[tc - 2])
                            sync.wait_ge(s_se.h, se_base + 3 * (tc - 2) + 1)
                        xt_ld[tc] = pa[tc % 2].inc(sync.dma_start(
                            out=x_tiles[tc % 2][:, :],
                            in_=x_h[tc * 128:(tc + 1) * 128, :]), 16)

                block.sync(p1_sync)

                def p1_se(s):
                    for tc in range(NC_S):
                        s.wait_ge(pa[tc % 2].h, xt_ld[tc])
                        s_se.inc(s.activation(out=sq_scr[:, :],
                                              in_=x_tiles[tc % 2][:, :],
                                              func=AF.Square,
                                              accum_out=st_st[:, 1:2]))
                        s.wait_ge(s_dv.h, dve_var_at[tc])
                        s_se.inc(s.activation(out=st_st[:, 4:5], in_=st_st[:, 3:4],
                                              func=AF.Ln, bias=c_eps[:, 0:1],
                                              scale=1.0))
                        s.wait_ge(s_se.h, s_se.n)
                        s_se.inc(s.activation(out=st_st[:, 5:6], in_=st_st[:, 4:5],
                                              func=AF.Exp, scale=-0.5))

                def p1_v(v):
                    for tc in range(NC_S):
                        v.wait_ge(pa[tc % 2].h, xt_ld[tc])
                        v.wait_ge(s_se.h, se_base + 3 * tc + 1)
                        s_dv.inc(v.reduce_sum(out=st_st[:, 0:1],
                                              in_=x_tiles[tc % 2][:, :],
                                              axis=AXX))
                        v.wait_ge(s_dv.h, s_dv.n)
                        s_dv.inc(v.tensor_scalar(out=st_st[:, 2:3],
                                                 in0=st_st[:, 0:1],
                                                 scalar1=1.0 / D, scalar2=None,
                                                 op0=OP.mult))
                        v.wait_ge(s_dv.h, s_dv.n)
                        s_dv.inc(v.tensor_tensor(out=st_st[:, 6:7],
                                                 in0=st_st[:, 2:3],
                                                 in1=st_st[:, 2:3], op=OP.mult))
                        v.wait_ge(s_dv.h, s_dv.n)
                        s_dv.inc(v.scalar_tensor_tensor(
                            out=st_st[:, 3:4], in0=st_st[:, 1:2], scalar=1.0 / D,
                            in1=st_st[:, 6:7], op0=OP.mult, op1=OP.subtract))
                        v.wait_ge(s_se.h, se_base + 3 * tc + 3)
                        if tc >= 1:
                            v.wait_ge(s_pe.h, pe_base + tc * NC_D)
                        s_dv.inc(v.tensor_scalar(out=xn_tm[:, :],
                                                 in0=x_tiles[tc % 2][:, :],
                                                 scalar1=st_st[:, 2:3],
                                                 scalar2=st_st[:, 5:6],
                                                 op0=OP.subtract, op1=OP.mult))
                        for dc in range(NC_D):
                            fl = tc * NC_D + dc
                            v.wait_ge(s_pe.h, pe_base + fl + 1)
                            off = ((fl // 2) % 4) * 128
                            s_dv.inc(v.tensor_copy(
                                out=xn_fm[:, dc * S + tc * 128:
                                          dc * S + (tc + 1) * 128],
                                in_=ps_tt[fl % 2][:, off:off + 128]))

                def p1_pe(t):
                    for tc in range(NC_S):
                        t.wait_ge(s_dv.h, dve_xn_at[tc])
                        for dc in range(NC_D):
                            fl = tc * NC_D + dc
                            if fl >= 2:
                                t.wait_ge(s_dv.h, evac_at[fl - 2])
                            off = ((fl // 2) % 4) * 128
                            s_pe.inc(t.transpose(
                                ps_tt[fl % 2][:, off:off + 128],
                                xn_tm[:, dc * 128:(dc + 1) * 128], c_idbf[:, :]))

                block.scalar(p1_se)
                block.vector(p1_v)
                block.tensor(p1_pe)
                assert s_dv.n == sim_dv, (s_dv.n, sim_dv)
                p1_dv_end = s_dv.n

                def p1_spill(g):
                    g.wait_ge(s_dv.h, p1_dv_end)
                    for dc in range(NC_D):
                        sc_ld.inc(g.dma_start(
                            out=xn_d[dc * 128:(dc + 1) * 128, :],
                            in_=xn_fm[:, dc * S:(dc + 1) * S]), 16)
                block.gpsimd(p1_spill)
                xn_spilled = sc_ld.n

            es_kq = ExitStack()
            k_fm = es_kq.enter_context(nc.sbuf_tensor("k_fm", [128, NC_D * S], BF16))

            with (
                nc.sbuf_tensor("wk_sb", [128, NC_D * D], BF16) as wk_sb,
                nc.psum_tensor("ps_a", [128, 512], F32) as ps_a,
                nc.psum_tensor("ps_b", [128, 512], F32) as ps_b,
            ):
                def p2_wk(g):
                    for k in range(NC_D):
                        s_wkl.inc(g.dma_start(
                            out=wk_sb[:, k * D:(k + 1) * D],
                            in_=wk_h[k * 128:(k + 1) * 128, :]), 16)
                block.gpsimd(p2_wk)
                wk_loaded = s_wkl.n

                pst = [ps_a, ps_b]
                pe0 = s_pe.n
                sim = s_dv.n
                k_evac = {n: sim + n + 1 for n in range(NC_D * 4)}

                def p2_pe(t):
                    t.wait_ge(s_wkl.h, wk_loaded)
                    t.wait_ge(s_dv.h, p1_dv_end)
                    n = 0
                    for ec in range(NC_D):
                        for qq in range(4):
                            if n >= 2:
                                t.wait_ge(s_dv.h, k_evac[n - 2])
                            for k in range(NC_D):
                                mm = t.matmul(
                                    pst[n % 2][:, :],
                                    wk_sb[:, k * D + ec * 128: k * D + (ec + 1) * 128],
                                    xn_fm[:, k * S + qq * 512: k * S + (qq + 1) * 512],
                                    start=(k == 0), stop=(k == NC_D - 1))
                            s_pe.inc(mm)
                            n += 1

                def p2_dv(v):
                    n = 0
                    for ec in range(NC_D):
                        for qq in range(4):
                            v.wait_ge(s_pe.h, pe0 + n + 1)
                            s_dv.inc(v.tensor_copy(
                                out=k_fm[:, ec * S + qq * 512: ec * S + (qq + 1) * 512],
                                in_=pst[n % 2][:, :]))
                            n += 1

                block.tensor(p2_pe)
                p2_pe_end = s_pe.n
                block.vector(p2_dv)
                assert s_dv.n == sim + NC_D * 4
                p2_dv_end = s_dv.n

            with (
                nc.sbuf_tensor("wv_sb", [128, NC_D * D], BF16) as wv_sb,
                nc.sbuf_tensor("v_sb", [128, D], BF16) as v_sb,
                nc.psum_tensor("ps_a3", [128, 512], F32) as ps_a,
                nc.psum_tensor("ps_b3", [128, 512], F32) as ps_b,
            ):
                def p3_wv(g):
                    g.wait_ge(s_pe.h, p2_pe_end)
                    for k in range(NC_D):
                        s_wvl.inc(g.dma_start(
                            out=wv_sb[:, k * D:(k + 1) * D],
                            in_=wv_h[k * 128:(k + 1) * 128, :]), 16)
                block.gpsimd(p3_wv)
                wv_loaded = s_wvl.n

                pst = [ps_a, ps_b]
                pe0 = s_pe.n
                st0 = s_st.n
                sim = s_dv.n
                v_evac = {n: sim + n + 1 for n in range(NC_S * 4)}
                v_store = {tc: st0 + 16 * (tc + 1) for tc in range(NC_S)}

                def p3_pe(t):
                    t.wait_ge(s_wvl.h, wv_loaded)
                    t.wait_ge(s_dv.h, p2_dv_end)
                    n = 0
                    for tc in range(NC_S):
                        for qq in range(4):
                            if n >= 2:
                                t.wait_ge(s_dv.h, v_evac[n - 2])
                            for k in range(NC_D):
                                mm = t.matmul(
                                    pst[n % 2][:, :],
                                    xn_fm[:, k * S + tc * 128: k * S + (tc + 1) * 128],
                                    wv_sb[:, k * D + qq * 512: k * D + (qq + 1) * 512],
                                    start=(k == 0), stop=(k == NC_D - 1))
                            s_pe.inc(mm)
                            n += 1

                def p3_dv(v):
                    n = 0
                    for tc in range(NC_S):
                        if tc >= 1:
                            v.wait_ge(s_st.h, v_store[tc - 1])
                        for qq in range(4):
                            v.wait_ge(s_pe.h, pe0 + n + 1)
                            s_dv.inc(v.tensor_copy(
                                out=v_sb[:, qq * 512:(qq + 1) * 512],
                                in_=pst[n % 2][:, :]))
                            n += 1

                def p3_st(sc):
                    for tc in range(NC_S):
                        sc.wait_ge(s_dv.h, v_evac[tc * 4 + 3])
                        st_store(sc, v_d[tc * 128:(tc + 1) * 128, :], v_sb[:, :])

                block.tensor(p3_pe)
                p3_pe_end = s_pe.n
                block.vector(p3_dv)
                block.scalar(p3_st)
                assert s_st.n == st0 + 16 * NC_S
                p3_dv_end = s_dv.n
                p3_st_end = s_st.n

            es_q = ExitStack()
            q_fm = es_q.enter_context(nc.sbuf_tensor("q_fm", [128, NC_D * S], BF16))
            with (
                nc.sbuf_tensor("wq_sb0", [128, D], BF16) as wq_sb0,
                nc.sbuf_tensor("wq_sb1", [128, D], BF16) as wq_sb1,
                nc.psum_tensor("ps_a4", [128, 512], F32) as ps_a,
                nc.psum_tensor("ps_b4", [128, 512], F32) as ps_b,
            ):
                wq_tiles = [wq_sb0, wq_sb1]
                pst = [ps_a, ps_b]
                pe0 = s_pe.n
                q_pe_at = {ec: pe0 + (ec + 1) * 4 for ec in range(NC_D)}
                sim = s_dv.n
                q_evac = {n: sim + n + 1 for n in range(NC_D * 4)}
                wq_ld = {}

                def p4_wq(g):
                    g.wait_ge(s_pe.h, p3_pe_end)
                    for ec in range(NC_D):
                        if ec >= 2:
                            g.wait_ge(s_pe.h, q_pe_at[ec - 2])
                        src = wq_h[:, ec * 128:(ec + 1) * 128].rearrange(
                            "(k p) n -> p k n", p=128)
                        dst = wq_tiles[ec % 2][:, :].rearrange(
                            "p (k n) -> p k n", n=128)
                        wq_ld[ec] = pb[ec % 2].inc(g.dma_start(out=dst, in_=src), 16)
                block.gpsimd(p4_wq)

                def p4_pe(t):
                    t.wait_ge(s_dv.h, p3_dv_end)
                    n = 0
                    for ec in range(NC_D):
                        t.wait_ge(pb[ec % 2].h, wq_ld[ec])
                        for qq in range(4):
                            if n >= 2:
                                t.wait_ge(s_dv.h, q_evac[n - 2])
                            for k in range(NC_D):
                                mm = t.matmul(
                                    pst[n % 2][:, :],
                                    wq_tiles[ec % 2][:, k * 128:(k + 1) * 128],
                                    xn_fm[:, k * S + qq * 512: k * S + (qq + 1) * 512],
                                    start=(k == 0), stop=(k == NC_D - 1))
                            s_pe.inc(mm)
                            n += 1

                def p4_dv(v):
                    n = 0
                    for ec in range(NC_D):
                        for qq in range(4):
                            v.wait_ge(s_pe.h, pe0 + n + 1)
                            s_dv.inc(v.tensor_copy(
                                out=q_fm[:, ec * S + qq * 512: ec * S + (qq + 1) * 512],
                                in_=pst[n % 2][:, :]))
                            n += 1

                block.tensor(p4_pe)
                p4_pe_end = s_pe.n
                block.vector(p4_dv)
                p4_dv_end = s_dv.n
            es_xn.close()

            with ExitStack() as es5:
                sc_sb = es5.enter_context(nc.sbuf_tensor("sc_sb", [128, S], F32))
                pr_sb = es5.enter_context(nc.sbuf_tensor("pr_sb", [128, S], BF16))
                prT_sb = es5.enter_context(nc.sbuf_tensor("prT_sb", [128, S], BF16))
                vs0 = es5.enter_context(nc.sbuf_tensor("vs0", [128, D], BF16))
                vs1 = es5.enter_context(nc.sbuf_tensor("vs1", [128, D], BF16))
                att_sb = es5.enter_context(nc.sbuf_tensor("att_sb", [128, D], F32))
                smx = es5.enter_context(nc.sbuf_tensor("smx", [128, 4], F32))
                ps_s0 = es5.enter_context(nc.psum_tensor("ps_s0", [128, 512], F32))
                ps_s1 = es5.enter_context(nc.psum_tensor("ps_s1", [128, 512], F32))
                ps_at = es5.enter_context(nc.psum_tensor("ps_at", [128, 2048], F32))
                ps_t5a = es5.enter_context(nc.psum_tensor("ps_t5a", [128, 512], BF16))
                ps_t5b = es5.enter_context(nc.psum_tensor("ps_t5b", [128, 512], BF16))
                ps_tt5 = [ps_t5a, ps_t5b]
                v_tiles = [vs0, vs1]
                ps_s = [ps_s0, ps_s1]
                pe0, dv0, se0, st0 = s_pe.n, s_dv.n, s_se.n, s_st.n
                sc_evac = {}
                g_of = {}
                pr_done = {}
                trT_evac = []
                att_evac = {}
                se_exp = {}
                pe_sc = {}
                pe_tr = {}
                pe_at = {}
                vseq = []
                att_store = {}
                sim_pe, sim_dv, sim_se, sim_st = pe0, dv0, se0, st0
                g = 0
                for qi in range(NC_S):
                    J = (qi + 1) * 128
                    nj = (J + 511) // 512
                    for js in range(nj):
                        g_of[(qi, js)] = g
                        sim_pe += 1
                        pe_sc[g] = sim_pe
                        sim_dv += 1
                        sc_evac[g] = sim_dv
                        g += 1
                    sim_dv += 3
                    sim_se += 1
                    se_exp[qi] = sim_se
                    sim_dv += 1
                    pr_done[qi] = sim_dv
                    for jc in range(qi + 1):
                        sim_pe += 1
                        pe_tr[(qi, jc)] = sim_pe
                        sim_dv += 1
                        trT_evac.append(sim_dv)
                    for jc in range(qi + 1):
                        vseq.append((qi, jc))
                        sim_pe += 1
                        pe_at[(qi, jc)] = sim_pe
                    sim_dv += 4
                    att_evac[qi] = sim_dv
                    sim_st += 16
                    att_store[qi] = sim_st

                def tri(qi, jc):
                    return qi * (qi + 1) // 2 + jc

                v_ld = {}

                def p5_ld(sync):
                    sync.wait_ge(s_st.h, p3_st_end)
                    sync.wait_ge(s_pe.h, p4_pe_end)
                    li = 0
                    for qi in range(NC_S):
                        for jc in range(qi + 1):
                            if li >= 2:
                                sync.wait_ge(s_pe.h, pe_at[vseq[li - 2]])
                            v_ld[li] = pa[li % 2].inc(sync.dma_start(
                                out=v_tiles[li % 2][:, :],
                                in_=v_d[jc * 128:(jc + 1) * 128, :]), 16)
                            li += 1
                block.sync(p5_ld)

                def p5_pe(t):
                    t.wait_ge(s_dv.h, p4_dv_end)
                    gg = 0
                    li = 0
                    for qi in range(NC_S):
                        J = (qi + 1) * 128
                        nj = (J + 511) // 512
                        for js in range(nj):
                            n = min(512, J - js * 512)
                            if gg >= 2:
                                t.wait_ge(s_dv.h, sc_evac[gg - 2])
                            for e in range(NC_D):
                                mm = t.matmul(
                                    ps_s[gg % 2][:, 0:n],
                                    q_fm[:, e * S + qi * 128: e * S + (qi + 1) * 128],
                                    k_fm[:, e * S + js * 512: e * S + js * 512 + n],
                                    start=(e == 0), stop=(e == NC_D - 1))
                            s_pe.inc(mm)
                            gg += 1
                        t.wait_ge(s_dv.h, pr_done[qi])
                        for jc in range(qi + 1):
                            fl = tri(qi, jc)
                            if fl >= 2:
                                t.wait_ge(s_dv.h, trT_evac[fl - 2])
                            off = ((fl // 2) % 4) * 128
                            s_pe.inc(t.transpose(
                                ps_tt5[fl % 2][:, off:off + 128],
                                pr_sb[:, jc * 128:(jc + 1) * 128], c_idbf[:, :]))
                        if qi >= 1:
                            t.wait_ge(s_dv.h, att_evac[qi - 1])
                        for jc in range(qi + 1):
                            t.wait_ge(pa[li % 2].h, v_ld[li])
                            t.wait_ge(s_dv.h, trT_evac[tri(qi, jc)])
                            for es in range(4):
                                mm = t.matmul(
                                    ps_at[:, es * 512:(es + 1) * 512],
                                    prT_sb[:, jc * 128:(jc + 1) * 128],
                                    v_tiles[li % 2][:, es * 512:(es + 1) * 512],
                                    start=(jc == 0), stop=(jc == qi))
                            s_pe.inc(mm)
                            li += 1

                def p5_dv(v):
                    for qi in range(NC_S):
                        J = (qi + 1) * 128
                        nj = (J + 511) // 512
                        for js in range(nj):
                            n = min(512, J - js * 512)
                            gg = g_of[(qi, js)]
                            v.wait_ge(s_pe.h, pe_sc[gg])
                            s_dv.inc(v.tensor_copy(
                                out=sc_sb[:, js * 512: js * 512 + n],
                                in_=ps_s[gg % 2][:, 0:n]))
                        v.wait_ge(s_dv.h, s_dv.n)
                        s_dv.inc(v.tensor_tensor(
                            out=sc_sb[:, qi * 128:(qi + 1) * 128],
                            in0=sc_sb[:, qi * 128:(qi + 1) * 128],
                            in1=c_umask[:, :], op=OP.add))
                        v.wait_ge(s_dv.h, s_dv.n)
                        s_dv.inc(v.reduce_max(out=smx[:, 0:1], in_=sc_sb[:, 0:J],
                                              axis=AXX))
                        v.wait_ge(s_dv.h, s_dv.n)
                        s_dv.inc(v.tensor_scalar(out=smx[:, 1:2], in0=smx[:, 0:1],
                                                 scalar1=-SCALE, scalar2=None,
                                                 op0=OP.mult))
                        v.wait_ge(s_se.h, se_exp[qi])
                        s_dv.inc(v.reciprocal(out=smx[:, 3:4], in_=smx[:, 2:3]))
                        for jc in range(qi + 1):
                            v.wait_ge(s_pe.h, pe_tr[(qi, jc)])
                            fl = tri(qi, jc)
                            off = ((fl // 2) % 4) * 128
                            s_dv.inc(v.tensor_copy(
                                out=prT_sb[:, jc * 128:(jc + 1) * 128],
                                in_=ps_tt5[fl % 2][:, off:off + 128]))
                        v.wait_ge(s_pe.h, pe_at[(qi, qi)])
                        if qi >= 1:
                            v.wait_ge(s_st.h, att_store[qi - 1])
                        v.wait_ge(s_dv.h, s_dv.n)
                        for es in range(4):
                            s_dv.inc(v.tensor_scalar(
                                out=att_sb[:, es * 512:(es + 1) * 512],
                                in0=ps_at[:, es * 512:(es + 1) * 512],
                                scalar1=smx[:, 3:4], scalar2=None, op0=OP.mult))

                def p5_scalar(s):
                    for qi in range(NC_S):
                        J = (qi + 1) * 128
                        s.wait_ge(s_dv.h, pr_done[qi] - 1)
                        if qi >= 1:
                            s.wait_ge(s_pe.h, pe_tr[(qi - 1, qi - 1)])
                        s_se.inc(s.activation(out=pr_sb[:, 0:J], in_=sc_sb[:, 0:J],
                                              func=AF.Exp, bias=smx[:, 1:2],
                                              scale=SCALE, accum_out=smx[:, 2:3]))
                        if qi >= 1:
                            s.wait_ge(s_dv.h, att_evac[qi - 1])
                            st_store(s, att_d[(qi - 1) * 128: qi * 128, :],
                                     att_sb[:, :])
                    s.wait_ge(s_dv.h, att_evac[NC_S - 1])
                    st_store(s, att_d[(NC_S - 1) * 128: NC_S * 128, :], att_sb[:, :])

                block.tensor(p5_pe)
                block.vector(p5_dv)
                block.scalar(p5_scalar)
                assert s_pe.n == sim_pe, (s_pe.n, sim_pe)
                assert s_dv.n == sim_dv, (s_dv.n, sim_dv)
                assert s_se.n == sim_se and s_st.n == sim_st
                p5_st_end = s_st.n
            es_q.close()
            es_kq.close()

            with ExitStack() as es55:
                mi_t = [es55.enter_context(nc.sbuf_tensor(f"mi{i}", [128, INTER], I32))
                        for i in range(2)]
                ib_t = [es55.enter_context(nc.sbuf_tensor(f"ib{i}", [128, INTER], BF16))
                        for i in range(2)]
                pl_t = [es55.enter_context(nc.sbuf_tensor(f"pl{i}", [128, 16 * 512], BF16))
                        for i in range(3)]
                ma_t = [es55.enter_context(nc.sbuf_tensor(f"ma{i}", [128, INTER], BF16))
                        for i in range(2)]
                stg_t = [es55.enter_context(nc.sbuf_tensor(f"stg{i}", [128, INTER], BF16))
                         for i in range(2)]
                ps_acc = [es55.enter_context(nc.psum_tensor(f"ps_ac{i}", [128, 512], F32))
                          for i in range(3)]
                ps_trm = [es55.enter_context(nc.psum_tensor(f"ps_tr{i}", [128, 1024], BF16))
                          for i in range(4)]
                chunks = []
                for c in range(NC_I):
                    chunks.append((gidx_h, c * 128, D, NC_D, wg_d, c * 128))
                    chunks.append((uidx_h, c * 128, D, NC_D, wu_d, c * 128))
                for kk in range(NC_D):
                    chunks.append((didx_h, kk * 128, INTER, NC_I, wd_d, kk * 128))
                NCH = len(chunks)

                def mregions(w):
                    out = []
                    off = 0
                    while off < w:
                        out.append((off, min(512, w - off)))
                        off += 512
                    return out

                cast_at = {}
                masks_at = {}
                masks_last = {}
                accum_at = {}
                evac_at = {}
                evac_last = {}
                tr_at = {}
                trevac_at = {}
                trevac_done = {}
                store_at = {}
                greg = []
                gbt = []
                gbt_idx = {}
                sim_se, sim_dv, sim_pe = s_se.n, s_dv.n, s_pe.n
                sim_sp = [s_sp[i].n for i in range(4)]
                for ci in range(NCH + 1):
                    cj = ci - 1
                    if 0 <= cj < NCH:
                        ntr = chunks[cj][3]
                        for tr in range(ntr):
                            sim_pe += 1
                            tr_at[(cj, tr)] = sim_pe
                        nbt = (ntr + 3) // 4
                        for bt in range(nbt):
                            sim_dv += 1
                            trevac_at[(cj, bt)] = sim_dv
                            gbt_idx[(cj, bt)] = len(gbt)
                            gbt.append((cj, bt))
                        trevac_done[cj] = trevac_at[(cj, nbt - 1)]
                        par = cj % 2
                        sim_sp[2 * par + 0] += 16
                        sim_sp[2 * par + 1] += 16
                        store_at[cj] = (sim_sp[2 * par + 0], sim_sp[2 * par + 1])
                    if ci < NCH:
                        w = chunks[ci][2]
                        sim_se += 1
                        cast_at[ci] = sim_se
                        for ri in range(len(mregions(w))):
                            sim_dv += 16
                            masks_at[(ci, ri)] = sim_dv
                            sim_pe += 1
                            accum_at[(ci, ri)] = sim_pe
                            sim_se += 1
                            evac_at[(ci, ri)] = sim_se
                            greg.append((ci, ri))
                        masks_last[ci] = masks_at[(ci, len(mregions(w)) - 1)]
                        evac_last[ci] = evac_at[(ci, len(mregions(w)) - 1)]

                greg_idx = {cr: i for i, cr in enumerate(greg)}
                mld_a = {}
                mld_b = {}

                def _m_store(eng, cj, half):
                    src2, r02, w2, ntr2, dst2, dr02 = chunks[cj]
                    hw2 = w2 // 2
                    par = cj % 2
                    si = 2 * par + half
                    eng.wait_ge(s_dv.h, trevac_done[cj])
                    if cj >= 2:
                        eng.wait_ge(s_sp[si].h, store_at[cj - 2][half])
                    lo = 0 if half == 0 else hw2
                    hi = hw2 if half == 0 else w2
                    v = s_sp[si].inc(eng.dma_start(
                        out=dst2[dr02:dr02 + 128, lo:hi],
                        in_=stg_t[cj % 2][:, lo:hi]), 16)
                    assert v == store_at[cj][half], (cj, half, v, store_at[cj])

                def m_ld_sync(sync):
                    sync.wait_ge(s_st.h, p5_st_end)
                    for ci in range(NCH + 1):
                        if ci < NCH:
                            src, r0, w, ntr, _, _ = chunks[ci]
                            hw_ = w // 2
                            if ci >= 2:
                                sync.wait_ge(s_se.h, cast_at[ci - 2])
                            mld_a[ci] = pa[ci % 2].inc(sync.dma_start(
                                out=mi_t[ci % 2][:, 0:hw_],
                                in_=src[r0:r0 + 128, 0:hw_]), 16)
                        cj = ci - 1
                        if 0 <= cj < NCH:
                            _m_store(sync, cj, 0)
                block.sync(m_ld_sync)

                def m_ld_gps(g):
                    g.wait_ge(s_st.h, p5_st_end)
                    for ci in range(NCH + 1):
                        if ci < NCH:
                            src, r0, w, ntr, _, _ = chunks[ci]
                            hw_ = w // 2
                            if ci >= 2:
                                g.wait_ge(s_se.h, cast_at[ci - 2])
                            mld_b[ci] = pb[ci % 2].inc(g.dma_start(
                                out=mi_t[ci % 2][:, hw_:w],
                                in_=src[r0:r0 + 128, hw_:w]), 16)
                        cj = ci - 1
                        if 0 <= cj < NCH:
                            _m_store(g, cj, 1)
                block.gpsimd(m_ld_gps)

                def m_sc(sc):
                    for ci in range(NCH):
                        w = chunks[ci][2]
                        sc.wait_ge(pa[ci % 2].h, mld_a[ci])
                        sc.wait_ge(pb[ci % 2].h, mld_b[ci])
                        if ci >= 2:
                            sc.wait_ge(s_dv.h, masks_last[ci - 2])
                        n = s_se.inc(sc.copy(out=ib_t[ci % 2][:, 0:w],
                                             in_=mi_t[ci % 2][:, 0:w]))
                        assert n == cast_at[ci]
                        for ri, (off, wr) in enumerate(mregions(w)):
                            sc.wait_ge(s_pe.h, accum_at[(ci, ri)])
                            if ri == 0 and ci >= 2:
                                sc.wait_ge(s_pe.h,
                                           tr_at[(ci - 2, chunks[ci - 2][3] - 1)])
                            g_i = greg_idx[(ci, ri)]
                            n = s_se.inc(sc.copy(
                                out=ma_t[ci % 2][:, off:off + wr],
                                in_=ps_acc[g_i % 3][:, 0:wr]))
                            assert n == evac_at[(ci, ri)]
                block.scalar(m_sc)

                def m_dv(v):
                    g_i = 0
                    for ci in range(NCH + 1):
                        cj = ci - 1
                        if 0 <= cj < NCH:
                            ntr = chunks[cj][3]
                            nbt = (ntr + 3) // 4
                            for bt in range(nbt):
                                t0_, t1_ = bt * 4, min(ntr, bt * 4 + 4)
                                v.wait_ge(s_pe.h, tr_at[(cj, t1_ - 1)])
                                if bt == 0 and cj >= 2:
                                    v.wait_ge(s_sp[2 * (cj % 2)].h,
                                              store_at[cj - 2][0])
                                    v.wait_ge(s_sp[2 * (cj % 2) + 1].h,
                                              store_at[cj - 2][1])
                                gb = gbt_idx[(cj, bt)]
                                n = s_dv.inc(v.tensor_copy(
                                    out=stg_t[cj % 2][:, t0_ * 128:t1_ * 128],
                                    in_=ps_trm[gb % 4][:, 0:(t1_ - t0_) * 128]))
                                assert n == trevac_at[(cj, bt)]
                        if ci < NCH:
                            w = chunks[ci][2]
                            v.wait_ge(s_se.h, cast_at[ci])
                            for ri, (off, wr) in enumerate(mregions(w)):
                                if g_i >= 3:
                                    v.wait_ge(s_pe.h, accum_at[greg[g_i - 3]])
                                for k in range(16):
                                    s_dv.inc(v.tensor_scalar(
                                        out=pl_t[g_i % 3][:, k * 512:k * 512 + wr],
                                        in0=ib_t[ci % 2][:, off:off + wr],
                                        scalar1=float(k), scalar2=None,
                                        op0=OP.is_equal))
                                assert s_dv.n == masks_at[(ci, ri)]
                                g_i += 1
                block.vector(m_dv)

                def m_pe(t):
                    t.wait_ge(s_dv.h, p0_lutI_dv)
                    g_i = 0
                    for ci in range(NCH + 1):
                        cj = ci - 1
                        if 0 <= cj < NCH:
                            ntr = chunks[cj][3]
                            t.wait_ge(s_se.h, evac_last[cj])
                            nbt = (ntr + 3) // 4
                            for bt in range(nbt):
                                gb = gbt_idx[(cj, bt)]
                                if gb >= 4:
                                    t.wait_ge(s_dv.h, trevac_at[gbt[gb - 4]])
                                t0_, t1_ = bt * 4, min(ntr, bt * 4 + 4)
                                for tr in range(t0_, t1_):
                                    n = s_pe.inc(t.transpose(
                                        ps_trm[gb % 4][:, (tr - t0_) * 128:
                                                       (tr - t0_ + 1) * 128],
                                        ma_t[cj % 2][:, tr * 128:(tr + 1) * 128],
                                        c_idbf[:, :]))
                                    assert n == tr_at[(cj, tr)]
                        if ci < NCH:
                            w = chunks[ci][2]
                            for ri, (off, wr) in enumerate(mregions(w)):
                                t.wait_ge(s_dv.h, masks_at[(ci, ri)])
                                if g_i >= 3:
                                    t.wait_ge(s_se.h, evac_at[greg[g_i - 3]])
                                mm = None
                                for k in range(16):
                                    mm = t.matmul(
                                        ps_acc[g_i % 3][:, 0:wr],
                                        c_lutI[:, k * 128:(k + 1) * 128],
                                        pl_t[g_i % 3][:, k * 512:k * 512 + wr],
                                        start=(k == 0), stop=(k == 15))
                                n = s_pe.inc(mm)
                                assert n == accum_at[(ci, ri)]
                                g_i += 1
                block.tensor(m_pe)

                assert s_se.n == sim_se and s_dv.n == sim_dv and s_pe.n == sim_pe
                for i in range(4):
                    assert s_sp[i].n == sim_sp[i], (i, s_sp[i].n, sim_sp[i])

            with ExitStack() as es6:
                xn_fm2 = es6.enter_context(nc.sbuf_tensor("xn_fm2", [128, NC_D * S], BF16, side="right"))
                h_sb = es6.enter_context(nc.sbuf_tensor("h_sb", [128, NC_I * TF], BF16))
                out_sb = es6.enter_context(nc.sbuf_tensor("out_sb", [128, 4 * D], F32))
                wgt0 = es6.enter_context(nc.sbuf_tensor("wgt0", [128, D], BF16))
                wgt1 = es6.enter_context(nc.sbuf_tensor("wgt1", [128, D], BF16))
                wut0 = es6.enter_context(nc.sbuf_tensor("wut0", [128, D], BF16))
                wut1 = es6.enter_context(nc.sbuf_tensor("wut1", [128, D], BF16))
                wdt0 = es6.enter_context(nc.sbuf_tensor("wdt0", [128, INTER], BF16))
                wdt1 = es6.enter_context(nc.sbuf_tensor("wdt1", [128, INTER], BF16))
                xr_sb = es6.enter_context(nc.sbuf_tensor("xr_sb", [128, D], F32))
                ar_sb = es6.enter_context(nc.sbuf_tensor("ar_sb", [128, D], F32))
                gact = es6.enter_context(nc.sbuf_tensor("gact", [128, TF], F32))
                dn0 = es6.enter_context(nc.sbuf_tensor("dn0", [128, TF], F32))
                dn1 = es6.enter_context(nc.sbuf_tensor("dn1", [128, TF], F32))
                ps_g0 = es6.enter_context(nc.psum_tensor("ps_g0", [128, 512], F32))
                ps_g1 = es6.enter_context(nc.psum_tensor("ps_g1", [128, 512], F32))
                ps_u0 = es6.enter_context(nc.psum_tensor("ps_u0", [128, 512], F32))
                ps_u1 = es6.enter_context(nc.psum_tensor("ps_u1", [128, 512], F32))
                ps_d0 = es6.enter_context(nc.psum_tensor("ps_d0", [128, 512], F32))
                ps_d1 = es6.enter_context(nc.psum_tensor("ps_d1", [128, 512], F32))
                ps_o0 = es6.enter_context(nc.psum_tensor("ps_o0", [128, 512], F32))
                ps_o1 = es6.enter_context(nc.psum_tensor("ps_o1", [128, 512], F32))
                ps_oo = [ps_o0, ps_o1]
                wg_t = [wgt0, wgt1]
                wu_t = [wut0, wut1]
                wd_t = [wdt0, wdt1]
                dn_t = [dn0, dn1]
                psg = [ps_g0, ps_g1]
                psu = [ps_u0, ps_u1]
                psd = [ps_d0, ps_d1]

                def p6_xn(sync):
                    sync.wait_ge(sc_ld.h, xn_spilled)
                    sync.wait_ge(s_st.h, p5_st_end)
                    for dc in range(NC_D):
                        sc_ld.inc(sync.dma_start(
                            out=xn_fm2[:, dc * S:(dc + 1) * S],
                            in_=xn_d[dc * 128:(dc + 1) * 128, :]), 16)
                block.sync(p6_xn)
                xn_reloaded = sc_ld.n

                pe0, dv0, se0, st0 = s_pe.n, s_dv.n, s_se.n, s_st.n
                sim_pe, sim_dv, sim_se, sim_st = pe0, dv0, se0, st0
                gu_pe = {}
                h_dv = {}
                silu_se = {}
                res_dv = {}
                dn_pe = {}
                dn_dv = {}
                tr_pe = {}
                acc_dv = {}
                out_st = {}
                gu_list = []
                dn_list = []
                for pp in range(NPASS):
                    for tq in range(4):
                        sim_dv += 1
                        res_dv[(pp, tq)] = sim_dv
                    for c in range(NC_I):
                        gu_list.append((pp, c))
                        sim_pe += 2
                        gu_pe[(pp, c)] = sim_pe
                        sim_se += 1
                        silu_se[(pp, c)] = sim_se
                        sim_dv += 1
                        h_dv[(pp, c)] = sim_dv
                    for kk in range(NC_D):
                        dn_list.append((pp, kk))
                        sim_pe += 1
                        dn_pe[(pp, kk)] = sim_pe
                        sim_dv += 1
                        dn_dv[(pp, kk)] = sim_dv
                        for tq in range(4):
                            sim_pe += 1
                            tr_pe[(pp, kk, tq)] = sim_pe
                            sim_dv += 1
                            acc_dv[(pp, kk, tq)] = sim_dv
                    for tq in range(4):
                        sim_st += 16
                        out_st[(pp, tq)] = sim_st

                wldg = {}
                wldd = {}

                def _wait_mat(eng, ci):
                    par = ci % 2
                    eng.wait_ge(s_sp[2 * par].h, store_at[ci][0])
                    eng.wait_ge(s_sp[2 * par + 1].h, store_at[ci][1])

                def p6_wld(sync):
                    gi = 0
                    di = 0
                    for pp in range(NPASS):
                        for c in range(NC_I):
                            if gi >= 2:
                                sync.wait_ge(s_pe.h, gu_pe[gu_list[gi - 2]])
                            if pp == 0:
                                _wait_mat(sync, 2 * c)
                                _wait_mat(sync, 2 * c + 1)
                            sg = pa[gi % 2]
                            sg.inc(sync.dma_start(
                                out=wg_t[gi % 2][:, :],
                                in_=wg_d[c * 128:(c + 1) * 128, :]), 16)
                            wldg[(pp, c)] = sg.inc(sync.dma_start(
                                out=wu_t[gi % 2][:, :],
                                in_=wu_d[c * 128:(c + 1) * 128, :]), 16)
                            gi += 1
                        for kk in range(NC_D):
                            if di >= 2:
                                sync.wait_ge(s_pe.h, dn_pe[dn_list[di - 2]])
                            if pp == 0:
                                _wait_mat(sync, 2 * NC_I + kk)
                            wldd[(pp, kk)] = pb[di % 2].inc(sync.dma_start(
                                out=wd_t[di % 2][:, :],
                                in_=wd_d[kk * 128:(kk + 1) * 128, :]), 16)
                            di += 1
                block.sync(p6_wld)

                xld = {}

                def p6_xld(g):
                    g.wait_ge(s_st.h, p5_st_end)
                    for pp in range(NPASS):
                        for tq in range(4):
                            r0 = pp * TF + tq * 128
                            s_xr.inc(g.dma_start(out=xr_sb[:, :],
                                                 in_=x_h[r0:r0 + 128, :]), 16)
                            xld[(pp, tq)] = s_xr.inc(g.dma_start(
                                out=ar_sb[:, :],
                                in_=att_d[r0:r0 + 128, :]), 16)
                            g.wait_ge(s_dv.h, res_dv[(pp, tq)])
                block.gpsimd(p6_xld)

                def p6_pe(t):
                    t.wait_ge(sc_ld.h, xn_reloaded)
                    gi = 0
                    di = 0
                    for pp in range(NPASS):
                        for c in range(NC_I):
                            t.wait_ge(pa[gi % 2].h, wldg[(pp, c)])
                            if gi >= 2:
                                prev = gu_list[gi - 2]
                                t.wait_ge(s_dv.h, h_dv[prev])
                                t.wait_ge(s_se.h, silu_se[prev])
                            for k in range(NC_D):
                                mm = t.matmul(
                                    psg[gi % 2][:, :],
                                    wg_t[gi % 2][:, k * 128:(k + 1) * 128],
                                    xn_fm2[:, k * S + pp * TF: k * S + (pp + 1) * TF],
                                    start=(k == 0), stop=(k == NC_D - 1))
                            s_pe.inc(mm)
                            for k in range(NC_D):
                                mm = t.matmul(
                                    psu[gi % 2][:, :],
                                    wu_t[gi % 2][:, k * 128:(k + 1) * 128],
                                    xn_fm2[:, k * S + pp * TF: k * S + (pp + 1) * TF],
                                    start=(k == 0), stop=(k == NC_D - 1))
                            s_pe.inc(mm)
                            gi += 1
                        for kk in range(NC_D):
                            t.wait_ge(pb[di % 2].h, wldd[(pp, kk)])
                            t.wait_ge(s_dv.h, h_dv[(pp, NC_I - 1)])
                            if di >= 2:
                                t.wait_ge(s_dv.h, dn_dv[dn_list[di - 2]])
                            for i in range(NC_I):
                                mm = t.matmul(
                                    psd[di % 2][:, :],
                                    wd_t[di % 2][:, i * 128:(i + 1) * 128],
                                    h_sb[:, i * TF:(i + 1) * TF],
                                    start=(i == 0), stop=(i == NC_I - 1))
                            s_pe.inc(mm)
                            t.wait_ge(s_dv.h, dn_dv[(pp, kk)])
                            for tq in range(4):
                                gl_tr = di * 4 + tq
                                if gl_tr >= 2:
                                    pfl = gl_tr - 2
                                    pk = dn_list[pfl // 4]
                                    t.wait_ge(s_dv.h,
                                              acc_dv[(pk[0], pk[1], pfl % 4)])
                                s_pe.inc(t.transpose(
                                    ps_oo[tq % 2][:, (tq // 2) * 128:
                                                  (tq // 2 + 1) * 128],
                                    dn_t[di % 2][:, tq * 128:(tq + 1) * 128],
                                    c_id32[:, :]))
                            di += 1

                def p6_scalar(s):
                    gi = 0
                    for pp in range(NPASS):
                        for c in range(NC_I):
                            s.wait_ge(s_pe.h, gu_pe[(pp, c)] - 1)
                            if gi >= 1:
                                s.wait_ge(s_dv.h, h_dv[gu_list[gi - 1]])
                            s_se.inc(s.activation(out=gact[:, :],
                                                  in_=psg[gi % 2][:, :],
                                                  func=AF.Silu))
                            gi += 1
                        for tq in range(4):
                            s.wait_ge(s_dv.h, acc_dv[(pp, NC_D - 1, tq)])
                            r0 = pp * TF + tq * 128
                            st_store(s, out_h[r0:r0 + 128, :],
                                     out_sb[:, tq * D:(tq + 1) * D])
                    s.wait_ge(s_st.h, s_st.n)

                def p6_dv(v):
                    gi = 0
                    di = 0
                    for pp in range(NPASS):
                        for tq in range(4):
                            v.wait_ge(s_xr.h, xld[(pp, tq)])
                            if pp > 0:
                                v.wait_ge(s_st.h, out_st[(pp - 1, 3)])
                            s_dv.inc(v.tensor_tensor(
                                out=out_sb[:, tq * D:(tq + 1) * D],
                                in0=xr_sb[:, :], in1=ar_sb[:, :], op=OP.add))
                        for c in range(NC_I):
                            v.wait_ge(s_se.h, silu_se[(pp, c)])
                            v.wait_ge(s_pe.h, gu_pe[(pp, c)])
                            s_dv.inc(v.tensor_tensor(
                                out=h_sb[:, c * TF:(c + 1) * TF],
                                in0=gact[:, :], in1=psu[gi % 2][:, :], op=OP.mult))
                            gi += 1
                        for kk in range(NC_D):
                            v.wait_ge(s_pe.h, dn_pe[(pp, kk)])
                            if di >= 2:
                                pk = dn_list[di - 2]
                                v.wait_ge(s_pe.h, tr_pe[(pk[0], pk[1], 3)])
                            s_dv.inc(v.tensor_copy(out=dn_t[di % 2][:, :],
                                                   in_=psd[di % 2][:, :]))
                            for tq in range(4):
                                v.wait_ge(s_pe.h, tr_pe[(pp, kk, tq)])
                                s_dv.inc(v.tensor_tensor(
                                    out=out_sb[:, tq * D + kk * 128:
                                               tq * D + (kk + 1) * 128],
                                    in0=out_sb[:, tq * D + kk * 128:
                                               tq * D + (kk + 1) * 128],
                                    in1=ps_oo[tq % 2][:, (tq // 2) * 128:
                                                      (tq // 2 + 1) * 128],
                                    op=OP.add))
                            di += 1

                block.tensor(p6_pe)
                block.scalar(p6_scalar)
                block.vector(p6_dv)
                assert s_pe.n == sim_pe, (s_pe.n, sim_pe)
                assert s_dv.n == sim_dv, (s_dv.n, sim_dv)
                assert s_se.n == sim_se and s_st.n == sim_st

    return nc


_NC_CACHE = {}


def kernel(L0_attn_x, L0_attn_Wq, L0_attn_Wk, L0_attn_Wv,
           L0_ffn_lut, L0_ffn_gate_idx, L0_ffn_up_idx, L0_ffn_down_idx):
    x = np.ascontiguousarray(np.asarray(L0_attn_x, dtype=np.float32))
    wq = np.ascontiguousarray(np.asarray(L0_attn_Wq, dtype=np.float32))
    wk = np.ascontiguousarray(np.asarray(L0_attn_Wk, dtype=np.float32))
    wv = np.ascontiguousarray(np.asarray(L0_attn_Wv, dtype=np.float32))
    lut = np.ascontiguousarray(np.asarray(L0_ffn_lut, dtype=np.float32)).reshape(1, 16)
    gidx = np.ascontiguousarray(np.asarray(L0_ffn_gate_idx, dtype=np.int32))
    uidx = np.ascontiguousarray(np.asarray(L0_ffn_up_idx, dtype=np.int32))
    didx = np.ascontiguousarray(np.asarray(L0_ffn_down_idx, dtype=np.int32))

    if "nc" not in _NC_CACHE:
        _NC_CACHE["nc"] = build_program()
    nc = _NC_CACHE["nc"]

    in_maps = []
    for b in range(8):
        in_maps.append({
            "x": np.ascontiguousarray(x[b]),
            "wq": wq, "wk": wk, "wv": wv, "lut": lut,
            "gidx": gidx, "uidx": uidx, "didx": didx,
        })
    res = run_bass_kernel_spmd(nc, in_maps, core_ids=list(range(8)))
    out = np.stack([np.asarray(res.results[b]["out"]) for b in range(8)], axis=0)
    return out.astype(np.float32)
